# version 08
# baseline (speedup 1.0000x reference)
import math
import numpy as np
import ml_dtypes
import concourse.bass as bass
import concourse.mybir as mybir
from concourse.bass_utils import run_bass_kernel_spmd

F32 = mybir.dt.float32
BF16 = mybir.dt.bfloat16
I32 = mybir.dt.int32
AF = mybir.ActivationFunctionType
ALU = mybir.AluOpType
AX = mybir.AxisListType
ts = bass.ts
BF = ml_dtypes.bfloat16

NCORES = 8
D = 1024
B = 4
JD = D // 128


_CM_KEEP = []


class Sem:
    def __init__(self, nc, name):
        cm = nc.semaphore(name)
        _CM_KEEP.append(cm)
        self.h = cm.__enter__()
        self.val = 0

    def inc(self, inst, n=1):
        inst.then_inc(self.h, n)
        self.val += n
        return self.val

    def wait(self, eng, val=None):
        eng.wait_ge(self.h, self.val if val is None else val)


def sb(nc, name, shape, dtype):
    cm = nc.sbuf_tensor(name, shape, dtype)
    _CM_KEEP.append(cm)
    return cm.__enter__()


def ps(nc, name, shape, dtype=F32):
    cm = nc.psum_tensor(name, shape, dtype)
    _CM_KEEP.append(cm)
    return cm.__enter__()


def din(nc, name, shape, dtype=F32):
    return nc.dram_tensor(name, shape, dtype, kind="ExternalInput").ap()


def dout(nc, name, shape, dtype=F32):
    return nc.dram_tensor(name, shape, dtype, kind="ExternalOutput").ap()


def dint(nc, name, shape, dtype=F32):
    return nc.dram_tensor(name, shape, dtype, kind="Internal").ap()



def build_prog_A(TLOC, phases=4, ctx=None, x_in=None, x_out=None, xaps=None):
    _TAG = "A"
    if ctx is None:
        nc = bass.Bass("TRN2", target_bir_lowering=False, debug=False)
        DIN = lambda nm, sh, dt=F32: din(nc, nm, sh, dt)
        DOUT = lambda nm, sh, dt=F32: dout(nc, nm, sh, dt)
        DINT = lambda nm, sh, dt=F32: dint(nc, nm, sh, dt)
        SB = lambda nm, sh, dt: sb(nc, _TAG + "_" + nm, sh, dt)
        PS = lambda nm, sh, dt=F32: ps(nc, _TAG + "_" + nm, sh, dt)
        _sems = {n: Sem(nc, n) for n in ['Ld', 'Lw', 'St', 'St1', 'Sq', 'Ld2', 'Lp', 'T', 'V', 'S', 's_mm', 's_pre', 's_tanh', 's_bl', 's_fl', 'V2']}
    else:
        nc = ctx.nc
        ctx.reset()
        def DIN(nm, sh, dt=F32):
            if xaps and nm in xaps:
                return xaps[nm]
            if nm == "x_i" and x_in is not None:
                return x_in
            return ctx.din(_TAG + "_" + nm, sh, dt)
        def DOUT(nm, sh, dt=F32):
            if xaps and nm in xaps:
                return xaps[nm]
            if nm == "xout" and x_out is not None:
                return x_out
            return ctx.dout(_TAG + "_" + nm, sh, dt)
        def DINT(nm, sh, dt=F32):
            if xaps and nm in xaps:
                return xaps[nm]
            return ctx.dint(_TAG + "_" + nm, sh, dt)
        SB = lambda nm, sh, dt: ctx.al(sh, dt)
        _PSM = {'ptr': 'ctx.pT[:, 0:128]', 'ptrf': 'ctx.pA[:, 0:128]', 'pp': 'ctx.pB'}
        PS = lambda nm, sh, dt=F32: ctx.psum_by_spec(_PSM[nm])
        _sems = ctx.sems
    Ld = _sems["Ld"]; Lw = _sems["Lw"]; St = _sems["St"]; St1 = _sems["St1"]; Sq = _sems["Sq"]; Ld2 = _sems["Ld2"]; Lp = _sems["Lp"]; T = _sems["T"]; V = _sems["V"]; S = _sems["S"]; s_mm = _sems["s_mm"]; s_pre = _sems["s_pre"]; s_tanh = _sems["s_tanh"]; s_bl = _sems["s_bl"]; s_fl = _sems["s_fl"]; V2 = _sems["V2"]
    TL = B * TLOC
    TLB = TL + 4
    NT = (TLB + 127) // 128
    NTC = TL // 512 if TL >= 512 else 1
    TCW = min(512, TL)

    xa = DIN("xa", [TLB, D])
    mixes = DIN("mixes", [6, 128, JD])
    wts = DIN("wts", [6, D, D])
    identb = DIN("identb", [128, 128], BF16)
    identf = DIN("identf", [128, 128], F32)
    xfm_o = DOUT("xfm", [JD, 128, TL])
    rg_o = DOUT("rg", [2, JD, 128, TL], BF16)
    sp_o = DOUT("sp", [2, JD, 128, TL])
    kva_s = DINT("kva_s", [3, JD, 128, TL], BF16)
    dec_s = DINT("dec_s", [JD, 128, TL])

    xt = SB("xt", [128, D], F32)
    xnt = SB("xnt", [128, D], F32)
    xnb = SB("xnb", [128, D], BF16)
    lnm1 = SB("lnm1", [128, 1], F32)
    lnm2 = SB("lnm2", [128, 1], F32)
    lnsq = SB("lnsq", [128, D], F32)
    idb = SB("idb", [128, 128], BF16)
    idf = SB("idf", [128, 128], F32)
    mix_sb = SB("mix_sb", [128, 6 * JD], F32)
    xne = SB("xne", [128, JD * TLB], BF16)
    dbf = SB("dbf", [128, JD * TL], BF16)
    zbf = SB("zbf", [128, JD * TL], BF16)
    wtf = SB("wtf", [128, D], F32)
    wtb = SB("wtb", [128, 8 * D], BF16)
    acts = SB("acts", [128, TL], F32)
    actb = SB("actb", [128, TL], BF16)
    t_a = SB("t_a", [128, TCW], F32)
    t_r = SB("t_r", [128, TCW], F32)
    xev = SB("xev", [128, 128], F32)
    kj = SB("kj", [128, TL], BF16)
    vj = SB("vj", [128, TL], BF16)
    aj = SB("aj", [128, TL], BF16)
    dj = SB("dj", [128, TL], F32)
    uj = SB("uj", [128, TL], F32)
    slj = SB("slj", [128, TL], F32)
    pcj = SB("pcj", [128, TL], F32)
    ptr = PS("ptr", [128, 128], BF16)
    ptrf = PS("ptrf", [128, 128], F32)
    pp = PS("pp", [128, TCW], F32)
    if True:
        v, s, t, g, sy = nc.vector, nc.scalar, nc.tensor, nc.gpsimd, nc.sync

        sy.dma_start(out=idb[:], in_=identb).then_inc(Ld.h, 16)
        sy.dma_start(out=idf[:], in_=identf).then_inc(Ld.h, 16)
        for m in range(6):
            sy.dma_start(out=mix_sb[:, m * JD:(m + 1) * JD], in_=mixes[m]).then_inc(Ld.h, 16)
        Ld.val += 16 * 8
        onesf = acts

        for i in range(NT):
            r0 = 128 * i
            rows = min(128, TLB - r0)
            if rows < 128:
                ei = V.inc(v.memset(xt[:], 0.0))
                V.wait(sy, ei)
            T.wait(sy)
            V.wait(sy)
            sy.dma_start(out=xt[0:rows, :],
                         in_=xa[r0:r0 + rows, :]).then_inc(Ld.h, 16)
            Ld.val += 16
            Ld.wait(v)
            v.tensor_reduce(lnm1[:], xt[:], AX.X, ALU.add)
            v.drain()
            v.tensor_scalar_mul(lnm1[:], lnm1[:], 1.0 / D)
            v.drain()
            v.tensor_scalar(xnt[:], xt[:], lnm1[:], None, ALU.subtract)
            v.drain()
            v.tensor_mul(lnsq[:], xnt[:], xnt[:])
            v.drain()
            v.tensor_reduce(lnm2[:], lnsq[:], AX.X, ALU.add)
            v.drain()
            ei = V.inc(v.tensor_scalar(lnm2[:], lnm2[:], 1.0 / D, 1e-5, ALU.mult, ALU.add))
            V.wait(s, ei)
            ei = S.inc(s.activation(lnm2[:], lnm2[:], AF.Sqrt))
            S.wait(v, ei)
            v.reciprocal(lnm2[:], lnm2[:])
            v.drain()
            v.tensor_scalar(xnt[:], xnt[:], lnm2[:], None, ALU.mult)
            v.drain()
            ei = V.inc(v.tensor_copy(xnb[:], xnt[:]))
            V.wait(t, ei)
            for j in range(JD):
                ei = T.inc(t.transpose(ptr[:], xnb[:, 128 * j:128 * (j + 1)], idb[:]))
                T.wait(v, ei)
                ei = V.inc(v.tensor_copy(xne[:, j * TLB + r0: j * TLB + r0 + rows],
                                         ptr[0:128, 0:rows]))
                V.wait(t, ei)
                c0 = max(r0, 4)
                n_real = r0 + rows - c0
                if n_real <= 0:
                    continue
                ei = T.inc(t.transpose(ptrf[:], xt[:, 128 * j:128 * (j + 1)], idf[:]))
                T.wait(v, ei)
                ei = V.inc(v.tensor_copy(xev[:, 0:rows], ptrf[:, 0:rows]))
                V.wait(g, ei)
                g.dma_start(out=xfm_o[j, :, c0 - 4: c0 - 4 + n_real],
                            in_=xev[:, c0 - r0: c0 - r0 + n_real]).then_inc(St.h, 16)
                St.val += 16
                V.wait(t, ei)
                St.wait(v)

        if phases < 2:
            St.wait(g)
            if ctx is not None:
                ctx.barrier()
            return nc
        St.wait(v)
        v.drain()
        for j in range(JD):
            v.tensor_tensor(dbf[:, j * TL:(j + 1) * TL],
                            xne[:, j * TLB: j * TLB + TL],
                            xne[:, j * TLB + 4: j * TLB + 4 + TL], ALU.subtract)
        ei = V.inc(v.drain())

        import os as _os
        WNAMES = ["r", "w", "k", "v", "a", "g"][:int(_os.environ.get("NMIX", "6"))]
        for m, wn in enumerate(WNAMES):
            T.wait(sy)
            for k in range(JD):
                if k > 0:
                    V.wait(sy)
                sy.dma_start(out=wtf[:],
                             in_=wts[m, 128 * k:128 * (k + 1), :]).then_inc(Ld.h, 16)
                Ld.val += 16
                Ld.wait(v)
                ei = V.inc(v.tensor_copy(wtb[:, k * D:(k + 1) * D], wtf[:]))
            T.wait(v)
            for j in range(JD):
                v.scalar_tensor_tensor(zbf[:, j * TL:(j + 1) * TL],
                                       dbf[:, j * TL:(j + 1) * TL],
                                       mix_sb[:, m * JD + j: m * JD + j + 1],
                                       xne[:, j * TLB + 4: j * TLB + 4 + TL],
                                       ALU.mult, ALU.add)
            ei = V.inc(v.drain())
            V.wait(t, ei)
            for m8 in range(JD):
                for tc in range(NTC):
                    c0 = tc * TCW
                    for k in range(JD):
                        mm = t.matmul(pp[:, 0:TCW],
                                      wtb[:, k * D + 128 * m8: k * D + 128 * (m8 + 1)],
                                      zbf[:, k * TL + c0: k * TL + c0 + TCW],
                                      start=(k == 0), stop=(k == JD - 1))
                    ei = T.inc(mm)
                    if wn == "r":
                        T.wait(s, ei)
                        ei = S.inc(s.activation(actb[:, c0:c0 + TCW], pp[:, 0:TCW], AF.Sigmoid))
                        S.wait(t, ei)
                        S.wait(g, ei)
                    elif wn == "k":
                        T.wait(s, ei)
                        ei = S.inc(s.activation(actb[:, c0:c0 + TCW], pp[:, 0:TCW], AF.Tanh))
                        S.wait(t, ei)
                        S.wait(g, ei)
                    elif wn in ("a", "g"):
                        T.wait(s, ei)
                        ei = S.inc(s.activation(actb[:, c0:c0 + TCW], pp[:, 0:TCW], AF.Sigmoid))
                        S.wait(t, ei)
                        S.wait(g, ei)
                    elif wn == "v":
                        T.wait(v, ei)
                        ei = V.inc(v.tensor_copy(actb[:, c0:c0 + TCW], pp[:, 0:TCW]))
                        V.wait(t, ei)
                        V.wait(g, ei)
                    elif wn == "w":
                        T.wait(s, ei)
                        ei_abs = S.inc(s.activation(t_a[:], pp[:, 0:TCW], AF.Abs))
                        s.drain()
                        s.activation(t_a[:], t_a[:], AF.Exp, scale=-1.0)
                        s.drain()
                        ei_ln = S.inc(s.activation(t_a[:], t_a[:], AF.Ln, bias=1.0))
                        S.wait(v, ei_abs)
                        v.tensor_relu(t_r[:], pp[:, 0:TCW])
                        v.drain()
                        S.wait(v, ei_ln)
                        ei_add = V.inc(v.tensor_add(t_a[:], t_a[:], t_r[:]))
                        V.wait(s, ei_add)
                        V.wait(t, ei_add)
                        ei = S.inc(s.activation(acts[:, c0:c0 + TCW], t_a[:], AF.Exp, scale=-1.0))
                        S.wait(t, ei)
                        S.wait(g, ei)
                if wn == "r":
                    g.dma_start(out=rg_o[0, m8], in_=actb[:, 0:TL]).then_inc(St.h, 16)
                elif wn == "g":
                    g.dma_start(out=rg_o[1, m8], in_=actb[:, 0:TL]).then_inc(St.h, 16)
                elif wn == "k":
                    g.dma_start(out=kva_s[0, m8], in_=actb[:, 0:TL]).then_inc(St.h, 16)
                elif wn == "v":
                    g.dma_start(out=kva_s[1, m8], in_=actb[:, 0:TL]).then_inc(St.h, 16)
                elif wn == "a":
                    g.dma_start(out=kva_s[2, m8], in_=actb[:, 0:TL]).then_inc(St.h, 16)
                elif wn == "w":
                    g.dma_start(out=dec_s[m8], in_=acts[:, 0:TL]).then_inc(St.h, 16)
                St.val += 16
                St.wait(s)
                St.wait(v)

        if phases < 4:
            St.wait(g)
            if ctx is not None:
                ctx.barrier()
            return nc
        St.wait(v)
        v.memset(onesf[:], 1.0)
        v.drain()
        for j in range(JD):
            St.wait(sy)
            V.wait(sy)
            sy.dma_start(out=kj[:], in_=kva_s[0, j]).then_inc(Ld.h, 16)
            sy.dma_start(out=vj[:], in_=kva_s[1, j]).then_inc(Ld.h, 16)
            sy.dma_start(out=aj[:], in_=kva_s[2, j]).then_inc(Ld.h, 16)
            sy.dma_start(out=dj[:], in_=dec_s[j]).then_inc(Ld.h, 16)
            Ld.val += 64
            Ld.wait(v)
            v.tensor_mul(uj[:], aj[:], kj[:])
            v.drain()
            v.tensor_mul(uj[:], uj[:], vj[:])
            v.drain()
            for b in range(B):
                v.tensor_tensor_scan(slj[:, b:TL:B], dj[:, b:TL:B], uj[:, b:TL:B],
                                     0.0, ALU.mult, ALU.add)
                v.tensor_tensor_scan(pcj[:, b:TL:B], dj[:, b:TL:B], onesf[:, b:TL:B],
                                     1.0, ALU.mult, ALU.mult)
            ei = V.inc(v.drain())
            V.wait(g, ei)
            g.dma_start(out=sp_o[0, j], in_=slj[:]).then_inc(St.h, 16)
            g.dma_start(out=sp_o[1, j], in_=pcj[:]).then_inc(St.h, 16)
            St.val += 32
            St.wait(v)
        St.wait(g)
    if ctx is not None:
        ctx.barrier()
    return nc



def host_prep_A(x_full, mix_list, w_list, core, TLOC):
    TG_ = x_full.shape[1]
    t0 = TLOC * core
    xs = x_full[:, t0:t0 + TLOC, :]
    xs = np.ascontiguousarray(xs.transpose(1, 0, 2).reshape(B * TLOC, D))
    if t0 == 0:
        bnd = np.zeros((B, D), np.float32)
    else:
        bnd = np.ascontiguousarray(x_full[:, t0 - 1, :])
    xa = np.concatenate([bnd, xs], axis=0)
    mixes = np.stack([m.reshape(JD, 128).T for m in mix_list])
    wts = np.stack([np.ascontiguousarray(w.T) for w in w_list])
    return {
        "xa": xa.astype(np.float32),
        "mixes": mixes.astype(np.float32),
        "wts": wts.astype(np.float32),
        "identb": np.eye(128, dtype=BF),
        "identf": np.eye(128, dtype=np.float32),
    }


def kernel(**inputs):
    raise NotImplementedError("assembling")



def build_prog_B(TLOC, ctx=None, x_in=None, x_out=None, xaps=None):
    _TAG = "B"
    if ctx is None:
        nc = bass.Bass("TRN2", target_bir_lowering=False, debug=False)
        DIN = lambda nm, sh, dt=F32: din(nc, nm, sh, dt)
        DOUT = lambda nm, sh, dt=F32: dout(nc, nm, sh, dt)
        DINT = lambda nm, sh, dt=F32: dint(nc, nm, sh, dt)
        SB = lambda nm, sh, dt: sb(nc, _TAG + "_" + nm, sh, dt)
        PS = lambda nm, sh, dt=F32: ps(nc, _TAG + "_" + nm, sh, dt)
        _sems = {n: Sem(nc, n) for n in ['Ld', 'Lw', 'St', 'St1', 'Sq', 'Ld2', 'Lp', 'T', 'V', 'S', 's_mm', 's_pre', 's_tanh', 's_bl', 's_fl', 'V2']}
    else:
        nc = ctx.nc
        ctx.reset()
        def DIN(nm, sh, dt=F32):
            if xaps and nm in xaps:
                return xaps[nm]
            if nm == "x_i" and x_in is not None:
                return x_in
            return ctx.din(_TAG + "_" + nm, sh, dt)
        def DOUT(nm, sh, dt=F32):
            if xaps and nm in xaps:
                return xaps[nm]
            if nm == "xout" and x_out is not None:
                return x_out
            return ctx.dout(_TAG + "_" + nm, sh, dt)
        def DINT(nm, sh, dt=F32):
            if xaps and nm in xaps:
                return xaps[nm]
            return ctx.dint(_TAG + "_" + nm, sh, dt)
        SB = lambda nm, sh, dt: ctx.al(sh, dt)
        _PSM = {'pg': 'ctx.pA', 'pg2': 'ctx.pB', 'pbc': 'ctx.pC', 'pbc2': 'ctx.pD', 'pp': 'ctx.pA'}
        PS = lambda nm, sh, dt=F32: ctx.psum_by_spec(_PSM[nm])
        _sems = ctx.sems
    Ld = _sems["Ld"]; Lw = _sems["Lw"]; St = _sems["St"]; St1 = _sems["St1"]; Sq = _sems["Sq"]; Ld2 = _sems["Ld2"]; Lp = _sems["Lp"]; T = _sems["T"]; V = _sems["V"]; S = _sems["S"]; s_mm = _sems["s_mm"]; s_pre = _sems["s_pre"]; s_tanh = _sems["s_tanh"]; s_bl = _sems["s_bl"]; s_fl = _sems["s_fl"]; V2 = _sems["V2"]
    TL = B * TLOC
    NTC = max(1, TL // 512)
    TCW = min(512, TL)
    GN_EPS = 0.00064

    xfm_i = DIN("xfm_i", [JD, 128, TL])
    rg_i = DIN("rg_i", [2, JD, 128, TL], BF16)
    sp_i = DIN("sp_i", [2, JD, 128, TL])
    pref = DIN("pref", [7, 2, 128, JD * B])
    gnwb = DIN("gnwb", [2, 128, JD])
    woutT = DIN("woutT", [D, D])
    bout = DIN("bout", [1, D])
    gsel = DIN("gsel", [128, 2], BF16)
    sel16 = DIN("sel16", [JD, 16, 128], BF16)
    onesr = DIN("onesr", [1, TL])
    xout_o = DOUT("xout", [JD, 128, TL])

    prf = SB("prf", [128, 7 * 2 * JD * B], F32)
    sin_sb = SB("sin_sb", [128, JD * B], F32)
    tmp_sb = SB("tmp_sb", [128, JD * B], F32)
    gnw_sb = SB("gnw_sb", [128, 2 * JD], F32)
    gsel_sb = SB("gsel_sb", [128, 2], BF16)
    sel16_sb = SB("sel16_sb", [16, JD * 128], BF16)
    ones_bf = SB("ones_bf", [1, TL], BF16)
    onesf_sb = SB("onesf_sb", [1, TL], F32)
    bout_bf = SB("bout_bf", [1, D], BF16)
    bout_f = SB("bout_f", [1, D], F32)
    slj = SB("slj", [128, TL], F32)
    pcj = SB("pcj", [128, TL], F32)
    rj = SB("rj", [128, TL], BF16)
    gj = SB("gj", [128, TL], BF16)
    mxb = SB("mxb", [128, JD * TL], BF16)
    sqb = SB("sqb", [128, TCW], BF16)
    gst = SB("gst", [16, 2 * TL], F32)
    pg_sb = SB("pg_sb", [2, 2 * 512], F32)
    g16 = SB("g16", [16, 2 * TL], BF16)
    wtf = SB("wtf", [128, D], F32)
    wtb = SB("wtb", [128, 8 * D], BF16)
    xfm = SB("xfm", [128, TL], F32)
    t_x = SB("t_x", [128, TCW], F32)
    t_y = SB("t_y", [128, TCW], F32)
    pg = PS("pg", [2, TCW], F32)
    pg2 = PS("pg2", [2, TCW], F32)
    pbc = PS("pbc", [128, TCW], F32)
    pbc2 = PS("pbc2", [128, TCW], F32)
    pp = PS("pp", [128, TCW], F32)

    v, s, t, g, sy = nc.vector, nc.scalar, nc.tensor, nc.gpsimd, nc.sync

    for k in range(7):
        for q in range(2):
            sy.dma_start(out=prf[:, (k * 2 + q) * JD * B:(k * 2 + q + 1) * JD * B],
                         in_=pref[k, q]).then_inc(Ld.h, 16)
    sy.dma_start(out=gnw_sb[:, 0:JD], in_=gnwb[0]).then_inc(Ld.h, 16)
    sy.dma_start(out=gnw_sb[:, JD:2 * JD], in_=gnwb[1]).then_inc(Ld.h, 16)
    sy.dma_start(out=gsel_sb[:], in_=gsel).then_inc(Ld.h, 16)
    for j in range(JD):
        sy.dma_start(out=sel16_sb[0:16, j * 128:(j + 1) * 128],
                     in_=sel16[j]).then_inc(Ld.h, 16)
    sy.dma_start(out=onesf_sb[:], in_=onesr).then_inc(Ld.h, 16)
    sy.dma_start(out=bout_f[:], in_=bout).then_inc(Ld.h, 16)
    Ld.val += 16 * (14 + 2 + 1 + JD + 1 + 1)
    Ld.wait(v)
    v.tensor_copy(ones_bf[:], onesf_sb[:])
    v.drain()
    v.tensor_copy(bout_bf[:], bout_f[:])
    v.drain()
    v.memset(sin_sb[:], 0.0)
    v.drain()
    for k in range(7):
        v.tensor_mul(tmp_sb[:], sin_sb[:], prf[:, (k * 2) * JD * B:(k * 2 + 1) * JD * B])
        v.drain()
        v.tensor_add(sin_sb[:], tmp_sb[:], prf[:, (k * 2 + 1) * JD * B:(k * 2 + 2) * JD * B])
        v.drain()

    for j in range(JD):
        St.wait(sy); V.wait(sy); S.wait(sy)
        Ld2.wait(sy)
        sy.dma_start(out=slj[:], in_=sp_i[0, j]).then_inc(Ld2.h, 16)
        sy.dma_start(out=pcj[:], in_=sp_i[1, j]).then_inc(Ld2.h, 16)
        sy.dma_start(out=rj[:], in_=rg_i[0, j]).then_inc(Ld2.h, 16)
        Ld2.val += 48
        Ld2.wait(v)
        for b in range(B):
            v.scalar_tensor_tensor(slj[:, b:TL:B], pcj[:, b:TL:B],
                                   sin_sb[:, B * j + b: B * j + b + 1],
                                   slj[:, b:TL:B], ALU.mult, ALU.add)
        v.drain()
        ei = V.inc(v.tensor_mul(mxb[:, j * TL:(j + 1) * TL], rj[:], slj[:]))
        v.drain()
        for tc in range(NTC):
            c0 = tc * TCW
            V.wait(s, ei)
            eis = S.inc(s.activation(sqb[:], mxb[:, j * TL + c0: j * TL + c0 + TCW],
                                     AF.Square))
            V.wait(t, ei)
            S.wait(t, eis)
            mm = t.matmul(pg[0:2, 0:TCW], gsel_sb[:], mxb[:, j * TL + c0: j * TL + c0 + TCW],
                          start=True, stop=True)
            mm2 = t.matmul(pg2[0:2, 0:TCW], gsel_sb[:], sqb[:], start=True, stop=True)
            eit = T.inc(mm2)
            T.wait(v, eit)
            v.tensor_copy(pg_sb[0:2, 0:TCW], pg[0:2, 0:TCW])
            v.drain()
            ei2 = V.inc(v.tensor_copy(pg_sb[0:2, TCW:2 * TCW], pg2[0:2, 0:TCW]))
            V.wait(t, ei2)
            V.wait(g, ei2)
            g.dma_start(out=gst[2 * j:2 * j + 2, c0:c0 + TCW], in_=pg_sb[0:2, 0:TCW]).then_inc(Sq.h, 16)
            g.dma_start(out=gst[2 * j:2 * j + 2, TL + c0:TL + c0 + TCW], in_=pg_sb[0:2, TCW:2 * TCW]).then_inc(Sq.h, 16)
            Sq.val += 32
            Sq.wait(v)
            Sq.wait(s)

    Sq.wait(v)
    NG = 64.0
    v.tensor_scalar_mul(gst[0:16, 0:TL], gst[0:16, 0:TL], 1.0 / NG)
    v.drain()
    v.tensor_scalar_mul(gst[0:16, TL:2 * TL], gst[0:16, TL:2 * TL], 1.0 / NG)
    v.drain()
    v.tensor_mul(g16[0:16, 0:TL], gst[0:16, 0:TL], gst[0:16, 0:TL])
    v.drain()
    v.tensor_sub(gst[0:16, TL:2 * TL], gst[0:16, TL:2 * TL], g16[0:16, 0:TL])
    v.drain()
    v.tensor_scalar_add(gst[0:16, TL:2 * TL], gst[0:16, TL:2 * TL], GN_EPS)
    ei = V.inc(v.drain())
    V.wait(s, ei)
    ei = S.inc(s.activation(gst[0:16, TL:2 * TL], gst[0:16, TL:2 * TL], AF.Sqrt))
    S.wait(v, ei)
    v.reciprocal(gst[0:16, TL:2 * TL], gst[0:16, TL:2 * TL])
    v.drain()
    v.tensor_copy(g16[0:16, :], gst[0:16, :])
    ei = V.inc(v.drain())
    V.wait(t, ei)

    for j in range(JD):
        V.wait(sy)
        Ld2.wait(sy)
        sy.dma_start(out=gj[:], in_=rg_i[1, j]).then_inc(Ld2.h, 16)
        Ld2.val += 16
        for tc in range(NTC):
            c0 = tc * TCW
            sl = slice(j * TL + c0, j * TL + c0 + TCW)
            mm = t.matmul(pbc[:, 0:TCW], sel16_sb[0:16, j * 128:(j + 1) * 128],
                          g16[0:16, c0:c0 + TCW], start=True, stop=True)
            mm2 = t.matmul(pbc2[:, 0:TCW], sel16_sb[0:16, j * 128:(j + 1) * 128],
                           g16[0:16, TL + c0:TL + c0 + TCW], start=True, stop=True)
            eit = T.inc(mm2)
            T.wait(v, eit)
            Ld2.wait(v)
            v.tensor_sub(t_x[:], mxb[:, sl], pbc[:, 0:TCW])
            v.drain()
            v.tensor_mul(t_x[:], t_x[:], pbc2[:, 0:TCW])
            v.drain()
            v.tensor_scalar(t_x[:], t_x[:], gnw_sb[:, j:j + 1],
                            gnw_sb[:, JD + j:JD + j + 1], ALU.mult, ALU.add)
            v.drain()
            ei = V.inc(v.tensor_mul(mxb[:, sl], t_x[:], gj[:, c0:c0 + TCW]))
            V.wait(t, ei)

    for k in range(JD):
        if k > 0:
            V.wait(sy)
        T.wait(sy)
        Ld2.wait(sy)
        sy.dma_start(out=wtf[:], in_=woutT[128 * k:128 * (k + 1), :]).then_inc(Ld2.h, 16)
        Ld2.val += 16
        Ld2.wait(v)
        V.inc(v.tensor_copy(wtb[:, k * D:(k + 1) * D], wtf[:]))
    ei = V.inc(v.drain())
    V.wait(t, ei)
    for m8 in range(JD):
        V.wait(sy)
        Ld2.wait(sy)
        sy.dma_start(out=xfm[:], in_=xfm_i[m8]).then_inc(Ld2.h, 16)
        Ld2.val += 16
        for tc in range(NTC):
            c0 = tc * TCW
            for k in range(JD):
                mm = t.matmul(pp[:, 0:TCW],
                              wtb[:, k * D + 128 * m8: k * D + 128 * (m8 + 1)],
                              mxb[:, k * TL + c0: k * TL + c0 + TCW],
                              start=(k == 0), stop=False)
            mm = t.matmul(pp[:, 0:TCW], bout_bf[:, 128 * m8:128 * (m8 + 1)],
                          ones_bf[:, c0:c0 + TCW], start=False, stop=True)
            eit = T.inc(mm)
            T.wait(v, eit)
            Ld2.wait(v)
            ei = V.inc(v.tensor_add(t_y[:], pp[:, 0:TCW],
                                    xfm[:, c0:c0 + TCW]))
            V.wait(t, ei)
            V.wait(g, ei)
            g.wait_ge(St.h, St.val)
            g.dma_start(out=xout_o[m8, :, c0:c0 + TCW], in_=t_y[:]).then_inc(St.h, 16)
            St.val += 16
            St.wait(v)
    St.wait(g)
    if ctx is not None:
        ctx.barrier()
    return nc


def host_prep_B(outA_list, gn_w, gn_b, W_out, b_out, core, TLOC):
    TL = B * TLOC
    myA = outA_list[core]
    pref = np.zeros((7, 2, 128, JD * B), np.float32)
    pref[:, 0, :, :] = 1.0
    for k in range(core):
        sp = outA_list[k]["sp"]
        pf = sp[1][:, :, TL - B:TL]
        sf = sp[0][:, :, TL - B:TL]
        pref[k, 0] = pf.transpose(1, 0, 2).reshape(128, JD * B)
        pref[k, 1] = sf.transpose(1, 0, 2).reshape(128, JD * B)
    gnwb = np.stack([gn_w.reshape(JD, 128).T, gn_b.reshape(JD, 128).T])
    gsel = np.zeros((128, 2), BF)
    gsel[0:64, 0] = 1; gsel[64:128, 1] = 1
    sel16 = np.zeros((JD, 16, 128), BF)
    for j in range(JD):
        sel16[j, 2 * j, 0:64] = 1
        sel16[j, 2 * j + 1, 64:128] = 1
    return {
        "xfm_i": myA["xfm"], "rg_i": myA["rg"], "sp_i": myA["sp"],
        "pref": pref,
        "gnwb": gnwb.astype(np.float32),
        "woutT": np.ascontiguousarray(W_out.T).astype(np.float32),
        "bout": b_out.reshape(1, D).astype(np.float32),
        "gsel": gsel, "sel16": sel16,
        "onesr": np.ones((1, TL), np.float32),
    }



def build_prog_C(TLOC, ctx=None, x_in=None, x_out=None, xaps=None):
    _TAG = "C"
    if ctx is None:
        nc = bass.Bass("TRN2", target_bir_lowering=False, debug=False)
        DIN = lambda nm, sh, dt=F32: din(nc, nm, sh, dt)
        DOUT = lambda nm, sh, dt=F32: dout(nc, nm, sh, dt)
        DINT = lambda nm, sh, dt=F32: dint(nc, nm, sh, dt)
        SB = lambda nm, sh, dt: sb(nc, _TAG + "_" + nm, sh, dt)
        PS = lambda nm, sh, dt=F32: ps(nc, _TAG + "_" + nm, sh, dt)
        _sems = {n: Sem(nc, n) for n in ['Ld', 'Lw', 'St', 'St1', 'Sq', 'Ld2', 'Lp', 'T', 'V', 'S', 's_mm', 's_pre', 's_tanh', 's_bl', 's_fl', 'V2']}
    else:
        nc = ctx.nc
        ctx.reset()
        def DIN(nm, sh, dt=F32):
            if xaps and nm in xaps:
                return xaps[nm]
            if nm == "x_i" and x_in is not None:
                return x_in
            return ctx.din(_TAG + "_" + nm, sh, dt)
        def DOUT(nm, sh, dt=F32):
            if xaps and nm in xaps:
                return xaps[nm]
            if nm == "xout" and x_out is not None:
                return x_out
            return ctx.dout(_TAG + "_" + nm, sh, dt)
        def DINT(nm, sh, dt=F32):
            if xaps and nm in xaps:
                return xaps[nm]
            return ctx.dint(_TAG + "_" + nm, sh, dt)
        SB = lambda nm, sh, dt: ctx.al(sh, dt)
        _PSM = {'pA': 'ctx.pA', 'pB': 'ctx.pB', 'pC': 'ctx.pC', 'pD': 'ctx.pD'}
        PS = lambda nm, sh, dt=F32: ctx.psum_by_spec(_PSM[nm])
        _sems = ctx.sems
    Ld = _sems["Ld"]; Lw = _sems["Lw"]; St = _sems["St"]; St1 = _sems["St1"]; Sq = _sems["Sq"]; Ld2 = _sems["Ld2"]; Lp = _sems["Lp"]; T = _sems["T"]; V = _sems["V"]; S = _sems["S"]; s_mm = _sems["s_mm"]; s_pre = _sems["s_pre"]; s_tanh = _sems["s_tanh"]; s_bl = _sems["s_bl"]; s_fl = _sems["s_fl"]; V2 = _sems["V2"]
    TL = B * TLOC
    NTC = max(1, TL // 512)
    TCW = min(512, TL)
    E = 8
    H = 4 * D
    HC = H // 128
    HQ = HC // 4

    x_i = DIN("x_i", [JD, 128, TL])
    gate_w = DIN("gate_w", [D, E])
    gate_b = DIN("gate_b", [1, E])
    w1_i = DIN("w1", [E, D, H])
    b1_i = DIN("b1", [E, 1, H])
    w2_i = DIN("w2", [E, H, D])
    b2_i = DIN("b2", [E, 1, D])
    identf = DIN("identf", [128, 128], F32)
    xout_o = DOUT("xout", [JD, 128, TL])

    uni = SB("uni", [128, JD * TCW], F32)
    xjs = uni
    xnb = SB("xnb", [128, JD * TL], BF16)
    xb_c = SB("xb_c", [128, TCW], BF16)
    sq_c = SB("sq_c", [128, TCW], BF16)
    onescol = SB("onescol", [128, 1], BF16)
    ones_row = SB("ones_row", [1, TL], BF16)
    st_f = SB("st_f", [1, 4 * TCW], F32)
    st_b = SB("st_b", [1, 2 * TCW], BF16)
    gw_sb = SB("gw_sb", [128, JD * E], BF16)
    gb_sb = SB("gb_sb", [1, E], BF16)
    gb_f = SB("gb_f", [1, E], F32)
    idf = SB("idf", [128, 128], F32)
    lg = SB("lg", [8, TCW], F32)
    lgt = SB("lgt", [128, E], F32)
    gm1 = SB("gm1", [128, 1], F32)
    gm8 = SB("gm8", [128, E], F32)
    wmap = SB("wmap", [128, E], F32)
    wfm = SB("wfm", [8, NTC * TCW], F32)
    wfmb = SB("wfmb", [8, NTC * TCW], BF16)
    wrow = SB("wrow", [1, TCW], F32)
    wrow_b = SB("wrow_b", [1, TCW], BF16)
    wbc = SB("wbc", [128, TCW], BF16)
    w1b = SB("w1b", [128, JD * 1024], BF16)
    w2b = SB("w2b", [128, HQ * D], BF16)
    wtf = SB("wtf", [128, 1024], F32)
    b1b = SB("b1b", [1, H], BF16)
    b2b = SB("b2b", [8, D], BF16)
    b2f = SB("b2f", [8, D], F32)
    hbuf = uni.bitcast(BF16)
    htmp = SB("htmp", [128, TCW], BF16)
    htmp2 = SB("htmp2", [128, TCW], BF16)
    oacc = SB("oacc", [128, JD * TL], F32)
    t_y = SB("t_y", [128, TCW], F32)
    pA = PS("pA", [128, TCW], F32)
    pB = PS("pB", [128, TCW], F32)
    pC = PS("pC", [128, TCW], F32)
    pD = PS("pD", [128, TCW], F32)
    pb1 = pA; pb2 = pB; pbc = pC; pbc2 = pD
    pl = pA; ptt = pB; pth = pC
    ph = pB; po = pC

    v, s, t, g, sy = nc.vector, nc.scalar, nc.tensor, nc.gpsimd, nc.sync

    sy.dma_start(out=idf[:], in_=identf).then_inc(Ld.h, 16)
    Ld.val += 16
    for j in range(JD):
        sy.dma_start(out=wtf[:, 0:E], in_=gate_w[128 * j:128 * (j + 1), :]).then_inc(Ld.h, 16)
        Ld.val += 16
        Ld.wait(v)
        V.inc(v.tensor_copy(gw_sb[:, j * E:(j + 1) * E], wtf[:, 0:E]))
        V.wait(sy)
    sy.dma_start(out=gb_f[:], in_=gate_b).then_inc(Ld.h, 16)
    Ld.val += 16
    Ld.wait(v)
    v.memset(onescol[:], 1.0)
    v.memset(ones_row[:], 1.0)
    v.tensor_copy(gb_sb[:], gb_f[:])
    v.memset(oacc[:], 0.0)
    ei = V.inc(v.drain())
    V.wait(t, ei)

    for tc in range(NTC):
        c0 = tc * TCW
        V.wait(sy); S.wait(sy); T.wait(sy)
        Ld.wait(sy)
        for j in range(JD):
            sy.dma_start(out=xjs[:, j * TCW:(j + 1) * TCW],
                         in_=x_i[j][:, c0:c0 + TCW]).then_inc(Ld.h, 16)
        Ld.val += 16 * JD
        Ld.wait(v); Ld.wait(s)
        for j in range(JD):
            xsl = xjs[:, j * TCW:(j + 1) * TCW]
            ei = V.inc(v.tensor_copy(xb_c[:], xsl))
            eis = S.inc(s.activation(sq_c[:], xsl, AF.Square))
            V.wait(t, ei); S.wait(t, eis)
            mm1 = t.matmul(pb1[0:1, 0:TCW], onescol[:], xb_c[:], start=(j == 0), stop=(j == JD - 1))
            mm2 = t.matmul(pb2[0:1, 0:TCW], onescol[:], sq_c[:], start=(j == 0), stop=(j == JD - 1))
            eit = T.inc(mm2)
            T.wait(v, eit)
            T.wait(s, eit)
        v.tensor_scalar_mul(st_f[:, 0:TCW], pb1[0:1, 0:TCW], 1.0 / D)
        v.drain()
        v.tensor_scalar_mul(st_f[:, TCW:2 * TCW], pb2[0:1, 0:TCW], 1.0 / D)
        v.drain()
        v.tensor_mul(st_f[:, 3 * TCW:4 * TCW], st_f[:, 0:TCW], st_f[:, 0:TCW])
        v.drain()
        v.tensor_sub(st_f[:, TCW:2 * TCW], st_f[:, TCW:2 * TCW], st_f[:, 3 * TCW:4 * TCW])
        v.drain()
        v.tensor_scalar_add(st_f[:, TCW:2 * TCW], st_f[:, TCW:2 * TCW], 1e-5)
        ei = V.inc(v.drain())
        V.wait(s, ei)
        ei = S.inc(s.activation(st_f[:, 2 * TCW:3 * TCW], st_f[:, TCW:2 * TCW], AF.Sqrt))
        S.wait(v, ei)
        v.reciprocal(st_f[:, 2 * TCW:3 * TCW], st_f[:, 2 * TCW:3 * TCW])
        v.drain()
        v.tensor_copy(st_b[:, 0:TCW], st_f[:, 0:TCW])
        v.drain()
        ei = V.inc(v.tensor_copy(st_b[:, TCW:2 * TCW], st_f[:, 2 * TCW:3 * TCW]))
        V.wait(t, ei)
        mm1 = t.matmul(pbc[:, 0:TCW], onescol[0:1, 0:1].bitcast(BF16) if False else ones_row[:, 0:128],
                       st_b[:, 0:TCW], start=True, stop=True) if False else None
        mm1 = t.matmul(pbc[:, 0:TCW], ones_row[:, 0:128], st_b[:, 0:TCW], start=True, stop=True)
        mm2 = t.matmul(pbc2[:, 0:TCW], ones_row[:, 0:128], st_b[:, TCW:2 * TCW], start=True, stop=True)
        eit = T.inc(mm2)
        T.wait(v, eit)
        for j in range(JD):
            v.tensor_sub(t_y[:], xjs[:, j * TCW:(j + 1) * TCW], pbc[:, 0:TCW])
            v.drain()
            v.tensor_mul(xnb[:, j * TL + c0: j * TL + c0 + TCW], t_y[:], pbc2[:, 0:TCW])
            v.drain()
        ei = V.inc(v.drain())
        V.wait(t, ei)
        for j in range(JD):
            mm = t.matmul(pl[0:8, 0:TCW], gw_sb[:, j * E:(j + 1) * E],
                          xnb[:, j * TL + c0: j * TL + c0 + TCW],
                          start=(j == 0), stop=False)
        mm = t.matmul(pl[0:8, 0:TCW], gb_sb[:], ones_row[:, 0:TCW], start=False, stop=True)
        eit = T.inc(mm)
        T.wait(v, eit)
        ei = V.inc(v.tensor_copy(lg[0:8, 0:TCW], pl[0:8, 0:TCW]))
        V.wait(t, ei)
        for sbt in range(TCW // 128):
            mm = t.matmul(ptt[:, 0:E], lg[0:8, 128 * sbt:128 * (sbt + 1)], idf[0:8, 0:E],
                          start=True, stop=True)
            eit = T.inc(mm)
            T.wait(v, eit)
            v.tensor_copy(lgt[:], ptt[:, 0:E])
            v.drain()
            v.reduce_max(gm1[:], lgt[:], AX.X)
            v.drain()
            ei = V.inc(v.tensor_scalar(lgt[:], lgt[:], gm1[:], None, ALU.subtract))
            V.wait(s, ei)
            ei = S.inc(s.activation(lgt[:], lgt[:], AF.Exp))
            S.wait(v, ei)
            v.tensor_reduce(gm1[:], lgt[:], AX.X, ALU.add)
            v.drain()
            v.reciprocal(gm1[:], gm1[:])
            v.drain()
            v.tensor_scalar(lgt[:], lgt[:], gm1[:], None, ALU.mult)
            v.drain()
            v.max(out=gm8[:], in_=lgt[:])
            v.drain()
            v.tensor_scalar(wmap[:], lgt[:], gm8[:, 1:2], None, ALU.is_ge)
            v.drain()
            v.tensor_mul(wmap[:], wmap[:], lgt[:])
            v.drain()
            v.tensor_reduce(gm1[:], wmap[:], AX.X, ALU.add)
            v.drain()
            v.reciprocal(gm1[:], gm1[:])
            v.drain()
            ei = V.inc(v.tensor_scalar(wmap[:], wmap[:], gm1[:], None, ALU.mult))
            V.wait(t, ei)
            mm = t.transpose(pth[0:128, 0:8][0:128, 0:8] if False else pth[:, :],
                             wmap[:], idf[:]) if False else None
            mm = t.transpose(pth[:, :], wmap[:, 0:E].broadcast_to([128, E]) if False else wmap[:],
                             idf[:]) if False else None
            mm = t.transpose(pth[0:8, 0:128], wmap[:], idf[:])
            eit = T.inc(mm)
            T.wait(v, eit)
            ei = V.inc(v.tensor_copy(wfm[0:8, c0 + 128 * sbt: c0 + 128 * (sbt + 1)],
                                     pth[0:8, 0:128]))
            V.wait(t, ei)
        v.drain()
    St.wait(v)
    ei = V.inc(v.tensor_copy(wfmb[0:8, :], wfm[0:8, :]))
    V.wait(t, ei); V.wait(g, ei)

    for e in range(E):
        sy.dma_start(out=b2f[e:e + 1, :] if e == 0 else b2f[e:e + 1, :],
                     in_=b2_i[e]).then_inc(Lw.h, 16)
        Lw.val += 16
    Lw.wait(v)
    v.drain()
    ei = V.inc(v.tensor_copy(b2b[0:8, :], b2f[0:8, :]))
    V.wait(t, ei)

    for e in range(E):
        for half in range(4):
            h0 = half * 1024
            T.wait(sy)
            for k in range(JD):
                V.wait(sy)
                Lw.wait(sy)
                sy.dma_start(out=wtf[:], in_=w1_i[e, 128 * k:128 * (k + 1), h0:h0 + 1024]).then_inc(Lw.h, 16)
                Lw.val += 16
                Lw.wait(v)
                V.inc(v.tensor_copy(w1b[:, k * 1024:(k + 1) * 1024], wtf[:]))
            for hk in range(HQ):
                V.wait(sy)
                Lw.wait(sy)
                sy.dma_start(out=wtf[:, 0:D],
                             in_=w2_i[e, 128 * (HQ * half + hk):128 * (HQ * half + hk + 1), :]).then_inc(Lw.h, 16)
                Lw.val += 16
                Lw.wait(v)
                V.inc(v.tensor_copy(w2b[:, hk * D:(hk + 1) * D], wtf[:, 0:D]))
            if half == 0:
                for bh in range(4):
                    V.wait(sy)
                    Lw.wait(sy)
                    sy.dma_start(out=wtf[0:1, 0:1024],
                                 in_=b1_i[e][:, 1024 * bh:1024 * (bh + 1)]).then_inc(Lw.h, 16)
                    Lw.val += 16
                    Lw.wait(v)
                    V.inc(v.tensor_copy(b1b[:, 1024 * bh:1024 * (bh + 1)], wtf[0:1, 0:1024]))
            ei = V.inc(v.drain())
            V.wait(t, ei)
            for tc in range(NTC):
                c0 = tc * TCW
                V.wait(sy); Lw.wait(sy)
                sy.dma_start(out=wrow[:], in_=wfm[e:e + 1, c0:c0 + TCW]).then_inc(Lw.h, 16)
                Lw.val += 16
                Lw.wait(v)
                v.tensor_copy(wrow_b[:], wrow[:])
                ei = V.inc(v.drain())
                V.wait(t, ei)
                mm = t.matmul(pbc[:, 0:TCW], ones_row[:, 0:128], wrow_b[:], start=True, stop=True)
                eit = T.inc(mm)
                T.wait(v, eit)
                ei = V.inc(v.tensor_copy(wbc[:], pbc[:, 0:TCW]))
                V.wait(t, ei)
                for hk in range(HQ):
                    hcol = 128 * hk
                    php = ph if hk % 2 == 0 else pD
                    htp = htmp if hk % 2 == 0 else htmp2
                    if hk >= 2:
                        S.wait(t, S.val - 1)
                    for k in range(JD):
                        mm = t.matmul(php[:, 0:TCW],
                                      w1b[:, k * 1024 + hcol: k * 1024 + hcol + 128],
                                      xnb[:, k * TL + c0: k * TL + c0 + TCW],
                                      start=(k == 0), stop=False)
                    mm = t.matmul(php[:, 0:TCW], b1b[:, h0 + hcol: h0 + hcol + 128],
                                  ones_row[:, 0:TCW], start=False, stop=True)
                    eit = T.inc(mm)
                    T.wait(s, eit)
                    ei = S.inc(s.activation(htp[:], php[:, 0:TCW], AF.Gelu))
                    S.wait(v, ei)
                    v.tensor_mul(hbuf[:, hk * TCW:(hk + 1) * TCW], htp[:], wbc[:])
                    eiv = V.inc(v.drain())
                    V.wait(s, eiv)
                ei = V.inc(v.drain())
                V.wait(t, ei)
                for m8 in range(JD):
                    pop = po if m8 % 2 == 0 else ph
                    for hk in range(HQ):
                        mm = t.matmul(pop[:, 0:TCW],
                                      w2b[:, hk * D + 128 * m8: hk * D + 128 * (m8 + 1)],
                                      hbuf[:, hk * TCW:(hk + 1) * TCW],
                                      start=(hk == 0), stop=(hk == HQ - 1))
                    eit = T.inc(mm)
                    T.wait(v, eit)
                    osl = oacc[:, m8 * TL + c0: m8 * TL + c0 + TCW]
                    v.tensor_add(osl, osl, pop[:, 0:TCW])
                    ei = V.inc(v.drain())
                    if m8 >= 1:
                        V.wait(t, ei - 1)
                if True:
                    V.wait(t, V.val)

    for m8 in range(JD):
        for tc in range(NTC):
            c0 = tc * TCW
            mm = t.matmul(po[:, 0:TCW], b2b[0:8, 128 * m8:128 * (m8 + 1)],
                          wfmb[0:8, c0:c0 + TCW], start=True, stop=True)
            eit = T.inc(mm)
            T.wait(v, eit)
            osl = oacc[:, m8 * TL + c0: m8 * TL + c0 + TCW]
            v.tensor_add(osl, osl, po[:, 0:TCW])
            ei = V.inc(v.drain())
            V.wait(t, ei)

    for m8 in range(JD):
        V.wait(sy)
        Ld.wait(sy)
        sy.dma_start(out=xjs[:, 0:TL] if TL <= JD * TCW else xjs[:, 0:TL],
                     in_=x_i[m8]).then_inc(Ld.h, 16)
        Ld.val += 16
        Ld.wait(v)
        St.wait(v)
        v.tensor_add(oacc[:, m8 * TL:(m8 + 1) * TL], oacc[:, m8 * TL:(m8 + 1) * TL],
                     xjs[:, 0:TL])
        ei = V.inc(v.drain())
        V.wait(g, ei)
        g.wait_ge(St.h, St.val)
        g.dma_start(out=xout_o[m8], in_=oacc[:, m8 * TL:(m8 + 1) * TL]).then_inc(St.h, 16)
        St.val += 16
    St.wait(g)
    if ctx is not None:
        ctx.barrier()
    return nc


def host_prep_C(x_fm_list_unused, gate_w, gate_b, w1, b1, w2, b2, TLOC):
    return {
        "gate_w": np.ascontiguousarray(gate_w).astype(np.float32),
        "gate_b": gate_b.reshape(1, -1).astype(np.float32),
        "w1": np.ascontiguousarray(w1).astype(np.float32),
        "b1": np.ascontiguousarray(b1.reshape(b1.shape[0], 1, -1)).astype(np.float32),
        "w2": np.ascontiguousarray(w2).astype(np.float32),
        "b2": np.ascontiguousarray(b2.reshape(b2.shape[0], 1, -1)).astype(np.float32),
        "identf": np.eye(128, dtype=np.float32),
    }



def build_prog_D(TLOC, ctx=None, x_in=None, x_out=None, xaps=None):
    _TAG = "D"
    if ctx is None:
        nc = bass.Bass("TRN2", target_bir_lowering=False, debug=False)
        DIN = lambda nm, sh, dt=F32: din(nc, nm, sh, dt)
        DOUT = lambda nm, sh, dt=F32: dout(nc, nm, sh, dt)
        DINT = lambda nm, sh, dt=F32: dint(nc, nm, sh, dt)
        SB = lambda nm, sh, dt: sb(nc, _TAG + "_" + nm, sh, dt)
        PS = lambda nm, sh, dt=F32: ps(nc, _TAG + "_" + nm, sh, dt)
        _sems = {n: Sem(nc, n) for n in ['Ld', 'Lw', 'St', 'St1', 'Sq', 'Ld2', 'Lp', 'T', 'V', 'S', 's_mm', 's_pre', 's_tanh', 's_bl', 's_fl', 'V2']}
    else:
        nc = ctx.nc
        ctx.reset()
        def DIN(nm, sh, dt=F32):
            if xaps and nm in xaps:
                return xaps[nm]
            if nm == "x_i" and x_in is not None:
                return x_in
            return ctx.din(_TAG + "_" + nm, sh, dt)
        def DOUT(nm, sh, dt=F32):
            if xaps and nm in xaps:
                return xaps[nm]
            if nm == "xout" and x_out is not None:
                return x_out
            return ctx.dout(_TAG + "_" + nm, sh, dt)
        def DINT(nm, sh, dt=F32):
            if xaps and nm in xaps:
                return xaps[nm]
            return ctx.dint(_TAG + "_" + nm, sh, dt)
        SB = lambda nm, sh, dt: ctx.al(sh, dt)
        _PSM = {'pA': 'ctx.pA', 'pB': 'ctx.pB', 'pC': 'ctx.pC', 'pD': 'ctx.pD'}
        PS = lambda nm, sh, dt=F32: ctx.psum_by_spec(_PSM[nm])
        _sems = ctx.sems
    Ld = _sems["Ld"]; Lw = _sems["Lw"]; St = _sems["St"]; St1 = _sems["St1"]; Sq = _sems["Sq"]; Ld2 = _sems["Ld2"]; Lp = _sems["Lp"]; T = _sems["T"]; V = _sems["V"]; S = _sems["S"]; s_mm = _sems["s_mm"]; s_pre = _sems["s_pre"]; s_tanh = _sems["s_tanh"]; s_bl = _sems["s_bl"]; s_fl = _sems["s_fl"]; V2 = _sems["V2"]
    TL = B * TLOC
    NTC = max(1, TL // 512)
    TCW = min(512, TL)
    EH = 2 * D

    x_i = DIN("x_i", [JD, 128, TL])
    winT = DIN("winT", [D, EH])
    b_in = DIN("b_in", [1, EH])
    b_h = DIN("b_h", [1, D])
    lnwb = DIN("lnwb", [2, 128, JD])
    g_o = DOUT("g_o", [JD, 128, TL], BF16)
    vb_o = DOUT("vb_o", [JD, 128, TL], BF16)
    xn2_o = DOUT("xn2_o", [JD, 128, TL])

    xjs = SB("xjs", [128, JD * TCW], F32)
    yjs = SB("yjs", [128, JD * TCW], F32)
    xnb = SB("xnb", [128, JD * TL], BF16)
    xb_c = SB("xb_c", [128, TCW], BF16)
    sq_c = SB("sq_c", [128, TCW], BF16)
    onescol = SB("onescol", [128, 1], BF16)
    ones_row = SB("ones_row", [1, TL], BF16)
    st_f = SB("st_f", [1, 4 * TCW], F32)
    st_b = SB("st_b", [1, 2 * TCW], BF16)
    lnw_sb = SB("lnw_sb", [128, 2 * JD], F32)
    wtf = SB("wtf", [128, 1024], F32)
    wtb = SB("wtb", [128, JD * EH], BF16)
    brow = SB("brow", [1, EH + D], BF16)
    t_y = SB("t_y", [128, TCW], F32)
    t_z = SB("t_z", [128, TCW], F32)
    pA = PS("pA", [128, TCW], F32)
    pB = PS("pB", [128, TCW], F32)
    pC = PS("pC", [128, TCW], F32)
    pD = PS("pD", [128, TCW], F32)

    v, s, t, g, sy = nc.vector, nc.scalar, nc.tensor, nc.gpsimd, nc.sync

    sy.dma_start(out=lnw_sb[:, 0:JD], in_=lnwb[0]).then_inc(Ld.h, 16)
    Ld.val += 16
    sy.dma_start(out=lnw_sb[:, JD:2 * JD], in_=lnwb[1]).then_inc(Ld.h, 16)
    Ld.val += 16
    sy.dma_start(out=wtf[0:1, 0:1024], in_=b_in[:, 0:1024]).then_inc(Ld.h, 16)
    Ld.val += 16
    Ld.wait(v)
    V.inc(v.tensor_copy(brow[:, 0:1024], wtf[0:1, 0:1024]))
    V.wait(sy)
    sy.dma_start(out=wtf[0:1, 0:1024], in_=b_in[:, 1024:2048]).then_inc(Ld.h, 16)
    Ld.val += 16
    Ld.wait(v)
    V.inc(v.tensor_copy(brow[:, 1024:2048], wtf[0:1, 0:1024]))
    V.wait(sy)
    sy.dma_start(out=wtf[0:1, 0:1024], in_=b_h).then_inc(Ld.h, 16)
    Ld.val += 16
    Ld.wait(v)
    V.inc(v.tensor_copy(brow[:, 2048:2048 + D], wtf[0:1, 0:1024]))
    v.memset(onescol[:], 1.0)
    v.memset(ones_row[:], 1.0)
    ei = V.inc(v.drain())
    V.wait(t, ei); V.wait(sy)
    for k in range(JD):
        for hh in range(2):
            Lw.wait(sy)
            V.wait(sy)
            sy.dma_start(out=wtf[:], in_=winT[128 * k:128 * (k + 1),
                                            1024 * hh:1024 * (hh + 1)]).then_inc(Lw.h, 16)
            Lw.val += 16
            Lw.wait(v)
            V.inc(v.tensor_copy(wtb[:, k * EH + 1024 * hh: k * EH + 1024 * (hh + 1)], wtf[:]))
    ei = V.inc(v.drain())
    V.wait(t, ei)

    def ln_stats(xsrc_sb, off_fn, eps):
        for j in range(JD):
            xsl = xsrc_sb[:, off_fn(j):off_fn(j) + TCW]
            ei = V.inc(v.tensor_copy(xb_c[:], xsl))
            eis = S.inc(s.activation(sq_c[:], xsl, AF.Square))
            V.wait(t, ei); S.wait(t, eis)
            t.matmul(pA[0:1, 0:TCW], onescol[:], xb_c[:], start=(j == 0), stop=(j == JD - 1))
            mm2 = t.matmul(pB[0:1, 0:TCW], onescol[:], sq_c[:], start=(j == 0), stop=(j == JD - 1))
            eit = T.inc(mm2)
            T.wait(v, eit); T.wait(s, eit)
        v.tensor_scalar_mul(st_f[:, 0:TCW], pA[0:1, 0:TCW], 1.0 / D)
        v.drain()
        v.tensor_scalar_mul(st_f[:, TCW:2 * TCW], pB[0:1, 0:TCW], 1.0 / D)
        v.drain()
        v.tensor_mul(st_f[:, 3 * TCW:4 * TCW], st_f[:, 0:TCW], st_f[:, 0:TCW])
        v.drain()
        v.tensor_sub(st_f[:, TCW:2 * TCW], st_f[:, TCW:2 * TCW], st_f[:, 3 * TCW:4 * TCW])
        v.drain()
        v.tensor_scalar_add(st_f[:, TCW:2 * TCW], st_f[:, TCW:2 * TCW], eps)
        ei = V.inc(v.drain())
        V.wait(s, ei)
        ei = S.inc(s.activation(st_f[:, 2 * TCW:3 * TCW], st_f[:, TCW:2 * TCW], AF.Sqrt))
        S.wait(v, ei)
        v.reciprocal(st_f[:, 2 * TCW:3 * TCW], st_f[:, 2 * TCW:3 * TCW])
        v.drain()
        v.tensor_copy(st_b[:, 0:TCW], st_f[:, 0:TCW])
        v.drain()
        ei = V.inc(v.tensor_copy(st_b[:, TCW:2 * TCW], st_f[:, 2 * TCW:3 * TCW]))
        V.wait(t, ei)
        mm1 = t.matmul(pC[:, 0:TCW], ones_row[:, 0:128], st_b[:, 0:TCW], start=True, stop=True)
        mm2 = t.matmul(pD[:, 0:TCW], ones_row[:, 0:128], st_b[:, TCW:2 * TCW], start=True, stop=True)
        eit = T.inc(mm2)
        T.wait(v, eit)

    for tc in range(NTC):
        c0 = tc * TCW
        V.wait(sy); S.wait(sy); T.wait(sy); Ld.wait(sy)
        for j in range(JD):
            sy.dma_start(out=xjs[:, j * TCW:(j + 1) * TCW],
                         in_=x_i[j][:, c0:c0 + TCW]).then_inc(Ld.h, 16)
        Ld.val += 16 * JD
        Ld.wait(v); Ld.wait(s)
        ln_stats(xjs, lambda j: j * TCW, 1e-5)
        for j in range(JD):
            v.tensor_sub(t_y[:], xjs[:, j * TCW:(j + 1) * TCW], pC[:, 0:TCW])
            v.drain()
            ei = V.inc(v.tensor_mul(yjs[:, j * TCW:(j + 1) * TCW], t_y[:], pD[:, 0:TCW]))
            v.drain()
            V.wait(g, ei)
            g.wait_ge(St.h, St.val)
            g.dma_start(out=xn2_o[j][:, c0:c0 + TCW],
                        in_=yjs[:, j * TCW:(j + 1) * TCW]).then_inc(St.h, 16)
            St.val += 16
        St.wait(v)
        ei = V.inc(v.drain())
        V.wait(t, ei)
        V.wait(s, ei)
        ln_stats(yjs, lambda j: j * TCW, 1e-5)
        for j in range(JD):
            v.tensor_sub(t_y[:], yjs[:, j * TCW:(j + 1) * TCW], pC[:, 0:TCW])
            v.drain()
            v.tensor_mul(t_z[:], t_y[:], pD[:, 0:TCW])
            v.drain()
            v.tensor_scalar(t_z[:], t_z[:], lnw_sb[:, j:j + 1], lnw_sb[:, JD + j:JD + j + 1],
                            ALU.mult, ALU.add)
            v.drain()
            v.tensor_copy(xnb[:, j * TL + c0: j * TL + c0 + TCW], t_z[:])
            v.drain()
        V.wait(t, V.inc(v.drain()))

    for m16 in range(16):
        for tc in range(NTC):
            c0 = tc * TCW
            for k in range(JD):
                mm = t.matmul(pA[:, 0:TCW],
                              wtb[:, k * EH + 128 * m16: k * EH + 128 * (m16 + 1)],
                              xnb[:, k * TL + c0: k * TL + c0 + TCW],
                              start=(k == 0), stop=False)
            mm = t.matmul(pA[:, 0:TCW], brow[:, 128 * m16:128 * (m16 + 1)],
                          ones_row[:, 0:TCW], start=False, stop=(m16 < 8))
            if m16 >= 8:
                mm = t.matmul(pA[:, 0:TCW], brow[:, 2048 + 128 * (m16 - 8):2048 + 128 * (m16 - 7)],
                              ones_row[:, 0:TCW], start=False, stop=True)
            eit = T.inc(mm)
            if m16 < 8:
                T.wait(s, eit)
                ei = S.inc(s.activation(t_y[:].bitcast(BF16)[:, 0:TCW], pA[:, 0:TCW], AF.Sigmoid))
                S.wait(t, ei); S.wait(g, ei)
                g.wait_ge(St.h, St.val)
                g.dma_start(out=g_o[m16][:, c0:c0 + TCW],
                            in_=t_y[:].bitcast(BF16)[:, 0:TCW]).then_inc(St.h, 16)
                St.val += 16
                St.wait(s); St.wait(v)
            else:
                T.wait(v, eit)
                ei = V.inc(v.tensor_copy(t_y[:].bitcast(BF16)[:, 0:TCW], pA[:, 0:TCW]))
                V.wait(t, ei); V.wait(g, ei)
                g.wait_ge(St.h, St.val)
                g.dma_start(out=vb_o[m16 - 8][:, c0:c0 + TCW],
                            in_=t_y[:].bitcast(BF16)[:, 0:TCW]).then_inc(St.h, 16)
                St.val += 16
                St.wait(v); St.wait(s)
    St.wait(g)
    if ctx is not None:
        ctx.barrier()
    return nc



def build_prog_E(TG_, NCORES_, ctx=None, x_in=None, x_out=None, xaps=None):
    _TAG = "E"
    if ctx is None:
        nc = bass.Bass("TRN2", target_bir_lowering=False, debug=False)
        DIN = lambda nm, sh, dt=F32: din(nc, nm, sh, dt)
        DOUT = lambda nm, sh, dt=F32: dout(nc, nm, sh, dt)
        DINT = lambda nm, sh, dt=F32: dint(nc, nm, sh, dt)
        SB = lambda nm, sh, dt: sb(nc, _TAG + "_" + nm, sh, dt)
        PS = lambda nm, sh, dt=F32: ps(nc, _TAG + "_" + nm, sh, dt)
        _sems = {n: Sem(nc, n) for n in ['Ld', 'Lw', 'St', 'St1', 'Sq', 'Ld2', 'Lp', 'T', 'V', 'S', 's_mm', 's_pre', 's_tanh', 's_bl', 's_fl', 'V2']}
    else:
        nc = ctx.nc
        ctx.reset()
        def DIN(nm, sh, dt=F32):
            if xaps and nm in xaps:
                return xaps[nm]
            if nm == "x_i" and x_in is not None:
                return x_in
            return ctx.din(_TAG + "_" + nm, sh, dt)
        def DOUT(nm, sh, dt=F32):
            if xaps and nm in xaps:
                return xaps[nm]
            if nm == "xout" and x_out is not None:
                return x_out
            return ctx.dout(_TAG + "_" + nm, sh, dt)
        def DINT(nm, sh, dt=F32):
            if xaps and nm in xaps:
                return xaps[nm]
            return ctx.dint(_TAG + "_" + nm, sh, dt)
        SB = lambda nm, sh, dt: ctx.al(sh, dt)
        _PSM = {'pz': 'ctx.pz'}
        PS = lambda nm, sh, dt=F32: ctx.psum_by_spec(_PSM[nm])
        _sems = ctx.sems
    Ld = _sems["Ld"]; Lw = _sems["Lw"]; St = _sems["St"]; St1 = _sems["St1"]; Sq = _sems["Sq"]; Ld2 = _sems["Ld2"]; Lp = _sems["Lp"]; T = _sems["T"]; V = _sems["V"]; S = _sems["S"]; s_mm = _sems["s_mm"]; s_pre = _sems["s_pre"]; s_tanh = _sems["s_tanh"]; s_bl = _sems["s_bl"]; s_fl = _sems["s_fl"]; V2 = _sems["V2"]
    CH = 64
    NCH = TG_ // CH
    COLS = JD * B
    TLL = B * (TG_ // NCORES_)
    CB = CH * B

    g_i = DIN("g_i", [NCORES_, JD, 128, TLL], BF16)
    vb_i = DIN("vb_i", [NCORES_, JD, 128, TLL], BF16)
    whT = DIN("whT", [D, D])
    hs_o = DOUT("hs_o", [JD, 128, B * TG_], BF16)

    wtf = SB("wtf", [128, D], F32)
    wtb = SB("wtb", [128, JD * D], BF16)
    gb0 = SB("gb0", [128, JD * CB], BF16)
    gb1 = SB("gb1", [128, JD * CB], BF16)
    vbb0 = SB("vbb0", [128, JD * CB], BF16)
    vbb1 = SB("vbb1", [128, JD * CB], BF16)
    h = SB("h", [128, COLS], F32)
    hbf = SB("hbf", [128, COLS], BF16)
    pre = SB("pre", [128, COLS], F32)
    cc = SB("cc", [128, COLS], F32)
    hs0 = SB("hs0", [128, JD * CB], BF16)
    hs1 = SB("hs1", [128, JD * CB], BF16)
    pz = PS("pz", [128, COLS], F32)

    v, s, t, g, sy = nc.vector, nc.scalar, nc.tensor, nc.gpsimd, nc.sync

    gbuf = [gb0, gb1]; vbuf = [vbb0, vbb1]; hsb = [hs0, hs1]

    for k in range(JD):
        if k > 0:
            Ld.wait(sy)
            V2.wait(sy)
        sy.dma_start(out=wtf[:], in_=whT[128 * k:128 * (k + 1), :]).then_inc(Ld.h, 16)
        Ld.val += 16
        Ld.wait(v)
        V2.inc(v.tensor_copy(wtb[:, k * D:(k + 1) * D], wtf[:]))
    v.drain()
    v.memset(h[:], 0.0)
    v.memset(hbf[:], 0.0).then_inc(s_bl.h, 1)
    s_bl.val += 1

    def chunk_src(c):
        r = c // (NCH // NCORES_)
        tl0 = (c % (NCH // NCORES_)) * CB
        return r, tl0

    r, tl0 = chunk_src(0)
    for j in range(JD):
        sy.dma_start(out=gbuf[0][:, j * CB:(j + 1) * CB],
                     in_=g_i[r, j][:, tl0:tl0 + CB]).then_inc(Lp.h, 16)
        sy.dma_start(out=vbuf[0][:, j * CB:(j + 1) * CB],
                     in_=vb_i[r, j][:, tl0:tl0 + CB]).then_inc(Lp.h, 16)
    Lp.val += 16 * 2 * JD
    for c in range(1, NCH):
        r, tl0 = chunk_src(c)
        sy.wait_ge(s_bl.h, 1 + (c - 1) * CH)
        Lp.wait(sy)
        for j in range(JD):
            sy.dma_start(out=gbuf[c % 2][:, j * CB:(j + 1) * CB],
                         in_=g_i[r, j][:, tl0:tl0 + CB]).then_inc(Lp.h, 16)
            sy.dma_start(out=vbuf[c % 2][:, j * CB:(j + 1) * CB],
                         in_=vb_i[r, j][:, tl0:tl0 + CB]).then_inc(Lp.h, 16)
        Lp.val += 16 * 2 * JD

    t.wait_ge(Ld.h, Ld.val)
    t.wait_ge(V2.h, V2.val)
    with t.Fori(0, TG_) as i:
        t.wait_ge(s_bl.h, i + 1)
        for o in range(JD):
            for k in range(JD):
                mm = t.matmul(pz[:, 4 * o:4 * (o + 1)],
                              wtb[:, k * D + 128 * o: k * D + 128 * (o + 1)],
                              hbf[:, 4 * k:4 * (k + 1)],
                              start=(k == 0), stop=(k == JD - 1))
        mm.then_inc(s_mm.h, 1)
    s_mm.val += TG_

    with s.Fori(0, TG_) as i:
        s.wait_ge(s_pre.h, i + 1)
        s.activation(cc[:], pre[:], AF.Tanh).then_inc(s_tanh.h, 1)
    s_tanh.val += TG_

    PAIRS = NCH // 2
    with v.Fori(0, PAIRS) as oc2:
        for p in (0, 1):
            v.wait_ge(Lp.h, oc2 * (16 * 2 * JD * 2) + (p + 1) * 16 * 2 * JD)
            with v.Fori(0, CH) as i:
                nsy = oc2 * (2 * CH) + p * CH
                v.wait_ge(s_mm.h, nsy + i + 1)
                v.scalar_tensor_tensor(
                    pre[:], pz[:], 1.0,
                    vbuf[p][:, 0:JD * CB].rearrange("q (j t) -> q j t", j=JD)[:, :, ts(i, 4)],
                    ALU.mult, ALU.add)
                v.drain().then_inc(s_pre.h, 1)
                v.wait_ge(s_tanh.h, nsy + i + 1)
                v.tensor_sub(cc[:], cc[:], h[:])
                v.drain()
                v.tensor_mul(cc[:], cc[:],
                             gbuf[p][:, 0:JD * CB].rearrange("q (j t) -> q j t", j=JD)[:, :, ts(i, 4)])
                v.drain()
                v.tensor_add(h[:], h[:], cc[:])
                v.drain()
                v.tensor_copy(hbf[:], h[:])
                v.drain()
                v.tensor_copy(hsb[p][:, 0:JD * CB].rearrange("q (j t) -> q j t", j=JD)[:, :, ts(i, 4)],
                              h[:]).then_inc(s_bl.h, 1)
            v.wait_ge(s_fl.h, oc2 * 2 + p + 1)
    s_pre.val += TG_
    s_bl.val += TG_
    s_fl_total = 0

    for c in range(NCH):
        par = c % 2
        Sc = St if par == 0 else St1
        g.wait_ge(s_bl.h, 1 + (c + 1) * CH)
        g.wait_ge(Sc.h, Sc.val)
        for j in range(JD):
            g.dma_start(out=hs_o[j][:, c * CB:(c + 1) * CB],
                        in_=hsb[par][:, j * CB:(j + 1) * CB]).then_inc(Sc.h, 16)
        Sc.val += 16 * JD
        g.wait_ge(Sc.h, Sc.val)
        g.engine_nop().then_inc(s_fl.h, 1)
        s_fl.val += 1
    g.wait_ge(St.h, St.val)
    g.wait_ge(St1.h, St1.val)
    if ctx is not None:
        ctx.barrier()
    return nc



def build_prog_F(TLOC, ctx=None, x_in=None, x_out=None, xaps=None, hs_off=None):
    _TAG = "F"
    if ctx is None:
        nc = bass.Bass("TRN2", target_bir_lowering=False, debug=False)
        DIN = lambda nm, sh, dt=F32: din(nc, nm, sh, dt)
        DOUT = lambda nm, sh, dt=F32: dout(nc, nm, sh, dt)
        DINT = lambda nm, sh, dt=F32: dint(nc, nm, sh, dt)
        SB = lambda nm, sh, dt: sb(nc, _TAG + "_" + nm, sh, dt)
        PS = lambda nm, sh, dt=F32: ps(nc, _TAG + "_" + nm, sh, dt)
        _sems = {n: Sem(nc, n) for n in ['Ld', 'Lw', 'St', 'St1', 'Sq', 'Ld2', 'Lp', 'T', 'V', 'S', 's_mm', 's_pre', 's_tanh', 's_bl', 's_fl', 'V2']}
    else:
        nc = ctx.nc
        ctx.reset()
        def DIN(nm, sh, dt=F32):
            if xaps and nm in xaps:
                return xaps[nm]
            if nm == "x_i" and x_in is not None:
                return x_in
            return ctx.din(_TAG + "_" + nm, sh, dt)
        def DOUT(nm, sh, dt=F32):
            if xaps and nm in xaps:
                return xaps[nm]
            if nm == "xout" and x_out is not None:
                return x_out
            return ctx.dout(_TAG + "_" + nm, sh, dt)
        def DINT(nm, sh, dt=F32):
            if xaps and nm in xaps:
                return xaps[nm]
            return ctx.dint(_TAG + "_" + nm, sh, dt)
        SB = lambda nm, sh, dt: ctx.al(sh, dt)
        _PSM = {'pA': 'ctx.pA'}
        PS = lambda nm, sh, dt=F32: ctx.psum_by_spec(_PSM[nm])
        _sems = ctx.sems
    Ld = _sems["Ld"]; Lw = _sems["Lw"]; St = _sems["St"]; St1 = _sems["St1"]; Sq = _sems["Sq"]; Ld2 = _sems["Ld2"]; Lp = _sems["Lp"]; T = _sems["T"]; V = _sems["V"]; S = _sems["S"]; s_mm = _sems["s_mm"]; s_pre = _sems["s_pre"]; s_tanh = _sems["s_tanh"]; s_bl = _sems["s_bl"]; s_fl = _sems["s_fl"]; V2 = _sems["V2"]
    TL = B * TLOC
    NTC = max(1, TL // 512)
    TCW = min(512, TL)

    x_i = DIN("x_i", [JD, 128, TL])
    xn2_i = DIN("xn2_i", [JD, 128, TL])
    hs_i = DIN("hs_i", [JD, 128, TL], BF16)
    woutT = DIN("woutT", [D, D])
    bout = DIN("bout", [1, D])
    xout_o = DOUT("xout", [JD, 128, TL])

    hsb = SB("hsb", [128, JD * TL], BF16)
    wtf = SB("wtf", [128, D], F32)
    wtb = SB("wtb", [128, JD * D], BF16)
    brow = SB("brow", [1, D], BF16)
    ones_row = SB("ones_row", [1, TL], BF16)
    xs = SB("xs", [128, TL], F32)
    xn2s = SB("xn2s", [128, TL], F32)
    t_y = SB("t_y", [128, TCW], F32)
    pA = PS("pA", [128, TCW], F32)

    v, s, t, g, sy = nc.vector, nc.scalar, nc.tensor, nc.gpsimd, nc.sync

    for j in range(JD):
        hsrc = hs_i[j] if hs_off is None else hs_i[j][:, ts(hs_off, TL)]
        sy.dma_start(out=hsb[:, j * TL:(j + 1) * TL], in_=hsrc).then_inc(Ld.h, 16)
        Ld.val += 16
    sy.dma_start(out=wtf[0:1, :], in_=bout).then_inc(Ld.h, 16)
    Ld.val += 16
    Ld.wait(v)
    v.tensor_copy(brow[:], wtf[0:1, :])
    v.memset(ones_row[:], 1.0)
    ei = V.inc(v.drain())
    V.wait(sy)
    for k in range(JD):
        Lw.wait(sy)
        V.wait(sy)
        sy.dma_start(out=wtf[:], in_=woutT[128 * k:128 * (k + 1), :]).then_inc(Lw.h, 16)
        Lw.val += 16
        Lw.wait(v)
        V.inc(v.tensor_copy(wtb[:, k * D:(k + 1) * D], wtf[:]))
    ei = V.inc(v.drain())
    V.wait(t, ei)

    for m8 in range(JD):
        V.wait(sy); Lw.wait(sy)
        sy.dma_start(out=xs[:], in_=x_i[m8]).then_inc(Lw.h, 16)
        sy.dma_start(out=xn2s[:], in_=xn2_i[m8]).then_inc(Lw.h, 16)
        Lw.val += 32
        for tc in range(NTC):
            c0 = tc * TCW
            for k in range(JD):
                mm = t.matmul(pA[:, 0:TCW],
                              wtb[:, k * D + 128 * m8: k * D + 128 * (m8 + 1)],
                              hsb[:, k * TL + c0: k * TL + c0 + TCW],
                              start=(k == 0), stop=False)
            mm = t.matmul(pA[:, 0:TCW], brow[:, 128 * m8:128 * (m8 + 1)],
                          ones_row[:, 0:TCW], start=False, stop=True)
            eit = T.inc(mm)
            T.wait(v, eit)
            Lw.wait(v)
            v.tensor_add(t_y[:], pA[:, 0:TCW], xs[:, c0:c0 + TCW])
            v.drain()
            ei = V.inc(v.tensor_add(t_y[:], t_y[:], xn2s[:, c0:c0 + TCW]))
            V.wait(t, ei)
            V.wait(g, ei)
            g.wait_ge(St.h, St.val)
            g.dma_start(out=xout_o[m8][:, c0:c0 + TCW], in_=t_y[:]).then_inc(St.h, 16)
            St.val += 16
            St.wait(v)
    St.wait(g)
    if ctx is not None:
        ctx.barrier()
    return nc



def build_prog_I(TLOC, ctx=None, x_in=None, x_out=None, xaps=None):
    _TAG = "I"
    if ctx is None:
        nc = bass.Bass("TRN2", target_bir_lowering=False, debug=False)
        DIN = lambda nm, sh, dt=F32: din(nc, nm, sh, dt)
        DOUT = lambda nm, sh, dt=F32: dout(nc, nm, sh, dt)
        DINT = lambda nm, sh, dt=F32: dint(nc, nm, sh, dt)
        SB = lambda nm, sh, dt: sb(nc, _TAG + "_" + nm, sh, dt)
        PS = lambda nm, sh, dt=F32: ps(nc, _TAG + "_" + nm, sh, dt)
        _sems = {n: Sem(nc, n) for n in ['Ld', 'Lw', 'St', 'St1', 'Sq', 'Ld2', 'Lp', 'T', 'V', 'S', 's_mm', 's_pre', 's_tanh', 's_bl', 's_fl', 'V2']}
    else:
        nc = ctx.nc
        ctx.reset()
        def DIN(nm, sh, dt=F32):
            if xaps and nm in xaps:
                return xaps[nm]
            if nm == "x_i" and x_in is not None:
                return x_in
            return ctx.din(_TAG + "_" + nm, sh, dt)
        def DOUT(nm, sh, dt=F32):
            if xaps and nm in xaps:
                return xaps[nm]
            if nm == "xout" and x_out is not None:
                return x_out
            return ctx.dout(_TAG + "_" + nm, sh, dt)
        def DINT(nm, sh, dt=F32):
            if xaps and nm in xaps:
                return xaps[nm]
            return ctx.dint(_TAG + "_" + nm, sh, dt)
        SB = lambda nm, sh, dt: ctx.al(sh, dt)
        _PSM = {'pA': 'ctx.pA', 'pB': 'ctx.pB', 'pC': 'ctx.pC', 'pD': 'ctx.pD'}
        PS = lambda nm, sh, dt=F32: ctx.psum_by_spec(_PSM[nm])
        _sems = ctx.sems
    Ld = _sems["Ld"]; Lw = _sems["Lw"]; St = _sems["St"]; St1 = _sems["St1"]; Sq = _sems["Sq"]; Ld2 = _sems["Ld2"]; Lp = _sems["Lp"]; T = _sems["T"]; V = _sems["V"]; S = _sems["S"]; s_mm = _sems["s_mm"]; s_pre = _sems["s_pre"]; s_tanh = _sems["s_tanh"]; s_bl = _sems["s_bl"]; s_fl = _sems["s_fl"]; V2 = _sems["V2"]
    TL = B * TLOC
    NTC = max(1, TL // 512)
    TCW = min(512, TL)
    E = 32
    HI = 512
    HCE = HI // 128
    HS = 1024
    HCS = HS // 128

    x_i = DIN("x_i", [JD, 128, TL])
    gw_i = DIN("gw", [D, E])
    gb_i = DIN("gb", [1, E])
    w1_i = DIN("w1", [E, D, HI])
    b1_i = DIN("b1", [E, HI])
    w3_i = DIN("w3", [E, D, HI])
    b3_i = DIN("b3", [E, HI])
    w2_i = DIN("w2", [E, HI, D])
    b2_i = DIN("b2", [E, D])
    sw1_i = DIN("sw1", [D, HS])
    sb1_i = DIN("sb1", [1, HS])
    sw3_i = DIN("sw3", [D, HS])
    sb3_i = DIN("sb3", [1, HS])
    sw2_i = DIN("sw2", [HS, D])
    sb2_i = DIN("sb2", [1, D])
    identf = DIN("identf", [128, 128], F32)
    xout_o = DOUT("xout", [JD, 128, TL])

    uni = SB("uni", [128, JD * TCW], F32)
    xjs = uni
    xnb = SB("xnb", [128, JD * TL], BF16)
    xb_c = SB("xb_c", [128, TCW], BF16)
    sq_c = SB("sq_c", [128, TCW], BF16)
    onescol = SB("onescol", [128, 1], BF16)
    ones_row = SB("ones_row", [1, TL], BF16)
    st_f = SB("st_f", [1, 4 * TCW], F32)
    st_b = SB("st_b", [1, 2 * TCW], BF16)
    gw_sb = SB("gw_sb", [128, JD * E], BF16)
    gb_sb = SB("gb_sb", [1, E], BF16)
    idf = SB("idf", [128, 128], F32)
    lg = SB("lg", [32, TCW], F32)
    lgt = SB("lgt", [128, E], F32)
    gm1 = SB("gm1", [128, 1], F32)
    gm8 = SB("gm8", [128, 8], F32)
    gsc = SB("gsc", [128, 8], F32)
    gtmp = SB("gtmp", [128, 8], F32)
    kbc = SB("kbc", [128, E], F32)
    wmap = SB("wmap", [128, E], F32)
    wfmb = SB("wfmb", [32, NTC * TCW], BF16)
    wrow_b = SB("wrow_b", [1, TCW], BF16)
    wbc = SB("wbc", [128, TCW], BF16)
    w1b = SB("w1b", [128, JD * HS], BF16)
    w3b = SB("w3b", [128, JD * HS], BF16)
    w2b = SB("w2b", [128, HCS * D], BF16)
    wtf = SB("wtf", [128, 1024], F32)
    brow = SB("brow", [1, 2 * HI], BF16)
    b2b = SB("b2b", [32, D], BF16)
    b2f = SB("b2f", [32, D], F32)
    sbrow = SB("sbrow", [1, 2 * HS + D], BF16)
    hbuf = uni.bitcast(BF16)
    htmp = SB("htmp", [128, TCW], BF16)
    htmp2 = SB("htmp2", [128, TCW], BF16)
    oacc = SB("oacc", [128, JD * TL], F32)
    t_y = SB("t_y", [128, TCW], F32)
    pA = PS("pA", [128, TCW], F32)
    pB = PS("pB", [128, TCW], F32)
    pC = PS("pC", [128, TCW], F32)
    pD = PS("pD", [128, TCW], F32)
    pb1 = pA; pb2 = pB; pbc = pC; pbc2 = pD
    pl = pA; ptt = pB; pth = pC
    pha = pA; phb = pB; po = pC; pw = pD

    v, s, t, g, sy = nc.vector, nc.scalar, nc.tensor, nc.gpsimd, nc.sync

    sy.dma_start(out=idf[:], in_=identf).then_inc(Ld.h, 16)
    Ld.val += 16
    for j in range(JD):
        Ld.wait(sy); V.wait(sy)
        sy.dma_start(out=wtf[:, 0:E], in_=gw_i[128 * j:128 * (j + 1), :]).then_inc(Ld.h, 16)
        Ld.val += 16
        Ld.wait(v)
        V.inc(v.tensor_copy(gw_sb[:, j * E:(j + 1) * E], wtf[:, 0:E]))
    Ld.wait(sy); V.wait(sy)
    sy.dma_start(out=wtf[0:1, 0:E], in_=gb_i).then_inc(Ld.h, 16)
    Ld.val += 16
    Ld.wait(v)
    v.tensor_copy(gb_sb[:], wtf[0:1, 0:E])
    v.memset(onescol[:], 1.0)
    v.memset(ones_row[:], 1.0)
    v.memset(oacc[:], 0.0)
    ei = V.inc(v.drain())
    V.wait(t, ei); V.wait(sy)
    for (name, src, off, w) in [("sb1", sb1_i, 0, HS), ("sb3", sb3_i, HS, HS), ("sb2", sb2_i, 2 * HS, D)]:
        Ld.wait(sy); V.wait(sy)
        sy.dma_start(out=wtf[0:1, 0:w], in_=src).then_inc(Ld.h, 16)
        Ld.val += 16
        Ld.wait(v)
        V.inc(v.tensor_copy(sbrow[:, off:off + w], wtf[0:1, 0:w]))
    Ld.wait(sy); V.wait(sy)
    sy.dma_start(out=b2f[0:32, :], in_=b2_i).then_inc(Ld.h, 16)
    Ld.val += 16
    Ld.wait(v)
    ei = V.inc(v.tensor_copy(b2b[0:32, :], b2f[0:32, :]))
    V.wait(t, ei)

    for tc in range(NTC):
        c0 = tc * TCW
        V.wait(sy); S.wait(sy); T.wait(sy); Ld.wait(sy)
        for j in range(JD):
            sy.dma_start(out=xjs[:, j * TCW:(j + 1) * TCW],
                         in_=x_i[j][:, c0:c0 + TCW]).then_inc(Ld.h, 16)
        Ld.val += 16 * JD
        Ld.wait(v); Ld.wait(s)
        for j in range(JD):
            xsl = xjs[:, j * TCW:(j + 1) * TCW]
            ei = V.inc(v.tensor_copy(xb_c[:], xsl))
            eis = S.inc(s.activation(sq_c[:], xsl, AF.Square))
            V.wait(t, ei); S.wait(t, eis)
            t.matmul(pb1[0:1, 0:TCW], onescol[:], xb_c[:], start=(j == 0), stop=(j == JD - 1))
            mm2 = t.matmul(pb2[0:1, 0:TCW], onescol[:], sq_c[:], start=(j == 0), stop=(j == JD - 1))
            eit = T.inc(mm2)
            T.wait(v, eit); T.wait(s, eit)
        v.tensor_scalar_mul(st_f[:, 0:TCW], pb1[0:1, 0:TCW], 1.0 / D)
        v.drain()
        v.tensor_scalar_mul(st_f[:, TCW:2 * TCW], pb2[0:1, 0:TCW], 1.0 / D)
        v.drain()
        v.tensor_mul(st_f[:, 3 * TCW:4 * TCW], st_f[:, 0:TCW], st_f[:, 0:TCW])
        v.drain()
        v.tensor_sub(st_f[:, TCW:2 * TCW], st_f[:, TCW:2 * TCW], st_f[:, 3 * TCW:4 * TCW])
        v.drain()
        v.tensor_scalar_add(st_f[:, TCW:2 * TCW], st_f[:, TCW:2 * TCW], 1e-5)
        ei = V.inc(v.drain())
        V.wait(s, ei)
        ei = S.inc(s.activation(st_f[:, 2 * TCW:3 * TCW], st_f[:, TCW:2 * TCW], AF.Sqrt))
        S.wait(v, ei)
        v.reciprocal(st_f[:, 2 * TCW:3 * TCW], st_f[:, 2 * TCW:3 * TCW])
        v.drain()
        v.tensor_copy(st_b[:, 0:TCW], st_f[:, 0:TCW])
        v.drain()
        ei = V.inc(v.tensor_copy(st_b[:, TCW:2 * TCW], st_f[:, 2 * TCW:3 * TCW]))
        V.wait(t, ei)
        mm1 = t.matmul(pbc[:, 0:TCW], ones_row[:, 0:128], st_b[:, 0:TCW], start=True, stop=True)
        mm2 = t.matmul(pbc2[:, 0:TCW], ones_row[:, 0:128], st_b[:, TCW:2 * TCW], start=True, stop=True)
        eit = T.inc(mm2)
        T.wait(v, eit)
        for j in range(JD):
            v.tensor_sub(t_y[:], xjs[:, j * TCW:(j + 1) * TCW], pbc[:, 0:TCW])
            v.drain()
            v.tensor_mul(xnb[:, j * TL + c0: j * TL + c0 + TCW], t_y[:], pbc2[:, 0:TCW])
            v.drain()
        ei = V.inc(v.drain())
        V.wait(t, ei)
        for j in range(JD):
            t.matmul(pl[0:32, 0:TCW], gw_sb[:, j * E:(j + 1) * E],
                     xnb[:, j * TL + c0: j * TL + c0 + TCW], start=(j == 0), stop=False)
        mm = t.matmul(pl[0:32, 0:TCW], gb_sb[:], ones_row[:, 0:TCW], start=False, stop=True)
        eit = T.inc(mm)
        T.wait(v, eit)
        ei = V.inc(v.tensor_copy(lg[0:32, 0:TCW], pl[0:32, 0:TCW]))
        V.wait(t, ei)
        for sbt in range(TCW // 128):
            mm = t.matmul(ptt[:, 0:E], lg[0:32, 128 * sbt:128 * (sbt + 1)], idf[0:32, 0:E],
                          start=True, stop=True)
            eit = T.inc(mm)
            T.wait(v, eit)
            v.tensor_copy(lgt[:], ptt[:, 0:E])
            v.drain()
            v.reduce_max(gm1[:], lgt[:], AX.X)
            v.drain()
            ei = V.inc(v.tensor_scalar(lgt[:], lgt[:], gm1[:], None, ALU.subtract))
            V.wait(s, ei)
            ei = S.inc(s.activation(lgt[:], lgt[:], AF.Exp))
            S.wait(v, ei)
            v.tensor_reduce(gm1[:], lgt[:], AX.X, ALU.add)
            v.drain()
            v.reciprocal(gm1[:], gm1[:])
            v.drain()
            v.tensor_scalar(lgt[:], lgt[:], gm1[:], None, ALU.mult)
            v.drain()
            pairs = [(0, 1), (0, 2), (0, 3), (1, 2), (1, 3), (2, 3)]
            first = True
            for (a, b_) in pairs:
                v.tensor_add(gtmp[:], lgt[:, a:E:4], lgt[:, b_:E:4])
                v.drain()
                if first:
                    v.tensor_copy(gsc[:], gtmp[:])
                    first = False
                else:
                    v.tensor_max(gsc[:], gsc[:], gtmp[:])
                v.drain()
            v.max(out=gm8[:], in_=gsc[:])
            v.drain()
            v.tensor_scalar(gsc[:], gsc[:], gm8[:, 3:4], None, ALU.is_ge)
            v.drain()
            v.tensor_copy(kbc[:].rearrange("p (g i) -> p g i", g=8),
                          gsc[:].unsqueeze(2).to_broadcast([128, 8, 4]))
            v.drain()
            v.tensor_mul(wmap[:], lgt[:], kbc[:])
            v.drain()
            v.tensor_scalar(kbc[:], kbc[:], -1.0, 1e30, ALU.add, ALU.mult)
            v.drain()
            v.tensor_add(wmap[:], wmap[:], kbc[:])
            v.drain()
            v.max(out=gm8[:], in_=wmap[:])
            v.drain()
            v.tensor_scalar(wmap[:], wmap[:], gm8[:, 3:4], None, ALU.is_ge)
            v.drain()
            ei = V.inc(v.tensor_mul(wmap[:], wmap[:], lgt[:]))
            V.wait(t, ei)
            mm = t.transpose(pth[0:32, 0:128], wmap[:], idf[:])
            eit = T.inc(mm)
            T.wait(v, eit)
            ei = V.inc(v.tensor_copy(wfmb[0:32, c0 + 128 * sbt: c0 + 128 * (sbt + 1)],
                                     pth[0:32, 0:128]))
            V.wait(t, ei)
        v.drain()
    St.wait(v)
    ei = V.inc(v.drain())
    V.wait(t, ei); V.wait(g, ei); V.wait(sy)

    for e in range(-1, E):
        HCX = HCS if e < 0 else HCE
        T.wait(sy)
        for k in range(JD):
            for (wsrc, dstb, hw) in ([(sw1_i, w1b, HS), (sw3_i, w3b, HS)] if e < 0
                                     else [(w1_i[e], w1b, HI), (w3_i[e], w3b, HI)]):
                V.wait(sy); Lw.wait(sy)
                sy.dma_start(out=wtf[:, 0:hw], in_=wsrc[128 * k:128 * (k + 1), :]).then_inc(Lw.h, 16)
                Lw.val += 16
                Lw.wait(v)
                V.inc(v.tensor_copy(dstb[:, k * hw:(k + 1) * hw], wtf[:, 0:hw]))
        for hk in range(HCX):
            V.wait(sy); Lw.wait(sy)
            src = sw2_i[128 * hk:128 * (hk + 1), :] if e < 0 else w2_i[e, 128 * hk:128 * (hk + 1), :]
            sy.dma_start(out=wtf[:, 0:D], in_=src).then_inc(Lw.h, 16)
            Lw.val += 16
            Lw.wait(v)
            V.inc(v.tensor_copy(w2b[:, hk * D:(hk + 1) * D], wtf[:, 0:D]))
        if e >= 0:
            for (bsrc, off) in [(b1_i[e:e + 1, :], 0), (b3_i[e:e + 1, :], HI)]:
                V.wait(sy); Lw.wait(sy)
                sy.dma_start(out=wtf[0:1, 0:HI], in_=bsrc).then_inc(Lw.h, 16)
                Lw.val += 16
                Lw.wait(v)
                V.inc(v.tensor_copy(brow[:, off:off + HI], wtf[0:1, 0:HI]))
        ei = V.inc(v.drain())
        V.wait(t, ei)
        for tc in range(NTC):
            c0 = tc * TCW
            if e >= 0:
                V.wait(sy); Lw.wait(sy)
                T.wait(sy)
                sy.dma_start(out=wrow_b[:], in_=wfmb[e:e + 1, c0:c0 + TCW]).then_inc(Lw.h, 16)
                Lw.val += 16
                Lw.wait(t)
                mm = t.matmul(pw[:, 0:TCW], ones_row[:, 0:128], wrow_b[:], start=True, stop=True)
                eit = T.inc(mm)
                T.wait(v, eit)
                ei = V.inc(v.tensor_copy(wbc[:], pw[:, 0:TCW]))
                V.wait(t, ei)
            for hk in range(HCX):
                hcol = 128 * hk
                HW = HS if e < 0 else HI
                pa_ = pha if hk % 2 == 0 else po
                pb_ = phb if hk % 2 == 0 else pw
                ht_ = htmp if hk % 2 == 0 else htmp2
                if hk >= 2:
                    V.wait(t, V.val - 1)
                for k in range(JD):
                    t.matmul(pa_[:, 0:TCW], w1b[:, k * HW + hcol: k * HW + hcol + 128],
                             xnb[:, k * TL + c0: k * TL + c0 + TCW],
                             start=(k == 0), stop=False)
                brsrc = sbrow[:, hcol:hcol + 128] if e < 0 else brow[:, hcol:hcol + 128]
                t.matmul(pa_[:, 0:TCW], brsrc, ones_row[:, 0:TCW], start=False, stop=True)
                for k in range(JD):
                    t.matmul(pb_[:, 0:TCW], w3b[:, k * HW + hcol: k * HW + hcol + 128],
                             xnb[:, k * TL + c0: k * TL + c0 + TCW],
                             start=(k == 0), stop=False)
                brsrc = sbrow[:, HS + hcol:HS + hcol + 128] if e < 0 else brow[:, HI + hcol:HI + hcol + 128]
                mm = t.matmul(pb_[:, 0:TCW], brsrc, ones_row[:, 0:TCW], start=False, stop=True)
                eit = T.inc(mm)
                T.wait(s, eit)
                ei = S.inc(s.activation(ht_[:], pa_[:, 0:TCW], AF.Silu))
                S.wait(v, ei)
                v.tensor_mul(hbuf[:, hk * TCW:(hk + 1) * TCW], ht_[:], pb_[:, 0:TCW])
                v.drain()
                if e >= 0:
                    v.tensor_mul(hbuf[:, hk * TCW:(hk + 1) * TCW],
                                 hbuf[:, hk * TCW:(hk + 1) * TCW], wbc[:])
                    v.drain()
                eiv = V.inc(v.drain())
                V.wait(s, eiv - 2 if hk >= 1 else 0)
            V.wait(t, V.val)
            for m8 in range(JD):
                pop = po if m8 % 2 == 0 else pw
                for hk in range(HCX):
                    mm = t.matmul(pop[:, 0:TCW],
                                  w2b[:, hk * D + 128 * m8: hk * D + 128 * (m8 + 1)],
                                  hbuf[:, hk * TCW:(hk + 1) * TCW],
                                  start=(hk == 0), stop=(hk == HCX - 1))
                eit = T.inc(mm)
                T.wait(v, eit)
                osl = oacc[:, m8 * TL + c0: m8 * TL + c0 + TCW]
                v.tensor_add(osl, osl, pop[:, 0:TCW])
                ei = V.inc(v.drain())
                if m8 >= 1:
                    V.wait(t, ei - 1)
            V.wait(t, V.val)

    for m8 in range(JD):
        for tc in range(NTC):
            c0 = tc * TCW
            t.matmul(po[:, 0:TCW], b2b[0:32, 128 * m8:128 * (m8 + 1)],
                     wfmb[0:32, c0:c0 + TCW], start=True, stop=False)
            mm = t.matmul(po[:, 0:TCW], sbrow[:, 2 * HS + 128 * m8:2 * HS + 128 * (m8 + 1)],
                          ones_row[:, 0:TCW], start=False, stop=True)
            eit = T.inc(mm)
            T.wait(v, eit)
            osl = oacc[:, m8 * TL + c0: m8 * TL + c0 + TCW]
            v.tensor_add(osl, osl, po[:, 0:TCW])
            ei = V.inc(v.drain())
            V.wait(t, ei)
    for m8 in range(JD):
        V.wait(sy); Ld.wait(sy)
        sy.dma_start(out=xjs[:, 0:TL], in_=x_i[m8]).then_inc(Ld.h, 16)
        Ld.val += 16
        Ld.wait(v)
        St.wait(v)
        v.tensor_add(oacc[:, m8 * TL:(m8 + 1) * TL], oacc[:, m8 * TL:(m8 + 1) * TL],
                     xjs[:, 0:TL])
        ei = V.inc(v.drain())
        V.wait(g, ei)
        g.wait_ge(St.h, St.val)
        g.dma_start(out=xout_o[m8], in_=oacc[:, m8 * TL:(m8 + 1) * TL]).then_inc(St.h, 16)
        St.val += 16
    St.wait(g)
    if ctx is not None:
        ctx.barrier()
    return nc



def build_prog_J(TLOC, ctx=None, x_in=None, x_out=None, xaps=None):
    _TAG = "J"
    if ctx is None:
        nc = bass.Bass("TRN2", target_bir_lowering=False, debug=False)
        DIN = lambda nm, sh, dt=F32: din(nc, nm, sh, dt)
        DOUT = lambda nm, sh, dt=F32: dout(nc, nm, sh, dt)
        DINT = lambda nm, sh, dt=F32: dint(nc, nm, sh, dt)
        SB = lambda nm, sh, dt: sb(nc, _TAG + "_" + nm, sh, dt)
        PS = lambda nm, sh, dt=F32: ps(nc, _TAG + "_" + nm, sh, dt)
        _sems = {n: Sem(nc, n) for n in ['Ld', 'Lw', 'St', 'St1', 'Sq', 'Ld2', 'Lp', 'T', 'V', 'S', 's_mm', 's_pre', 's_tanh', 's_bl', 's_fl', 'V2']}
    else:
        nc = ctx.nc
        ctx.reset()
        def DIN(nm, sh, dt=F32):
            if xaps and nm in xaps:
                return xaps[nm]
            if nm == "x_i" and x_in is not None:
                return x_in
            return ctx.din(_TAG + "_" + nm, sh, dt)
        def DOUT(nm, sh, dt=F32):
            if xaps and nm in xaps:
                return xaps[nm]
            if nm == "xout" and x_out is not None:
                return x_out
            return ctx.dout(_TAG + "_" + nm, sh, dt)
        def DINT(nm, sh, dt=F32):
            if xaps and nm in xaps:
                return xaps[nm]
            return ctx.dint(_TAG + "_" + nm, sh, dt)
        SB = lambda nm, sh, dt: ctx.al(sh, dt)
        _PSM = {'pA': 'ctx.pA', 'pB': 'ctx.pB', 'pC': 'ctx.pC', 'pD': 'ctx.pD'}
        PS = lambda nm, sh, dt=F32: ctx.psum_by_spec(_PSM[nm])
        _sems = ctx.sems
    Ld = _sems["Ld"]; Lw = _sems["Lw"]; St = _sems["St"]; St1 = _sems["St1"]; Sq = _sems["Sq"]; Ld2 = _sems["Ld2"]; Lp = _sems["Lp"]; T = _sems["T"]; V = _sems["V"]; S = _sems["S"]; s_mm = _sems["s_mm"]; s_pre = _sems["s_pre"]; s_tanh = _sems["s_tanh"]; s_bl = _sems["s_bl"]; s_fl = _sems["s_fl"]; V2 = _sems["V2"]
    TL = B * TLOC
    NTC = max(1, TL // 512)
    TCW = min(512, TL)
    CH = 64
    NCHJ = TLOC // CH
    PI = float(np.pi)
    x_i = DIN("x_i", [JD, 128, TL])
    Wk_i = DIN("Wk", [D, D]); bk_i = DIN("bk", [1, D])
    Wv_i = DIN("Wv", [D, D]); bv_i = DIN("bv", [1, D])
    Wq_i = DIN("Wq", [D, D]); bq_i = DIN("bq", [1, D])
    Wkp_i = DIN("Wkp", [D, D]); bkp_i = DIN("bkp", [1, D])
    Wqp_i = DIN("Wqp", [D, D]); bqp_i = DIN("bqp", [1, D])
    Wo_i = DIN("Wo", [D, D]); bo_i = DIN("bo", [1, D])
    ps_i = DIN("phs", [128, JD])
    lnwb = DIN("lnwb", [2, 128, JD])
    xout_o = DOUT("xout", [JD, 128, TL])
    xn1_s = DINT("xn1_s", [JD, 128, TL])

    xnb = SB("xnb", [128, JD * TL], BF16)
    kqb = SB("kqb", [128, JD * TL], BF16)
    vbuf = SB("vbuf", [128, JD * TL], BF16)
    mrb = SB("mrb", [128, JD * TL], BF16)
    mib = SB("mib", [128, JD * TL], BF16)
    wtf = SB("wtf", [128, 512], F32)
    wtb = SB("wtb", [128, JD * D], BF16)
    brow = SB("brow", [65, 2 * D], BF16)

    def bro(q, lo, w_):
        p0 = 32 * (q % 3)
        c0_ = D * (q // 3)
        return brow[p0:p0 + 1, c0_ + lo:c0_ + lo + w_]

    def onr(q, w_):
        p0 = 32 * (q % 3)
        return ones_row[p0:p0 + 1, 0:w_]
    lnw_sb = SB("lnw_sb", [128, 2 * JD], F32)
    ps_sb = SB("ps_sb", [128, JD], F32)
    onescol = SB("onescol", [128, 1], BF16)
    ones_row = SB("ones_row", [65, TCW], BF16)
    st_f = SB("st_f", [1, 4 * TCW], F32)
    st_b = SB("st_b", [1, 2 * TCW], BF16)
    xb_c = SB("xb_c", [128, TCW], BF16)
    sq_c = SB("sq_c", [128, TCW], BF16)
    t_y = SB("t_y", [128, TCW], F32)
    t_z = SB("t_z", [128, TCW], F32)
    t_w = SB("t_w", [128, TCW], F32)
    pA = PS("pA", [128, TCW], F32)
    pB = PS("pB", [128, TCW], F32)
    pC = PS("pC", [128, TCW], F32)
    pD = PS("pD", [128, TCW], F32)

    v, s, t, g, sy = nc.vector, nc.scalar, nc.tensor, nc.gpsimd, nc.sync

    sy.dma_start(out=lnw_sb[:, 0:JD], in_=lnwb[0]).then_inc(Ld.h, 16); Ld.val += 16
    sy.dma_start(out=lnw_sb[:, JD:2 * JD], in_=lnwb[1]).then_inc(Ld.h, 16); Ld.val += 16
    sy.dma_start(out=ps_sb[:], in_=ps_i).then_inc(Ld.h, 16); Ld.val += 16
    for q, src in enumerate([bk_i, bv_i, bq_i, bkp_i, bqp_i, bo_i]):
        for hh in range(2):
            Ld.wait(sy); V.wait(sy)
            sy.dma_start(out=wtf[0:1, :], in_=src[:, 512 * hh:512 * (hh + 1)]).then_inc(Ld.h, 16)
            Ld.val += 16
            Ld.wait(v)
            V.inc(v.tensor_copy(bro(q, 512 * hh, 512), wtf[0:1, :]))
    v.memset(onescol[:], 1.0)
    v.memset(ones_row[:], 1.0)
    ei = V.inc(v.drain())
    V.wait(t, ei); V.wait(sy)

    def load_w(wsrc):
        for k in range(JD):
            for hh in range(2):
                Lw.wait(sy); V.wait(sy); T.wait(sy)
                sy.dma_start(out=wtf[:], in_=wsrc[128 * k:128 * (k + 1),
                                                  512 * hh:512 * (hh + 1)]).then_inc(Lw.h, 16)
                Lw.val += 16
                Lw.wait(v)
                V.inc(v.tensor_copy(wtb[:, k * D + 512 * hh: k * D + 512 * (hh + 1)], wtf[:]))
        ei = V.inc(v.drain())
        V.wait(t, ei)

    def proj(src_sb, dst_sb, brow_off):
        for m8 in range(JD):
            for tc in range(NTC):
                c0 = tc * TCW
                for k in range(JD):
                    t.matmul(pA[:, 0:TCW], wtb[:, k * D + 128 * m8: k * D + 128 * (m8 + 1)],
                             src_sb[:, k * TL + c0: k * TL + c0 + TCW],
                             start=(k == 0), stop=False)
                mm = t.matmul(pA[:, 0:TCW], bro(brow_off, 128 * m8, 128),
                              onr(brow_off, TCW), start=False, stop=True)
                eit = T.inc(mm)
                T.wait(v, eit)
                ei = V.inc(v.tensor_copy(dst_sb[:, m8 * TL + c0: m8 * TL + c0 + TCW],
                                         pA[:, 0:TCW]))
                V.wait(t, ei)

    def emit_trig(phase_src_psum, m8, tc):
        c0 = tc * TCW
        eit = T.inc(phase_src_psum)
        T.wait(s, eit)
        ei = S.inc(s.activation(t_y[:], pA[:, 0:TCW], AF.Tanh))
        S.wait(t, ei)
        S.wait(v, ei)
        v.tensor_scalar(t_y[:], t_y[:], ps_sb[:, m8:m8 + 1], None, ALU.mult)
        v.drain()
        v.tensor_scalar_mul(t_z[:], t_y[:], 1.0 / (2 * PI))
        v.drain()
        v.tensor_copy(t_w[:].bitcast(I32), t_z[:])
        v.drain()
        v.tensor_copy(t_z[:], t_w[:].bitcast(I32))
        v.drain()
        v.scalar_tensor_tensor(t_y[:], t_z[:], -2 * PI, t_y[:], ALU.mult, ALU.add)
        v.drain()
        v.tensor_scalar(t_z[:], t_y[:], PI, -2 * PI, ALU.is_gt, ALU.mult)
        v.drain()
        v.tensor_add(t_y[:], t_y[:], t_z[:])
        v.drain()
        v.tensor_scalar(t_z[:], t_y[:], -PI, 2 * PI, ALU.is_lt, ALU.mult)
        v.drain()
        v.tensor_add(t_y[:], t_y[:], t_z[:])
        v.drain()

    def sin_of(src_ap, dst_ap):
        ei = V.inc(v.drain())
        V.wait(s, ei)
        ei = S.inc(s.activation(dst_ap, src_ap, AF.Sin))
        S.wait(v, ei)

    for tc in range(NTC):
        c0 = tc * TCW
        V.wait(sy); S.wait(sy); T.wait(sy); Ld.wait(sy)
        for j in range(JD):
            sy.dma_start(out=t_w[:], in_=x_i[j][:, c0:c0 + TCW]).then_inc(Ld.h, 16)
            Ld.val += 16
            Ld.wait(v); Ld.wait(s)
            ei = V.inc(v.tensor_copy(xb_c[:], t_w[:]))
            eis = S.inc(s.activation(sq_c[:], t_w[:], AF.Square))
            V.wait(t, ei); S.wait(t, eis)
            t.matmul(pA[0:1, 0:TCW], onescol[:], xb_c[:], start=(j == 0), stop=(j == JD - 1))
            mm2 = t.matmul(pB[0:1, 0:TCW], onescol[:], sq_c[:], start=(j == 0), stop=(j == JD - 1))
            eit = T.inc(mm2)
            T.wait(v, eit); T.wait(s, eit); T.wait(sy, eit)
            V.wait(sy)
        v.tensor_scalar_mul(st_f[:, 0:TCW], pA[0:1, 0:TCW], 1.0 / D)
        v.drain()
        v.tensor_scalar_mul(st_f[:, TCW:2 * TCW], pB[0:1, 0:TCW], 1.0 / D)
        v.drain()
        v.tensor_mul(st_f[:, 3 * TCW:4 * TCW], st_f[:, 0:TCW], st_f[:, 0:TCW])
        v.drain()
        v.tensor_sub(st_f[:, TCW:2 * TCW], st_f[:, TCW:2 * TCW], st_f[:, 3 * TCW:4 * TCW])
        v.drain()
        v.tensor_scalar_add(st_f[:, TCW:2 * TCW], st_f[:, TCW:2 * TCW], 1e-5)
        ei = V.inc(v.drain())
        V.wait(s, ei)
        ei = S.inc(s.activation(st_f[:, 2 * TCW:3 * TCW], st_f[:, TCW:2 * TCW], AF.Sqrt))
        S.wait(v, ei)
        v.reciprocal(st_f[:, 2 * TCW:3 * TCW], st_f[:, 2 * TCW:3 * TCW])
        v.drain()
        v.tensor_copy(st_b[:, 0:TCW], st_f[:, 0:TCW])
        v.drain()
        ei = V.inc(v.tensor_copy(st_b[:, TCW:2 * TCW], st_f[:, 2 * TCW:3 * TCW]))
        V.wait(t, ei)
        mm1 = t.matmul(pC[:, 0:TCW], ones_row[0:1, 0:128], st_b[:, 0:TCW], start=True, stop=True)
        mm2 = t.matmul(pD[:, 0:TCW], ones_row[0:1, 0:128], st_b[:, TCW:2 * TCW], start=True, stop=True)
        eit = T.inc(mm2)
        T.wait(v, eit)
        T.wait(sy, eit)
        for j in range(JD):
            sy.dma_start(out=t_w[:], in_=x_i[j][:, c0:c0 + TCW]).then_inc(Ld.h, 16)
            Ld.val += 16
            Ld.wait(v)
            v.tensor_sub(t_y[:], t_w[:], pC[:, 0:TCW])
            v.drain()
            v.tensor_mul(t_z[:], t_y[:], pD[:, 0:TCW])
            v.drain()
            ei = V.inc(v.tensor_copy(xnb[:, j * TL + c0: j * TL + c0 + TCW], t_z[:]))
            V.wait(g, ei)
            g.wait_ge(St.h, St.val)
            g.dma_start(out=xn1_s[j][:, c0:c0 + TCW], in_=t_z[:]).then_inc(St.h, 16)
            St.val += 16
            St.wait(v)
            V.wait(sy)
        V.wait(t, V.val)

    load_w(Wv_i)
    proj(xnb, vbuf, 1)
    load_w(Wk_i)
    proj(xnb, kqb, 0)
    load_w(Wkp_i)
    for m8 in range(JD):
        for tc in range(NTC):
            c0 = tc * TCW
            for k in range(JD):
                t.matmul(pA[:, 0:TCW], wtb[:, k * D + 128 * m8: k * D + 128 * (m8 + 1)],
                         kqb[:, k * TL + c0: k * TL + c0 + TCW],
                         start=(k == 0), stop=False)
            mm = t.matmul(pA[:, 0:TCW], bro(3, 128 * m8, 128),
                          onr(3, TCW), start=False, stop=True)
            emit_trig(mm, m8, tc)
            vsl = vbuf[:, m8 * TL + c0: m8 * TL + c0 + TCW]
            sin_of(t_y[:], t_w[:])
            v.tensor_mul(mib[:, m8 * TL + c0: m8 * TL + c0 + TCW], vsl, t_w[:])
            v.drain()
            v.tensor_scalar_add(t_y[:], t_y[:], PI / 2)
            v.drain()
            v.tensor_scalar(t_z[:], t_y[:], PI, -2 * PI, ALU.is_gt, ALU.mult)
            v.drain()
            v.tensor_add(t_y[:], t_y[:], t_z[:])
            sin_of(t_y[:], t_w[:])
            ei = V.inc(v.tensor_mul(mrb[:, m8 * TL + c0: m8 * TL + c0 + TCW], vsl, t_w[:]))
            V.wait(s, ei)

    for j in range(JD):
        jo = j * TL
        for b_ in range(B):
            for ch in range(NCHJ):
                off = jo + ch * CH * B + b_
                e_ = off + (CH - 1) * B + 1
                v.tensor_tensor_scan(mrb[:, off:e_:B], mrb[:, off:e_:B],
                                     mrb[:, off:e_:B], 0.0, ALU.add, ALU.bypass)
                v.tensor_tensor_scan(mib[:, off:e_:B], mib[:, off:e_:B],
                                     mib[:, off:e_:B], 0.0, ALU.add, ALU.bypass)
        v.drain()

    load_w(Wq_i)
    proj(xnb, kqb, 2)
    load_w(Wqp_i)
    for m8 in range(JD):
        for tc in range(NTC):
            c0 = tc * TCW
            sl = slice(m8 * TL + c0, m8 * TL + c0 + TCW)
            for k in range(JD):
                t.matmul(pA[:, 0:TCW], wtb[:, k * D + 128 * m8: k * D + 128 * (m8 + 1)],
                         kqb[:, k * TL + c0: k * TL + c0 + TCW],
                         start=(k == 0), stop=False)
            mm = t.matmul(pA[:, 0:TCW], bro(4, 128 * m8, 128),
                          onr(4, TCW), start=False, stop=True)
            emit_trig(mm, m8, tc)
            sin_of(t_y[:], t_w[:])
            v.tensor_mul(t_z[:], mib[:, sl], t_w[:])
            v.drain()
            v.tensor_scalar_add(t_y[:], t_y[:], PI / 2)
            v.drain()
            v.tensor_scalar(t_w[:], t_y[:], PI, -2 * PI, ALU.is_gt, ALU.mult)
            v.drain()
            v.tensor_add(t_y[:], t_y[:], t_w[:])
            sin_of(t_y[:], t_w[:])
            v.tensor_mul(t_y[:], mrb[:, sl], t_w[:])
            v.drain()
            ei = V.inc(v.tensor_add(mrb[:, sl], t_y[:], t_z[:]))
            V.wait(s, ei)

    for tc in range(NTC):
        c0 = tc * TCW
        S.wait(v); T.wait(v)
        for j in range(JD):
            xsl = mrb[:, j * TL + c0: j * TL + c0 + TCW]
            eis = S.inc(s.activation(sq_c[:], xsl, AF.Square))
            S.wait(t, eis)
            t.matmul(pA[0:1, 0:TCW], onescol[:], xsl, start=(j == 0), stop=(j == JD - 1))
            mm2 = t.matmul(pB[0:1, 0:TCW], onescol[:], sq_c[:], start=(j == 0), stop=(j == JD - 1))
            eit = T.inc(mm2)
            T.wait(s, eit)
        T.wait(v)
        v.tensor_scalar_mul(st_f[:, 0:TCW], pA[0:1, 0:TCW], 1.0 / D)
        v.drain()
        v.tensor_scalar_mul(st_f[:, TCW:2 * TCW], pB[0:1, 0:TCW], 1.0 / D)
        v.drain()
        v.tensor_mul(st_f[:, 3 * TCW:4 * TCW], st_f[:, 0:TCW], st_f[:, 0:TCW])
        v.drain()
        v.tensor_sub(st_f[:, TCW:2 * TCW], st_f[:, TCW:2 * TCW], st_f[:, 3 * TCW:4 * TCW])
        v.drain()
        v.tensor_scalar_add(st_f[:, TCW:2 * TCW], st_f[:, TCW:2 * TCW], float(D) * 1e-5)
        ei = V.inc(v.drain())
        V.wait(s, ei)
        ei = S.inc(s.activation(st_f[:, 2 * TCW:3 * TCW], st_f[:, TCW:2 * TCW], AF.Sqrt))
        S.wait(v, ei)
        v.reciprocal(st_f[:, 2 * TCW:3 * TCW], st_f[:, 2 * TCW:3 * TCW])
        v.drain()
        v.tensor_copy(st_b[:, 0:TCW], st_f[:, 0:TCW])
        v.drain()
        ei = V.inc(v.tensor_copy(st_b[:, TCW:2 * TCW], st_f[:, 2 * TCW:3 * TCW]))
        V.wait(t, ei)
        mm1 = t.matmul(pC[:, 0:TCW], ones_row[0:1, 0:128], st_b[:, 0:TCW], start=True, stop=True)
        mm2 = t.matmul(pD[:, 0:TCW], ones_row[0:1, 0:128], st_b[:, TCW:2 * TCW], start=True, stop=True)
        eit = T.inc(mm2)
        T.wait(v, eit)
        for j in range(JD):
            sl = slice(j * TL + c0, j * TL + c0 + TCW)
            v.tensor_sub(t_y[:], mrb[:, sl], pC[:, 0:TCW])
            v.drain()
            v.tensor_mul(t_y[:], t_y[:], pD[:, 0:TCW])
            v.drain()
            v.tensor_scalar(t_y[:], t_y[:], lnw_sb[:, j:j + 1], lnw_sb[:, JD + j:JD + j + 1],
                            ALU.mult, ALU.add)
            v.drain()
            v.tensor_copy(mrb[:, sl], t_y[:])
            v.drain()
        ei = V.inc(v.drain())
        V.wait(t, ei)

    load_w(Wo_i)
    for m8 in range(JD):
        for tc in range(NTC):
            c0 = tc * TCW
            for k in range(JD):
                t.matmul(pA[:, 0:TCW], wtb[:, k * D + 128 * m8: k * D + 128 * (m8 + 1)],
                         mrb[:, k * TL + c0: k * TL + c0 + TCW],
                         start=(k == 0), stop=False)
            mm = t.matmul(pA[:, 0:TCW], bro(5, 128 * m8, 128),
                          onr(5, TCW), start=False, stop=True)
            eit = T.inc(mm)
            T.wait(sy, eit)
            V.wait(sy)
            sy.dma_start(out=t_w[:], in_=x_i[m8][:, c0:c0 + TCW]).then_inc(Ld.h, 16)
            Ld.val += 16
            sy.dma_start(out=t_z[:], in_=xn1_s[m8][:, c0:c0 + TCW]).then_inc(Ld.h, 16)
            Ld.val += 16
            T.wait(v, eit)
            Ld.wait(v)
            v.tensor_add(t_y[:], pA[:, 0:TCW], t_w[:])
            v.drain()
            ei = V.inc(v.tensor_add(t_y[:], t_y[:], t_z[:]))
            V.wait(t, ei)
            V.wait(g, ei)
            g.wait_ge(St.h, St.val)
            g.dma_start(out=xout_o[m8][:, c0:c0 + TCW], in_=t_y[:]).then_inc(St.h, 16)
            St.val += 16
            St.wait(v)
    St.wait(g)
    if ctx is not None:
        ctx.barrier()
    return nc



def build_prog_H(TLOC, ctx=None, x_in=None, x_out=None, xaps=None):
    _TAG = "H"
    if ctx is None:
        nc = bass.Bass("TRN2", target_bir_lowering=False, debug=False)
        DIN = lambda nm, sh, dt=F32: din(nc, nm, sh, dt)
        DOUT = lambda nm, sh, dt=F32: dout(nc, nm, sh, dt)
        DINT = lambda nm, sh, dt=F32: dint(nc, nm, sh, dt)
        SB = lambda nm, sh, dt: sb(nc, _TAG + "_" + nm, sh, dt)
        PS = lambda nm, sh, dt=F32: ps(nc, _TAG + "_" + nm, sh, dt)
        _sems = {n: Sem(nc, n) for n in ['Ld', 'Lw', 'St', 'St1', 'Sq', 'Ld2', 'Lp', 'T', 'V', 'S', 's_mm', 's_pre', 's_tanh', 's_bl', 's_fl', 'V2']}
    else:
        nc = ctx.nc
        ctx.reset()
        def DIN(nm, sh, dt=F32):
            if xaps and nm in xaps:
                return xaps[nm]
            if nm == "x_i" and x_in is not None:
                return x_in
            return ctx.din(_TAG + "_" + nm, sh, dt)
        def DOUT(nm, sh, dt=F32):
            if xaps and nm in xaps:
                return xaps[nm]
            if nm == "xout" and x_out is not None:
                return x_out
            return ctx.dout(_TAG + "_" + nm, sh, dt)
        def DINT(nm, sh, dt=F32):
            if xaps and nm in xaps:
                return xaps[nm]
            return ctx.dint(_TAG + "_" + nm, sh, dt)
        SB = lambda nm, sh, dt: ctx.al(sh, dt)
        _PSM = {'pA': 'ctx.pA', 'pB': 'ctx.pB', 'pC': 'ctx.pC', 'pD': 'ctx.pD', 'pT': 'ctx.pT'}
        PS = lambda nm, sh, dt=F32: ctx.psum_by_spec(_PSM[nm])
        _sems = ctx.sems
    Ld = _sems["Ld"]; Lw = _sems["Lw"]; St = _sems["St"]; St1 = _sems["St1"]; Sq = _sems["Sq"]; Ld2 = _sems["Ld2"]; Lp = _sems["Lp"]; T = _sems["T"]; V = _sems["V"]; S = _sems["S"]; s_mm = _sems["s_mm"]; s_pre = _sems["s_pre"]; s_tanh = _sems["s_tanh"]; s_bl = _sems["s_bl"]; s_fl = _sems["s_fl"]; V2 = _sems["V2"]
    import os as _os
    TL = B * TLOC
    NTC = max(1, TL // 512)
    TCW = min(512, TL)
    EH = 2 * D
    NPT = TL // 128
    NG = NPT // 4

    x_i = DIN("x_i", [JD, 128, TL])
    lnwb = DIN("lnwb", [2, 128, JD])
    wuT = DIN("wuT", [D, EH])
    bu_i = DIN("bu", [1, EH])
    wvT = DIN("wvT", [D, EH])
    bv_i = DIN("bv", [1, EH])
    wbT = DIN("wbT", [D, 128])
    bb_i = DIN("bb", [128, 1])
    qaff = DIN("qaff", [128, 4])
    woT = DIN("woT", [EH, D])
    bo_i = DIN("bo", [1, D])
    wrel = DIN("wrel", [1, 1023])
    costab = DIN("costab", [128, TL], BF16)
    sintab = DIN("sintab", [128, TL], BF16)
    maskt = DIN("maskt", [128, 128], BF16)
    aind = DIN("aind", [4, 128], BF16)
    identb = DIN("identb", [128, 128], BF16)
    xout_o = DOUT("xout", [JD, 128, TL])
    xn1_s = DINT("xn1_s", [JD, 128, TL])

    xnb = SB("xnb", [128, JD * TL], BF16)
    wub = SB("wub", [128, JD * EH], BF16)
    wvb = SB("wvb", [128, JD * EH], BF16)
    wob = SB("wob", [128, 16 * D], BF16)
    wtf = SB("wtf", [128, 1024], F32)
    base_b = SB("base_b", [128, TL], BF16)
    qq_b = SB("qq_b", [128, TL], BF16)
    qk_b = SB("qk_b", [128, TL], BF16)
    cs_sb = SB("cs_sb", [128, max(TL, EH)], BF16)
    sn_sb = SB("sn_sb", [128, max(TL, EH)], BF16)
    u_tm = cs_sb
    v_tm = sn_sb
    prod_fm = SB("prod_fm", [128, 16 * 512], BF16)
    mask_sb = SB("mask_sb", [128, 128], BF16)
    aind_sb = SB("aind_sb", [4, 128], BF16)
    b4_sb = SB("b4_sb", [4, 128], BF16)
    idb = SB("idb", [128, 128], BF16)
    w7 = SB("w7", [1, 16], F32)
    w7b = SB("w7b", [1, 128], BF16)
    lnw_sb = SB("lnw_sb", [128, 2 * JD], F32)
    bbp = SB("bbp", [128, 1], F32)
    qaf = SB("qaf", [128, 8], F32)
    brow = SB("brow", [1, 2 * EH + D], BF16)
    onescol = SB("onescol", [128, 1], BF16)
    ones_row = SB("ones_row", [1, TCW], BF16)
    st_f = SB("st_f", [1, 4 * TCW], F32)
    st_b = SB("st_b", [1, 2 * TCW], BF16)
    xb_c = SB("xb_c", [128, TCW], BF16)
    sq_c = SB("sq_c", [128, TCW], BF16)
    t_y = SB("t_y", [128, TCW], F32)
    t_z = SB("t_z", [128, TCW], F32)
    t_w = SB("t_w", [128, TCW], F32)
    sG = SB("sG", [128, 128], F32)
    sGb = SB("sGb", [128, 128], BF16)
    sS = SB("sS", [128, 128], BF16)
    pA = PS("pA", [128, TCW], F32)
    pB = PS("pB", [128, TCW], F32)
    pC = PS("pC", [128, TCW], F32)
    pD = PS("pD", [128, TCW], F32)
    pT = PS("pT", [128, 128], BF16)

    v, s, t, g, sy = nc.vector, nc.scalar, nc.tensor, nc.gpsimd, nc.sync

    for dst, src in [(cs_sb[:, 0:TL], costab), (sn_sb[:, 0:TL], sintab), (mask_sb, maskt),
                     (aind_sb, aind), (idb, identb), (lnw_sb[:, 0:JD], lnwb[0]),
                     (lnw_sb[:, JD:2 * JD], lnwb[1]), (bbp, bb_i), (qaf[:, 0:4], qaff)]:
        sy.dma_start(out=dst if not hasattr(dst, "shape") else dst[:] if hasattr(dst, "ap") else dst,
                     in_=src).then_inc(Ld.h, 16)
        Ld.val += 16
    sy.dma_start(out=w7[0:1, 0:7], in_=wrel[:, 508:515]).then_inc(Ld.h, 16)
    Ld.val += 16
    Ld.wait(v)
    v.tensor_scalar_mul(qaf[:, 4:8], qaf[:, 0:4], 0.125)
    v.drain()
    v.memset(onescol[:], 1.0)
    v.memset(ones_row[:], 1.0)
    for r in range(4):
        for b_ in range(4):
            v.tensor_copy(w7b[0:1, b_:b_ + 125:4],
                          w7[0:1, 3 - r + b_:3 - r + b_ + 1].to_broadcast([1, 32]))
        ei = V.inc(v.drain())
        V.wait(g, ei)
        g.wait_ge(St.h, St.val)
        g.dma_start(out=b4_sb[r:r + 1, :].bitcast(BF16) if False else b4_sb[r:r + 1, :],
                    in_=w7b[0:1, :]).then_inc(St.h, 16)
        St.val += 16
        St.wait(v)
    ei = V.inc(v.drain())
    V.wait(t, ei)

    def load_cast(dst, src, cols, koff=0):
        for k in range(JD if cols != 16 else 16):
            pass
    for k in range(JD):
        for hh in range(2):
            Lw.wait(sy); V.wait(sy)
            sy.dma_start(out=wtf[:], in_=wuT[128 * k:128 * (k + 1),
                                            1024 * hh:1024 * (hh + 1)]).then_inc(Lw.h, 16)
            Lw.val += 16
            Lw.wait(v)
            V.inc(v.tensor_copy(wub[:, k * EH + 1024 * hh: k * EH + 1024 * (hh + 1)], wtf[:]))
    for k in range(JD):
        for hh in range(2):
            Lw.wait(sy); V.wait(sy)
            sy.dma_start(out=wtf[:], in_=wvT[128 * k:128 * (k + 1),
                                            1024 * hh:1024 * (hh + 1)]).then_inc(Lw.h, 16)
            Lw.val += 16
            Lw.wait(v)
            V.inc(v.tensor_copy(wvb[:, k * EH + 1024 * hh: k * EH + 1024 * (hh + 1)], wtf[:]))
    for k in range(16):
        Lw.wait(sy); V.wait(sy)
        sy.dma_start(out=wtf[:], in_=woT[128 * k:128 * (k + 1), :]).then_inc(Lw.h, 16)
        Lw.val += 16
        Lw.wait(v)
        V.inc(v.tensor_copy(wob[:, k * D:(k + 1) * D], wtf[:]))
    wbb = SB("wbb", [128, JD * 128], BF16)
    for k in range(JD):
        Lw.wait(sy); V.wait(sy)
        sy.dma_start(out=wtf[:, 0:128], in_=wbT[128 * k:128 * (k + 1), :]).then_inc(Lw.h, 16)
        Lw.val += 16
        Lw.wait(v)
        V.inc(v.tensor_copy(wbb[:, k * 128:(k + 1) * 128], wtf[:, 0:128]))
    for q, (src, w_) in enumerate([(bu_i, EH), (bv_i, EH), (bo_i, D)]):
        for hh in range(w_ // 1024):
            Lw.wait(sy); V.wait(sy)
            sy.dma_start(out=wtf[0:1, :], in_=src[:, 1024 * hh:1024 * (hh + 1)]).then_inc(Lw.h, 16)
            Lw.val += 16
            Lw.wait(v)
            V.inc(v.tensor_copy(brow[0:1, q * EH + 1024 * hh: q * EH + 1024 * (hh + 1)],
                                wtf[0:1, :]))
    ei = V.inc(v.drain())
    V.wait(t, ei)

    yjs = prod_fm.bitcast(F32)
    for tc in range(NTC):
        c0 = tc * TCW

        def ln_pass(src_getter, eps):
            for j in range(JD):
                xsl = src_getter(j)
                ei = V.inc(v.tensor_copy(xb_c[:], xsl))
                eis = S.inc(s.activation(sq_c[:], xsl, AF.Square))
                V.wait(t, ei); S.wait(t, eis)
                t.matmul(pA[0:1, 0:TCW], onescol[:], xb_c[:], start=(j == 0), stop=(j == JD - 1))
                mm2 = t.matmul(pB[0:1, 0:TCW], onescol[:], sq_c[:], start=(j == 0), stop=(j == JD - 1))
                eit = T.inc(mm2)
                T.wait(v, eit); T.wait(s, eit)
            v.tensor_scalar_mul(st_f[:, 0:TCW], pA[0:1, 0:TCW], 1.0 / D)
            v.drain()
            v.tensor_scalar_mul(st_f[:, TCW:2 * TCW], pB[0:1, 0:TCW], 1.0 / D)
            v.drain()
            v.tensor_mul(st_f[:, 3 * TCW:4 * TCW], st_f[:, 0:TCW], st_f[:, 0:TCW])
            v.drain()
            v.tensor_sub(st_f[:, TCW:2 * TCW], st_f[:, TCW:2 * TCW], st_f[:, 3 * TCW:4 * TCW])
            v.drain()
            v.tensor_scalar_add(st_f[:, TCW:2 * TCW], st_f[:, TCW:2 * TCW], eps)
            ei = V.inc(v.drain())
            V.wait(s, ei)
            ei = S.inc(s.activation(st_f[:, 2 * TCW:3 * TCW], st_f[:, TCW:2 * TCW], AF.Sqrt))
            S.wait(v, ei)
            v.reciprocal(st_f[:, 2 * TCW:3 * TCW], st_f[:, 2 * TCW:3 * TCW])
            v.drain()
            v.tensor_copy(st_b[:, 0:TCW], st_f[:, 0:TCW])
            v.drain()
            ei = V.inc(v.tensor_copy(st_b[:, TCW:2 * TCW], st_f[:, 2 * TCW:3 * TCW]))
            V.wait(t, ei)
            mm1 = t.matmul(pC[:, 0:TCW], ones_row[0:1, 0:128], st_b[:, 0:TCW], start=True, stop=True)
            mm2 = t.matmul(pD[:, 0:TCW], ones_row[0:1, 0:128], st_b[:, TCW:2 * TCW], start=True, stop=True)
            eit = T.inc(mm2)
            T.wait(v, eit)

        V.wait(sy); S.wait(sy); T.wait(sy); Ld.wait(sy)
        for j in range(JD):
            sy.dma_start(out=yjs[:, j * TCW:(j + 1) * TCW],
                         in_=x_i[j][:, c0:c0 + TCW]).then_inc(Ld.h, 16)
        Ld.val += 16 * JD
        Ld.wait(v); Ld.wait(s)
        S.wait(v)
        ln_pass(lambda j: yjs[:, j * TCW:(j + 1) * TCW], 1e-5)
        for j in range(JD):
            v.tensor_sub(t_y[:], yjs[:, j * TCW:(j + 1) * TCW], pC[:, 0:TCW])
            v.drain()
            ei = V.inc(v.tensor_mul(yjs[:, j * TCW:(j + 1) * TCW], t_y[:], pD[:, 0:TCW]))
            v.drain()
            V.wait(g, ei)
            g.wait_ge(St.h, St.val)
            g.dma_start(out=xn1_s[j][:, c0:c0 + TCW],
                        in_=yjs[:, j * TCW:(j + 1) * TCW]).then_inc(St.h, 16)
            St.val += 16
        St.wait(v)
        v.drain()
        V.wait(s); V.wait(t)
        ln_pass(lambda j: yjs[:, j * TCW:(j + 1) * TCW], 1e-5)
        for j in range(JD):
            v.tensor_sub(t_y[:], yjs[:, j * TCW:(j + 1) * TCW], pC[:, 0:TCW])
            v.drain()
            v.tensor_mul(t_z[:], t_y[:], pD[:, 0:TCW])
            v.drain()
            v.tensor_scalar(t_z[:], t_z[:], lnw_sb[:, j:j + 1], lnw_sb[:, JD + j:JD + j + 1],
                            ALU.mult, ALU.add)
            v.drain()
            v.tensor_copy(xnb[:, j * TL + c0: j * TL + c0 + TCW], t_z[:])
            v.drain()
        ei = V.inc(v.drain())
        V.wait(t, ei)

    if _os.environ.get("HDBG") == "1":
        for j in range(JD):
            V.wait(g, V.inc(v.tensor_copy(t_y[:, 0:TCW], xnb[:, j * TL: j * TL + TCW])))
            g.wait_ge(St.h, St.val)
            g.dma_start(out=xout_o[j][:, 0:TCW], in_=t_y[:, 0:TCW]).then_inc(St.h, 16)
            St.val += 16
            St.wait(v)
        St.wait(g)
        if ctx is not None:
            ctx.barrier()
        return nc
    for tc in range(NTC):
        c0 = tc * TCW
        for k in range(JD):
            mm = t.matmul(pA[:, 0:TCW], wbb[:, k * 128:(k + 1) * 128],
                          xnb[:, k * TL + c0: k * TL + c0 + TCW],
                          start=(k == 0), stop=(k == JD - 1))
        eit = T.inc(mm)
        T.wait(s, eit)
        ei = S.inc(s.activation(base_b[:, c0:c0 + TCW], pA[:, 0:TCW], AF.Silu, bias=bbp[:]))
        S.wait(t, ei); S.wait(v, ei)
    for (dst, wq, bq2) in [(qq_b, 4, 5), (qk_b, 6, 7)]:
        v.tensor_scalar(t_y[:, 0:128] if False else dst[:], base_b[:],
                        qaf[:, wq:wq + 1], qaf[:, bq2:bq2 + 1], ALU.mult, ALU.add)
        v.drain()
    for (dst,) in [(qq_b,), (qk_b,)]:
        for tc in range(NTC):
            c0 = tc * TCW
            v.tensor_copy(t_w[0:64, 0:TCW], dst[64:128, c0:c0 + TCW])
            v.drain()
            v.tensor_copy(t_w[64:128, 0:TCW], dst[0:64, c0:c0 + TCW])
            v.drain()
            v.tensor_mul(t_y[0:64, 0:TCW], dst[0:64, c0:c0 + TCW], cs_sb[0:64, c0:c0 + TCW])
            v.drain()
            v.tensor_mul(t_z[0:64, 0:TCW], t_w[0:64, 0:TCW], sn_sb[0:64, c0:c0 + TCW])
            v.drain()
            v.tensor_mul(t_y[64:128, 0:TCW], dst[64:128, c0:c0 + TCW], cs_sb[64:128, c0:c0 + TCW])
            v.drain()
            v.tensor_mul(t_z[64:128, 0:TCW], t_w[64:128, 0:TCW], sn_sb[64:128, c0:c0 + TCW])
            v.drain()
            v.tensor_sub(dst[0:64, c0:c0 + TCW], t_y[0:64, 0:TCW], t_z[0:64, 0:TCW])
            v.drain()
            v.tensor_add(dst[64:128, c0:c0 + TCW], t_y[64:128, 0:TCW], t_z[64:128, 0:TCW])
            v.drain()
    ei = V.inc(v.drain())
    V.wait(t, ei)

    if _os.environ.get("HDBG") == "2":
        for (nm, buf) in [(0, base_b), (1, qq_b), (2, qk_b)]:
            V.wait(g, V.inc(v.tensor_copy(t_y[:, 0:TCW], buf[:, 0:TCW])))
            g.wait_ge(St.h, St.val)
            g.dma_start(out=xout_o[nm][:, 0:TCW], in_=t_y[:, 0:TCW]).then_inc(St.h, 16)
            St.val += 16
            St.wait(v)
        St.wait(g)
        if ctx is not None:
            ctx.barrier()
        return nc
    for tg in range(NG):
        for ti in range(4):
            pt0 = (tg * 4 + ti) * 128
            for (dstb, wsrc, bq) in [(u_tm, wub, 0), (v_tm, wvb, 1)]:
                for ec in range(4):
                    for k in range(JD):
                        t.matmul(pA[:, 0:TCW],
                                 xnb[:, k * TL + pt0: k * TL + pt0 + 128],
                                 wsrc[:, k * EH + 512 * ec: k * EH + 512 * (ec + 1)],
                                 start=(k == 0), stop=False)
                    mm = t.matmul(pA[:, 0:TCW], ones_row[0:1, 0:128],
                                  brow[0:1, bq * EH + 512 * ec: bq * EH + 512 * (ec + 1)],
                                  start=False, stop=True)
                    eit = T.inc(mm)
                    T.wait(s, eit)
                    ei = S.inc(s.activation(dstb[:, 512 * ec:512 * (ec + 1)], pA[:, 0:TCW], AF.Silu))
                    S.wait(t, ei)
            t.matmul(pA[:, 0:128], qq_b[:, pt0:pt0 + 128], qk_b[:, pt0:pt0 + 128],
                     start=True, stop=False)
            mm = t.matmul(pA[:, 0:128], aind_sb[0:4, :], b4_sb[0:4, :], start=False, stop=True)
            eit = T.inc(mm)
            T.wait(v, eit)
            v.tensor_relu(sG[:], pA[:, 0:128])
            ei = V.inc(v.drain())
            V.wait(s, ei); V.wait(t, ei)
            ei = S.inc(s.activation(sG[:], sG[:], AF.Square))
            S.wait(v, ei)
            ei = V.inc(v.tensor_mul(sGb[:], sG[:], mask_sb[:]))
            V.wait(t, ei)
            mm = t.transpose(pT[:, 0:128], sGb[:], idb[:])
            eit = T.inc(mm)
            T.wait(v, eit)
            ei = V.inc(v.tensor_copy(sS[:], pT[:, 0:128]))
            V.wait(t, ei)
            for ec in range(4):
                mm = t.matmul(pB[:, 0:TCW], sS[:], v_tm[:, 512 * ec:512 * (ec + 1)],
                              start=True, stop=True)
                eit = T.inc(mm)
                T.wait(v, eit)
                ei = V.inc(v.tensor_mul(xb_c[:], u_tm[:, 512 * ec:512 * (ec + 1)], pB[:, 0:TCW]))
                V.wait(t, ei)
                for q4 in range(4):
                    mm = t.transpose(pT[:, 0:128], xb_c[:, 128 * q4:128 * (q4 + 1)], idb[:])
                    eit = T.inc(mm)
                    T.wait(v, eit)
                    ei = V.inc(v.tensor_copy(
                        prod_fm[:, (ec * 4 + q4) * 512 + ti * 128:(ec * 4 + q4) * 512 + ti * 128 + 128],
                        pT[:, 0:128]))
                    V.wait(t, ei)
                    V.wait(s, ei) if False else None
        if _os.environ.get("HDBG") == "3" and tg == 0:
            for (nm, buf, wd) in [(0, u_tm, EH), (1, v_tm, EH), (2, sS, 128), (3, prod_fm, 512)]:
                wd2 = min(wd, TCW)
                V.wait(g, V.inc(v.tensor_copy(t_y[:, 0:wd2], buf[:, 0:wd2])))
                g.wait_ge(St.h, St.val)
                g.dma_start(out=xout_o[nm][:, 0:wd2], in_=t_y[:, 0:wd2]).then_inc(St.h, 16)
                St.val += 16
                St.wait(v)
            St.wait(g)
            if ctx is not None:
                ctx.barrier()
            return nc
        gc0 = tg * 512
        for m8 in range(JD):
            for ke in range(16):
                t.matmul(pC[:, 0:TCW], wob[:, ke * D + 128 * m8: ke * D + 128 * (m8 + 1)],
                         prod_fm[:, ke * 512:(ke + 1) * 512],
                         start=(ke == 0), stop=False)
            mm = t.matmul(pC[:, 0:TCW], ones_row[0:1, 0:TCW] if False else brow[0:1, 2 * EH + 128 * m8: 2 * EH + 128 * (m8 + 1)],
                          ones_row[0:1, 0:TCW], start=False, stop=True)
            eit = T.inc(mm)
            T.wait(sy, eit)
            V.wait(sy)
            sy.dma_start(out=t_w[:], in_=x_i[m8][:, gc0:gc0 + TCW]).then_inc(Ld.h, 16)
            Ld.val += 16
            sy.dma_start(out=t_z[:], in_=xn1_s[m8][:, gc0:gc0 + TCW]).then_inc(Ld.h, 16)
            Ld.val += 16
            T.wait(v, eit)
            Ld.wait(v)
            v.tensor_add(t_y[:], pC[:, 0:TCW], t_w[:])
            v.drain()
            ei = V.inc(v.tensor_add(t_y[:], t_y[:], t_z[:]))
            V.wait(t, ei)
            V.wait(g, ei)
            g.wait_ge(St.h, St.val)
            g.dma_start(out=xout_o[m8][:, gc0:gc0 + TCW], in_=t_y[:]).then_inc(St.h, 16)
            St.val += 16
            St.wait(v)
    St.wait(g)
    if ctx is not None:
        ctx.barrier()
    return nc



_PROGS = {}


def _get_progs(TLOC, only=None):
    names = only or ["A", "B", "C", "D", "E", "F", "H", "I", "J"]
    builders = {"A": lambda: build_prog_A(TLOC), "B": lambda: build_prog_B(TLOC),
                "C": lambda: build_prog_C(TLOC), "D": lambda: build_prog_D(TLOC),
                "E": lambda: build_prog_E(NCORES * TLOC, NCORES),
                "F": lambda: build_prog_F(TLOC), "H": lambda: build_prog_H(TLOC),
                "I": lambda: build_prog_I(TLOC), "J": lambda: build_prog_J(TLOC)}
    for n in names:
        if n not in _PROGS:
            _PROGS[n] = builders[n]()
    return _PROGS


def _from_fm(xo, TLOC):
    return np.asarray(xo).reshape(D, TLOC, B).transpose(2, 1, 0)


def kernel_unfused(**inp):
    TLOC = 512
    TL = B * TLOC
    P = _get_progs(TLOC)
    run = lambda nc_, ins: run_bass_kernel_spmd(nc_, ins, core_ids=list(range(NCORES))).results

    vec = lambda a: np.ascontiguousarray(np.asarray(a, np.float32).reshape(-1))
    x0 = np.asarray(inp["L0_attn_a_x"], np.float32)

    mixes = [vec(inp[f"L0_attn_a_x_{m}"]) for m in "rwkvag"]
    wlist = [np.asarray(inp[f"L0_attn_a_W_{m}"], np.float32) for m in "rwkvag"]
    insA = [host_prep_A(x0, mixes, wlist, c, TLOC) for c in range(NCORES)]
    outA = run(P["A"], insA)
    insB = [host_prep_B(outA, vec(inp["L0_attn_a_gn_w"]), vec(inp["L0_attn_a_gn_b"]),
                        np.asarray(inp["L0_attn_a_W_out"], np.float32),
                        vec(inp["L0_attn_a_b_out"]), c, TLOC) for c in range(NCORES)]
    outB = run(P["B"], insB)
    X = [np.asarray(o["xout"]) for o in outB]

    statC = host_prep_C(None, np.asarray(inp["L0_ffn_gate_w"], np.float32),
                        vec(inp["L0_ffn_gate_b"]),
                        np.asarray(inp["L0_ffn_w1"], np.float32),
                        np.asarray(inp["L0_ffn_b1"], np.float32),
                        np.asarray(inp["L0_ffn_w2"], np.float32),
                        np.asarray(inp["L0_ffn_b2"], np.float32), TLOC)
    insC = []
    for c in range(NCORES):
        d = dict(statC); d["x_i"] = X[c]; insC.append(d)
    outC = run(P["C"], insC)
    X = [np.asarray(o["xout"]) for o in outC]

    insD = [{
        "x_i": X[c],
        "winT": np.ascontiguousarray(np.asarray(inp["L0_attn_b_W_in"], np.float32).T),
        "b_in": vec(inp["L0_attn_b_b_in"]).reshape(1, -1),
        "b_h": vec(inp["L0_attn_b_b_h"]).reshape(1, -1),
        "lnwb": np.stack([vec(inp["L0_attn_b_ln_w"]).reshape(JD, 128).T,
                          vec(inp["L0_attn_b_ln_b"]).reshape(JD, 128).T]).astype(np.float32),
    } for c in range(NCORES)]
    outD = run(P["D"], insD)
    g_all = np.stack([np.asarray(o["g_o"]) for o in outD])
    vb_all = np.stack([np.asarray(o["vb_o"]) for o in outD])
    insE = {"g_i": g_all, "vb_i": vb_all,
            "whT": np.ascontiguousarray(np.asarray(inp["L0_attn_b_W_h"], np.float32).T)}
    outE = run(P["E"], [insE] * NCORES)
    hs_full = np.asarray(outE[0]["hs_o"])
    insF = [{
        "x_i": X[c], "xn2_i": np.asarray(outD[c]["xn2_o"]),
        "hs_i": np.ascontiguousarray(hs_full[:, :, c * TL:(c + 1) * TL]),
        "woutT": np.ascontiguousarray(np.asarray(inp["L0_attn_b_W_out"], np.float32).T),
        "bout": vec(inp["L0_attn_b_b_out"]).reshape(1, -1),
    } for c in range(NCORES)]
    outF = run(P["F"], insF)
    X = [np.asarray(o["xout"]) for o in outF]

    insC2 = []
    for c in range(NCORES):
        d = dict(statC); d["x_i"] = X[c]; insC2.append(d)
    outC2 = run(P["C"], insC2)
    X = [np.asarray(o["xout"]) for o in outC2]

    invf = 10000.0 ** (-np.arange(64, dtype=np.float32) / 64)
    pos_col = np.tile(np.arange(B, dtype=np.float32), TLOC)
    sinus = pos_col[None, :] * np.concatenate([invf, invf])[:, None]
    costab = np.cos(sinus).astype(BF)
    sintab = np.sin(sinus).astype(BF)
    maskt = np.zeros((128, 128), BF)
    for p_ in range(128):
        for q_ in range(128):
            if p_ // 4 == q_ // 4 and q_ % 4 <= p_ % 4:
                maskt[p_, q_] = 1
    aind = np.zeros((4, 128), BF)
    for r_ in range(4):
        aind[r_, r_::4] = 1
    statH = {
        "lnwb": np.stack([vec(inp["L1_attn_a_ln_g"]).reshape(JD, 128).T,
                          vec(inp["L1_attn_a_ln_b"]).reshape(JD, 128).T]).astype(np.float32),
        "wuT": np.ascontiguousarray(np.asarray(inp["L1_attn_a_Wu"], np.float32).T),
        "bu": vec(inp["L1_attn_a_bu"]).reshape(1, -1),
        "wvT": np.ascontiguousarray(np.asarray(inp["L1_attn_a_Wv"], np.float32).T),
        "bv": vec(inp["L1_attn_a_bv"]).reshape(1, -1),
        "wbT": np.ascontiguousarray(np.asarray(inp["L1_attn_a_Wb"], np.float32).T),
        "bb": vec(inp["L1_attn_a_bb"]).reshape(128, 1),
        "qaff": np.stack([vec(inp["L1_attn_a_qqw"]), vec(inp["L1_attn_a_qqb"]),
                          vec(inp["L1_attn_a_qkw"]), vec(inp["L1_attn_a_qkb"])],
                         axis=1).astype(np.float32),
        "woT": np.ascontiguousarray(np.asarray(inp["L1_attn_a_Wo"], np.float32).T),
        "bo": vec(inp["L1_attn_a_bo"]).reshape(1, -1),
        "wrel": vec(inp["L1_attn_a_w_rel"]).reshape(1, -1),
        "costab": costab, "sintab": sintab, "maskt": maskt, "aind": aind,
        "identb": np.eye(128, dtype=BF),
    }
    insH = []
    for c in range(NCORES):
        d = dict(statH); d["x_i"] = X[c]; insH.append(d)
    outH = run(P["H"], insH)
    X = [np.asarray(o["xout"]) for o in outH]

    statI = {
        "gw": np.asarray(inp["L1_ffn_gw"], np.float32),
        "gb": vec(inp["L1_ffn_gb"]).reshape(1, -1),
        "w1": np.asarray(inp["L1_ffn_w1"], np.float32),
        "b1": np.asarray(inp["L1_ffn_b1"], np.float32),
        "w3": np.asarray(inp["L1_ffn_w3"], np.float32),
        "b3": np.asarray(inp["L1_ffn_b3"], np.float32),
        "w2": np.asarray(inp["L1_ffn_w2"], np.float32),
        "b2": np.asarray(inp["L1_ffn_b2"], np.float32),
        "sw1": np.asarray(inp["L1_ffn_sw1"], np.float32),
        "sb1": vec(inp["L1_ffn_sb1"]).reshape(1, -1),
        "sw3": np.asarray(inp["L1_ffn_sw3"], np.float32),
        "sb3": vec(inp["L1_ffn_sb3"]).reshape(1, -1),
        "sw2": np.asarray(inp["L1_ffn_sw2"], np.float32),
        "sb2": vec(inp["L1_ffn_sb2"]).reshape(1, -1),
        "identf": np.eye(128, dtype=np.float32),
    }
    insI = []
    for c in range(NCORES):
        d = dict(statI); d["x_i"] = X[c]; insI.append(d)
    outI = run(P["I"], insI)
    X = [np.asarray(o["xout"]) for o in outI]

    statJ = {
        "Wk": np.asarray(inp["L1_attn_b_Wk"], np.float32), "bk": vec(inp["L1_attn_b_bk"]).reshape(1, -1),
        "Wv": np.asarray(inp["L1_attn_b_Wv"], np.float32), "bv": vec(inp["L1_attn_b_bv"]).reshape(1, -1),
        "Wq": np.asarray(inp["L1_attn_b_Wq"], np.float32), "bq": vec(inp["L1_attn_b_bq"]).reshape(1, -1),
        "Wkp": np.asarray(inp["L1_attn_b_Wkp"], np.float32), "bkp": vec(inp["L1_attn_b_bkp"]).reshape(1, -1),
        "Wqp": np.asarray(inp["L1_attn_b_Wqp"], np.float32), "bqp": vec(inp["L1_attn_b_bqp"]).reshape(1, -1),
        "Wo": np.asarray(inp["L1_attn_b_Wo"], np.float32), "bo": vec(inp["L1_attn_b_bo"]).reshape(1, -1),
        "phs": np.ascontiguousarray(vec(inp["L1_attn_b_phase_scale"]).reshape(JD, 128).T),
        "lnwb": np.stack([vec(inp["L1_attn_b_ln_g"]).reshape(JD, 128).T,
                          vec(inp["L1_attn_b_ln_b"]).reshape(JD, 128).T]).astype(np.float32),
    }
    insJ = []
    for c in range(NCORES):
        d = dict(statJ); d["x_i"] = X[c]; insJ.append(d)
    outJ = run(P["J"], insJ)
    X = [np.asarray(o["xout"]) for o in outJ]

    insI2 = []
    for c in range(NCORES):
        d = dict(statI); d["x_i"] = X[c]; insI2.append(d)
    outI2 = run(P["I"], insI2)
    X = [np.asarray(o["xout"]) for o in outI2]

    out = np.empty((B, NCORES * TLOC, D), np.float32)
    for c in range(NCORES):
        out[:, c * TLOC:(c + 1) * TLOC, :] = _from_fm(X[c], TLOC)
    return out



class Ctx:
    def __init__(self, nc, ncores):
        self.nc = nc
        self.ncores = ncores
        self.dram = {}
        self._live = []
        self._ctr = 0
        self.sems = {n: Sem(nc, n) for n in
                     ["Ld", "Lw", "St", "St1", "Sq", "Ld2", "Lp", "T", "V", "S",
                      "s_mm", "s_pre", "s_tanh", "s_bl", "s_fl", "V2", "CC", "BAR"]}
        self.pA = ps(nc, "pA", [128, 512], F32)
        self.pB = ps(nc, "pB", [128, 512], F32)
        self.pC = ps(nc, "pC", [128, 512], F32)
        self.pD = ps(nc, "pD", [128, 512], F32)
        self.pT = ps(nc, "pTT", [128, 128], BF16)
        self.pz = ps(nc, "pz", [128, 32], F32)

    def reset(self):
        for cm in reversed(self._live):
            cm.__exit__(None, None, None)
        self._live = []

    def psum_by_spec(self, spec):
        spec = spec.replace("ctx.", "")
        if "[" in spec:
            nm, rest = spec.split("[", 1)
            w = int(rest.split("0:")[1].split("]")[0])
            return getattr(self, nm)[:, 0:w]
        return getattr(self, spec)

    def al(self, shape, dtype):
        self._ctr += 1
        cm = self.nc.sbuf_tensor(f"ar{self._ctr}", shape, dtype)
        h = cm.__enter__()
        self._live.append(cm)
        return h

    def din(self, name, shape, dtype=F32):
        if name not in self.dram:
            self.dram[name] = self.nc.dram_tensor(name, shape, dtype,
                                                  kind="ExternalInput").ap()
        return self.dram[name]

    def dint(self, name, shape, dtype=F32):
        if name not in self.dram:
            self.dram[name] = self.nc.dram_tensor(name, shape, dtype,
                                                  kind="Internal").ap()
        return self.dram[name]

    def dout(self, name, shape, dtype=F32):
        if name not in self.dram:
            self.dram[name] = self.nc.dram_tensor(name, shape, dtype,
                                                  kind="ExternalOutput").ap()
        return self.dram[name]

    def barrier(self):
        nc = self.nc
        B_ = self.sems["BAR"]
        base = B_.val
        engs = [nc.tensor, nc.vector, nc.scalar, nc.gpsimd, nc.sync]
        for e in engs:
            e.drain().then_inc(B_.h, 1)
        B_.val += 5
        for e in engs:
            e.wait_ge(B_.h, base + 5)



def build_fused(TLOC, NC_=NCORES):
    TL = B * TLOC
    TG_ = NC_ * TLOC
    nc = bass.Bass("TRN2", target_bir_lowering=False, debug=False)
    ctx = Ctx(nc, NC_)
    v, s, t, g, sy = nc.vector, nc.scalar, nc.tensor, nc.gpsimd, nc.sync
    SE = ctx.sems
    rg = [list(range(NC_))]

    xbufA = ctx.dint("xbufA", [JD, 128, TL])
    xbufB = ctx.dint("xbufB", [JD, 128, TL])
    rg_s = ctx.dint("rg_s", [2, JD, 128, TL], BF16)
    sp_s = ctx.dint("sp_s", [2, JD, 128, TL])
    cst_in = ctx.dint("cst_in", [128, 2 * JD * B])
    cst_out = ctx.dint("cst_out", [NC_ * 128, 2 * JD * B])
    pref_s = ctx.dint("pref_s", [7, 2, 128, JD * B])
    g_cin = ctx.dint("g_cin", [JD * 128, TL], BF16)
    vb_cin = ctx.dint("vb_cin", [JD * 128, TL], BF16)
    g_cout = ctx.dint("g_cout", [NC_ * JD * 128, TL], BF16)
    vb_cout = ctx.dint("vb_cout", [NC_ * JD * 128, TL], BF16)
    hs_s = ctx.dint("hs_s", [JD, 128, B * TG_], BF16)
    xn2_s = ctx.dint("xn2_s", [JD, 128, TL])
    rmask = ctx.din("FU_rmask", [128, 16])
    xout_final = ctx.dout("xout", [JD, 128, TL])

    build_prog_A(TLOC, ctx=ctx, xaps={"xfm": xbufA, "rg": rg_s, "sp": sp_s})

    ctx.reset()
    cstb = ctx.al([128, 2 * JD * B], F32)
    tmp1 = ctx.al([128, JD * B], F32)
    tmp2 = ctx.al([128, JD * B], F32)
    rm_sb = ctx.al([128, 16], F32)
    Ld = SE["Ld"]; St = SE["St"]; V = SE["V"]; CC = SE["CC"]
    sy.dma_start(out=rm_sb[:, :], in_=rmask).then_inc(Ld.h, 16)
    Ld.val += 16
    for q in range(2):
        for j in range(JD):
            sy.dma_start(out=cstb[:, q * JD * B + j * B: q * JD * B + (j + 1) * B],
                         in_=sp_s[1 - q, j][:, TL - B:TL]).then_inc(Ld.h, 16)
            Ld.val += 16
    Ld.wait(g)
    g.dma_start(out=cst_in[:, :], in_=cstb[:, :]).then_inc(St.h, 16)
    St.val += 16
    g.wait_ge(St.h, St.val)
    g.collective_compute("AllGather", ALU.bypass, replica_groups=rg,
                         ins=[cst_in[:, :]], outs=[cst_out[:, :]]).then_inc(CC.h, 16)
    CC.val += 16
    CC.wait(sy)
    for k in range(7):
        V.wait(sy)
        St.wait(v)
        if k < NC_ - 1:
            sy.dma_start(out=cstb[:, :],
                         in_=cst_out[128 * k:128 * (k + 1), :]).then_inc(Ld.h, 16)
            Ld.val += 16
            Ld.wait(v)
            v.tensor_scalar(tmp1[:, :], cstb[:, 0:JD * B], rm_sb[:, k:k + 1],
                            rm_sb[:, 8 + k:8 + k + 1], ALU.mult, ALU.add)
            v.drain()
            v.tensor_scalar(tmp2[:, :], cstb[:, JD * B:2 * JD * B], rm_sb[:, k:k + 1],
                            None, ALU.mult)
        else:
            v.memset(tmp1[:, :], 1.0)
            v.drain()
            v.memset(tmp2[:, :], 0.0)
        ei = V.inc(v.drain())
        V.wait(g, ei)
        g.wait_ge(St.h, St.val)
        g.dma_start(out=pref_s[k, 0], in_=tmp1[:, :]).then_inc(St.h, 16)
        g.dma_start(out=pref_s[k, 1], in_=tmp2[:, :]).then_inc(St.h, 16)
        St.val += 32
    g.wait_ge(St.h, St.val)
    ctx.barrier()

    build_prog_B(TLOC, ctx=ctx, x_out=xbufB,
                 xaps={"xfm_i": xbufA, "rg_i": rg_s, "sp_i": sp_s, "pref": pref_s})

    build_prog_C(TLOC, ctx=ctx, x_in=xbufB, x_out=xbufA)

    build_prog_D(TLOC, ctx=ctx, x_in=xbufA,
                 xaps={"g_o": g_cin.rearrange("(j p) t -> j p t", p=128),
                       "vb_o": vb_cin.rearrange("(j p) t -> j p t", p=128),
                       "xn2_o": xn2_s})

    ctx.barrier()
    CC = SE["CC"]
    g.collective_compute("AllGather", ALU.bypass, replica_groups=rg,
                         ins=[g_cin[:, :]], outs=[g_cout[:, :]]).then_inc(CC.h, 16)
    CC.val += 16
    g.collective_compute("AllGather", ALU.bypass, replica_groups=rg,
                         ins=[vb_cin[:, :]], outs=[vb_cout[:, :]]).then_inc(CC.h, 16)
    CC.val += 16
    for e_ in [nc.tensor, nc.vector, nc.scalar, nc.sync, nc.gpsimd]:
        e_.wait_ge(CC.h, CC.val)

    build_prog_E(TG_, NC_, ctx=ctx,
                 xaps={"g_i": g_cout.rearrange("(r j p) t -> r j p t", j=JD, p=128),
                       "vb_i": vb_cout.rearrange("(r j p) t -> r j p t", j=JD, p=128),
                       "hs_o": hs_s})

    ctx.reset()
    hs_loc = ctx.dint("hs_loc", [JD, 128, TL], BF16)
    cmask = ctx.din("FU_cmask", [128, NC_])
    cm_sb = ctx.al([128, NC_], F32)
    acc_f = ctx.al([128, TL], F32)
    tmp_b = ctx.al([128, TL], BF16)
    Ld = SE["Ld"]; St = SE["St"]; V = SE["V"]
    sy.dma_start(out=cm_sb[:, :], in_=cmask).then_inc(Ld.h, 16)
    Ld.val += 16
    for j in range(JD):
        V.wait(sy)
        St.wait(sy)
        v.memset(acc_f[:, :], 0.0)
        v.drain()
        for r in range(NC_):
            Ld.wait(sy)
            sy.dma_start(out=tmp_b[:, :],
                         in_=hs_s[j][:, r * TL:(r + 1) * TL]).then_inc(Ld.h, 16)
            Ld.val += 16
            Ld.wait(v)
            v.scalar_tensor_tensor(acc_f[:, :], tmp_b[:, :], cm_sb[:, r:r + 1],
                                   acc_f[:, :], ALU.mult, ALU.add)
            ei = V.inc(v.drain())
            V.wait(sy, ei)
        v.tensor_copy(tmp_b[:, :], acc_f[:, :])
        ei = V.inc(v.drain())
        V.wait(g, ei)
        g.wait_ge(St.h, St.val)
        g.dma_start(out=hs_loc[j], in_=tmp_b[:, :]).then_inc(St.h, 16)
        St.val += 16
        St.wait(v)
    ctx.barrier()
    build_prog_F(TLOC, ctx=ctx, x_in=xbufA, x_out=xbufB,
                 xaps={"xn2_i": xn2_s, "hs_i": hs_loc})

    build_prog_C(TLOC, ctx=ctx, x_in=xbufB, x_out=xbufA)

    build_prog_H(TLOC, ctx=ctx, x_in=xbufA, x_out=xbufB)

    build_prog_I(TLOC, ctx=ctx, x_in=xbufB, x_out=xbufA)

    build_prog_J(TLOC, ctx=ctx, x_in=xbufA, x_out=xbufB)

    build_prog_I(TLOC, ctx=ctx, x_in=xbufB, x_out=xout_final)
    return nc


def _mk_common(inp, TLOC):
    TL = B * TLOC
    vec = lambda a: np.ascontiguousarray(np.asarray(a, np.float32).reshape(-1))
    mixes = [vec(inp[f"L0_attn_a_x_{m}"]) for m in "rwkvag"]
    wlist = [np.asarray(inp[f"L0_attn_a_W_{m}"], np.float32) for m in "rwkvag"]
    invf = 10000.0 ** (-np.arange(64, dtype=np.float32) / 64)
    pos_col = np.tile(np.arange(B, dtype=np.float32), TLOC)
    sinus = pos_col[None, :] * np.concatenate([invf, invf])[:, None]
    maskt = np.zeros((128, 128), BF)
    for p_ in range(128):
        for q_ in range(128):
            if p_ // 4 == q_ // 4 and q_ % 4 <= p_ % 4:
                maskt[p_, q_] = 1
    aind = np.zeros((4, 128), BF)
    for r_ in range(4):
        aind[r_, r_::4] = 1
    gsel = np.zeros((128, 2), BF)
    gsel[0:64, 0] = 1; gsel[64:128, 1] = 1
    sel16 = np.zeros((JD, 16, 128), BF)
    for j in range(JD):
        sel16[j, 2 * j, 0:64] = 1
        sel16[j, 2 * j + 1, 64:128] = 1

    common = {
        "A_mixes": np.stack([m.reshape(JD, 128).T for m in mixes]).astype(np.float32),
        "A_wts": np.stack([np.ascontiguousarray(w.T) for w in wlist]).astype(np.float32),
        "A_identb": np.eye(128, dtype=BF),
        "A_identf": np.eye(128, dtype=np.float32),
        "B_gnwb": np.stack([vec(inp["L0_attn_a_gn_w"]).reshape(JD, 128).T,
                            vec(inp["L0_attn_a_gn_b"]).reshape(JD, 128).T]).astype(np.float32),
        "B_woutT": np.ascontiguousarray(np.asarray(inp["L0_attn_a_W_out"], np.float32).T),
        "B_bout": vec(inp["L0_attn_a_b_out"]).reshape(1, D),
        "B_gsel": gsel, "B_sel16": sel16,
        "B_onesr": np.ones((1, TL), np.float32),
        "C_gate_w": np.ascontiguousarray(np.asarray(inp["L0_ffn_gate_w"], np.float32)),
        "C_gate_b": vec(inp["L0_ffn_gate_b"]).reshape(1, -1),
        "C_w1": np.ascontiguousarray(np.asarray(inp["L0_ffn_w1"], np.float32)),
        "C_b1": np.ascontiguousarray(np.asarray(inp["L0_ffn_b1"], np.float32).reshape(8, 1, -1)),
        "C_w2": np.ascontiguousarray(np.asarray(inp["L0_ffn_w2"], np.float32)),
        "C_b2": np.ascontiguousarray(np.asarray(inp["L0_ffn_b2"], np.float32).reshape(8, 1, -1)),
        "C_identf": np.eye(128, dtype=np.float32),
        "D_winT": np.ascontiguousarray(np.asarray(inp["L0_attn_b_W_in"], np.float32).T),
        "D_b_in": vec(inp["L0_attn_b_b_in"]).reshape(1, -1),
        "D_b_h": vec(inp["L0_attn_b_b_h"]).reshape(1, -1),
        "D_lnwb": np.stack([vec(inp["L0_attn_b_ln_w"]).reshape(JD, 128).T,
                            vec(inp["L0_attn_b_ln_b"]).reshape(JD, 128).T]).astype(np.float32),
        "E_whT": np.ascontiguousarray(np.asarray(inp["L0_attn_b_W_h"], np.float32).T),
        "F_woutT": np.ascontiguousarray(np.asarray(inp["L0_attn_b_W_out"], np.float32).T),
        "F_bout": vec(inp["L0_attn_b_b_out"]).reshape(1, D),
        "H_lnwb": np.stack([vec(inp["L1_attn_a_ln_g"]).reshape(JD, 128).T,
                            vec(inp["L1_attn_a_ln_b"]).reshape(JD, 128).T]).astype(np.float32),
        "H_wuT": np.ascontiguousarray(np.asarray(inp["L1_attn_a_Wu"], np.float32).T),
        "H_bu": vec(inp["L1_attn_a_bu"]).reshape(1, -1),
        "H_wvT": np.ascontiguousarray(np.asarray(inp["L1_attn_a_Wv"], np.float32).T),
        "H_bv": vec(inp["L1_attn_a_bv"]).reshape(1, -1),
        "H_wbT": np.ascontiguousarray(np.asarray(inp["L1_attn_a_Wb"], np.float32).T),
        "H_bb": vec(inp["L1_attn_a_bb"]).reshape(128, 1),
        "H_qaff": np.stack([vec(inp["L1_attn_a_qqw"]), vec(inp["L1_attn_a_qqb"]),
                            vec(inp["L1_attn_a_qkw"]), vec(inp["L1_attn_a_qkb"])],
                           axis=1).astype(np.float32),
        "H_woT": np.ascontiguousarray(np.asarray(inp["L1_attn_a_Wo"], np.float32).T),
        "H_bo": vec(inp["L1_attn_a_bo"]).reshape(1, -1),
        "H_wrel": vec(inp["L1_attn_a_w_rel"]).reshape(1, -1),
        "H_costab": np.cos(sinus).astype(BF), "H_sintab": np.sin(sinus).astype(BF),
        "H_maskt": maskt, "H_aind": aind, "H_identb": np.eye(128, dtype=BF),
        "I_gw": np.ascontiguousarray(np.asarray(inp["L1_ffn_gw"], np.float32)),
        "I_gb": vec(inp["L1_ffn_gb"]).reshape(1, -1),
        "I_w1": np.ascontiguousarray(np.asarray(inp["L1_ffn_w1"], np.float32)),
        "I_b1": np.ascontiguousarray(np.asarray(inp["L1_ffn_b1"], np.float32)),
        "I_w3": np.ascontiguousarray(np.asarray(inp["L1_ffn_w3"], np.float32)),
        "I_b3": np.ascontiguousarray(np.asarray(inp["L1_ffn_b3"], np.float32)),
        "I_w2": np.ascontiguousarray(np.asarray(inp["L1_ffn_w2"], np.float32)),
        "I_b2": np.ascontiguousarray(np.asarray(inp["L1_ffn_b2"], np.float32)),
        "I_sw1": np.ascontiguousarray(np.asarray(inp["L1_ffn_sw1"], np.float32)),
        "I_sb1": vec(inp["L1_ffn_sb1"]).reshape(1, -1),
        "I_sw3": np.ascontiguousarray(np.asarray(inp["L1_ffn_sw3"], np.float32)),
        "I_sb3": vec(inp["L1_ffn_sb3"]).reshape(1, -1),
        "I_sw2": np.ascontiguousarray(np.asarray(inp["L1_ffn_sw2"], np.float32)),
        "I_sb2": vec(inp["L1_ffn_sb2"]).reshape(1, -1),
        "I_identf": np.eye(128, dtype=np.float32),
        "J_Wk": np.ascontiguousarray(np.asarray(inp["L1_attn_b_Wk"], np.float32)),
        "J_bk": vec(inp["L1_attn_b_bk"]).reshape(1, -1),
        "J_Wv": np.ascontiguousarray(np.asarray(inp["L1_attn_b_Wv"], np.float32)),
        "J_bv": vec(inp["L1_attn_b_bv"]).reshape(1, -1),
        "J_Wq": np.ascontiguousarray(np.asarray(inp["L1_attn_b_Wq"], np.float32)),
        "J_bq": vec(inp["L1_attn_b_bq"]).reshape(1, -1),
        "J_Wkp": np.ascontiguousarray(np.asarray(inp["L1_attn_b_Wkp"], np.float32)),
        "J_bkp": vec(inp["L1_attn_b_bkp"]).reshape(1, -1),
        "J_Wqp": np.ascontiguousarray(np.asarray(inp["L1_attn_b_Wqp"], np.float32)),
        "J_bqp": vec(inp["L1_attn_b_bqp"]).reshape(1, -1),
        "J_Wo": np.ascontiguousarray(np.asarray(inp["L1_attn_b_Wo"], np.float32)),
        "J_bo": vec(inp["L1_attn_b_bo"]).reshape(1, -1),
        "J_phs": np.ascontiguousarray(vec(inp["L1_attn_b_phase_scale"]).reshape(JD, 128).T),
        "J_lnwb": np.stack([vec(inp["L1_attn_b_ln_g"]).reshape(JD, 128).T,
                            vec(inp["L1_attn_b_ln_b"]).reshape(JD, 128).T]).astype(np.float32),
    }
    return common, mixes, wlist


def kernel_fused(**inp):
    TLOC = 512
    TL = B * TLOC
    if "FUSED" not in _PROGS:
        _PROGS["FUSED"] = build_fused(TLOC, NCORES)
    ncF = _PROGS["FUSED"]
    vec = lambda a: np.ascontiguousarray(np.asarray(a, np.float32).reshape(-1))
    x0 = np.asarray(inp["L0_attn_a_x"], np.float32)

    common, mixes, wlist = _mk_common(inp, TLOC)
    in_maps = []
    for c in range(NCORES):
        d = dict(common)
        ax = host_prep_A(x0, mixes, wlist, c, TLOC)
        d["A_xa"] = ax["xa"]
        rm = np.zeros((128, 16), np.float32)
        rm[:, 0:c] = 1.0
        rm[:, 8:16] = 1.0 - rm[:, 0:8]
        d["FU_rmask"] = rm
        cm = np.zeros((128, NCORES), np.float32)
        cm[:, c] = 1.0
        d["FU_cmask"] = cm
        in_maps.append(d)
    res = run_bass_kernel_spmd(ncF, in_maps, core_ids=list(range(NCORES)))
    out = np.empty((B, NCORES * TLOC, D), np.float32)
    for c in range(NCORES):
        out[:, c * TLOC:(c + 1) * TLOC, :] = _from_fm(np.asarray(res.results[c]["xout"]), TLOC)
    return out


def build_merged_2(TLOC, NC_=NCORES):
    TL = B * TLOC
    nc = bass.Bass("TRN2", target_bir_lowering=False, debug=False)
    ctx = Ctx(nc, NC_)
    xbufB = ctx.dint("xbufB", [JD, 128, TL])
    x2out = ctx.dout("x2out", [JD, 128, TL])
    g_out = ctx.dout("g_out", [JD, 128, TL], BF16)
    vb_out = ctx.dout("vb_out", [JD, 128, TL], BF16)
    xn2_out = ctx.dout("xn2_out", [JD, 128, TL])
    build_prog_B(TLOC, ctx=ctx, x_out=xbufB)
    build_prog_C(TLOC, ctx=ctx, x_in=xbufB, x_out=x2out)
    build_prog_D(TLOC, ctx=ctx, x_in=x2out,
                 xaps={"g_o": g_out, "vb_o": vb_out, "xn2_o": xn2_out})
    return nc


def build_merged_3(TLOC, NC_=NCORES):
    TL = B * TLOC
    TG_ = NC_ * TLOC
    nc = bass.Bass("TRN2", target_bir_lowering=False, debug=False)
    ctx = Ctx(nc, NC_)
    v, s, t, g, sy = nc.vector, nc.scalar, nc.tensor, nc.gpsimd, nc.sync
    SE = ctx.sems
    xbufA = ctx.dint("xbufA", [JD, 128, TL])
    xbufB = ctx.dint("xbufB", [JD, 128, TL])
    hs_s = ctx.dint("hs_s", [JD, 128, B * TG_], BF16)
    x2_i = ctx.din("M_x2", [JD, 128, TL])
    xn2_i = ctx.din("M_xn2", [JD, 128, TL])
    xout_final = ctx.dout("xout", [JD, 128, TL])
    build_prog_E(TG_, NC_, ctx=ctx, xaps={"hs_o": hs_s})
    ctx.reset()
    hs_loc = ctx.dint("hs_loc", [JD, 128, TL], BF16)
    cmask = ctx.din("FU_cmask", [128, NC_])
    cm_sb = ctx.al([128, NC_], F32)
    acc_f = ctx.al([128, TL], F32)
    tmp_b = ctx.al([128, TL], BF16)
    Ld = SE["Ld"]; St = SE["St"]; V = SE["V"]
    sy.dma_start(out=cm_sb[:, :], in_=cmask).then_inc(Ld.h, 16)
    Ld.val += 16
    for j in range(JD):
        V.wait(sy)
        St.wait(sy)
        v.memset(acc_f[:, :], 0.0)
        v.drain()
        for r in range(NC_):
            Ld.wait(sy)
            sy.dma_start(out=tmp_b[:, :],
                         in_=hs_s[j][:, r * TL:(r + 1) * TL]).then_inc(Ld.h, 16)
            Ld.val += 16
            Ld.wait(v)
            v.scalar_tensor_tensor(acc_f[:, :], tmp_b[:, :], cm_sb[:, r:r + 1],
                                   acc_f[:, :], ALU.mult, ALU.add)
            ei = V.inc(v.drain())
            V.wait(sy, ei)
        v.tensor_copy(tmp_b[:, :], acc_f[:, :])
        ei = V.inc(v.drain())
        V.wait(g, ei)
        g.wait_ge(St.h, St.val)
        g.dma_start(out=hs_loc[j], in_=tmp_b[:, :]).then_inc(St.h, 16)
        St.val += 16
        St.wait(v)
    ctx.barrier()
    build_prog_F(TLOC, ctx=ctx, x_in=x2_i, x_out=xbufB,
                 xaps={"xn2_i": xn2_i, "hs_i": hs_loc})
    build_prog_C(TLOC, ctx=ctx, x_in=xbufB, x_out=xbufA)
    build_prog_H(TLOC, ctx=ctx, x_in=xbufA, x_out=xbufB)
    build_prog_I(TLOC, ctx=ctx, x_in=xbufB, x_out=xbufA)
    build_prog_J(TLOC, ctx=ctx, x_in=xbufA, x_out=xbufB)
    build_prog_I(TLOC, ctx=ctx, x_in=xbufB, x_out=xout_final)
    return nc


def kernel(**inp):
    TLOC = 512
    TL = B * TLOC
    if "M2" not in _PROGS:
        _PROGS["M2"] = build_merged_2(TLOC, NCORES)
    if "M3" not in _PROGS:
        _PROGS["M3"] = build_merged_3(TLOC, NCORES)
    P = _get_progs(TLOC, only=["A"])
    common, mixes, wlist = _mk_common(inp, TLOC)
    x0 = np.asarray(inp["L0_attn_a_x"], np.float32)
    run = lambda nc_, ins: run_bass_kernel_spmd(nc_, ins, core_ids=list(range(NCORES))).results

    insA = [host_prep_A(x0, mixes, wlist, c, TLOC) for c in range(NCORES)]
    outA = run(P["A"], insA)

    vec = lambda a: np.ascontiguousarray(np.asarray(a, np.float32).reshape(-1))
    k2_stat = {k: v for k, v in common.items()
               if k.split("_")[0] in ("B", "C", "D")}
    insM2 = []
    for c in range(NCORES):
        hb = host_prep_B(outA, vec(inp["L0_attn_a_gn_w"]), vec(inp["L0_attn_a_gn_b"]),
                         np.asarray(inp["L0_attn_a_W_out"], np.float32),
                         vec(inp["L0_attn_a_b_out"]), c, TLOC)
        d = dict(k2_stat)
        for k in ("xfm_i", "rg_i", "sp_i", "pref"):
            d["B_" + k] = hb[k]
        insM2.append(d)
    outM2 = run(_PROGS["M2"], insM2)
    g_all = np.stack([np.asarray(o["g_out"]) for o in outM2])
    vb_all = np.stack([np.asarray(o["vb_out"]) for o in outM2])

    k3_stat = {k: v for k, v in common.items()
               if k.split("_")[0] in ("E", "F", "C", "H", "I", "J")}
    k3_stat["E_g_i"] = g_all
    k3_stat["E_vb_i"] = vb_all
    insM3 = []
    for c in range(NCORES):
        d = dict(k3_stat)
        d["M_x2"] = np.asarray(outM2[c]["x2out"])
        d["M_xn2"] = np.asarray(outM2[c]["xn2_out"])
        cm = np.zeros((128, NCORES), np.float32)
        cm[:, c] = 1.0
        d["FU_cmask"] = cm
        insM3.append(d)
    outM3 = run(_PROGS["M3"], insM3)
    out = np.empty((B, NCORES * TLOC, D), np.float32)
    for c in range(NCORES):
        out[:, c * TLOC:(c + 1) * TLOC, :] = _from_fm(np.asarray(outM3[c]["xout"]), TLOC)
    return out


# version 09
# speedup vs baseline: 1.0624x; 1.0624x over previous
import math
import numpy as np
import ml_dtypes
import concourse.bass as bass
import concourse.mybir as mybir
from concourse.bass_utils import run_bass_kernel_spmd

F32 = mybir.dt.float32
BF16 = mybir.dt.bfloat16
I32 = mybir.dt.int32
AF = mybir.ActivationFunctionType
ds = bass.ds
ALU = mybir.AluOpType
AX = mybir.AxisListType
ts = bass.ts
BF = ml_dtypes.bfloat16

NCORES = 8
D = 1024
B = 4
JD = D // 128


_CM_KEEP = []


class Sem:
    def __init__(self, nc, name):
        cm = nc.semaphore(name)
        _CM_KEEP.append(cm)
        self.h = cm.__enter__()
        self.val = 0

    def inc(self, inst, n=1):
        inst.then_inc(self.h, n)
        self.val += n
        return self.val

    def wait(self, eng, val=None):
        eng.wait_ge(self.h, self.val if val is None else val)


def sb(nc, name, shape, dtype):
    cm = nc.sbuf_tensor(name, shape, dtype)
    _CM_KEEP.append(cm)
    return cm.__enter__()


def ps(nc, name, shape, dtype=F32):
    cm = nc.psum_tensor(name, shape, dtype)
    _CM_KEEP.append(cm)
    return cm.__enter__()


def din(nc, name, shape, dtype=F32):
    return nc.dram_tensor(name, shape, dtype, kind="ExternalInput").ap()


def dout(nc, name, shape, dtype=F32):
    return nc.dram_tensor(name, shape, dtype, kind="ExternalOutput").ap()


def dint(nc, name, shape, dtype=F32):
    return nc.dram_tensor(name, shape, dtype, kind="Internal").ap()



def build_prog_A(TLOC, phases=4, ctx=None, x_in=None, x_out=None, xaps=None):
    _TAG = "A"
    if ctx is None:
        nc = bass.Bass("TRN2", target_bir_lowering=False, debug=False)
        DIN = lambda nm, sh, dt=F32: din(nc, nm, sh, dt)
        DOUT = lambda nm, sh, dt=F32: dout(nc, nm, sh, dt)
        DINT = lambda nm, sh, dt=F32: dint(nc, nm, sh, dt)
        SB = lambda nm, sh, dt: sb(nc, _TAG + "_" + nm, sh, dt)
        PS = lambda nm, sh, dt=F32: ps(nc, _TAG + "_" + nm, sh, dt)
        _sems = {n: Sem(nc, n) for n in ['Ld', 'Lw', 'St', 'St1', 'Sq', 'Ld2', 'Lp', 'T', 'V', 'S', 's_mm', 's_pre', 's_tanh', 's_bl', 's_fl', 'V2']}
    else:
        nc = ctx.nc
        ctx.reset()
        def DIN(nm, sh, dt=F32):
            if xaps and nm in xaps:
                return xaps[nm]
            if nm == "x_i" and x_in is not None:
                return x_in
            return ctx.din(_TAG + "_" + nm, sh, dt)
        def DOUT(nm, sh, dt=F32):
            if xaps and nm in xaps:
                return xaps[nm]
            if nm == "xout" and x_out is not None:
                return x_out
            return ctx.dout(_TAG + "_" + nm, sh, dt)
        def DINT(nm, sh, dt=F32):
            if xaps and nm in xaps:
                return xaps[nm]
            return ctx.dint(_TAG + "_" + nm, sh, dt)
        SB = lambda nm, sh, dt: ctx.al(sh, dt)
        _PSM = {'ptr': 'ctx.pT[:, 0:128]', 'ptrf': 'ctx.pA[:, 0:128]', 'pp': 'ctx.pB'}
        PS = lambda nm, sh, dt=F32: ctx.psum_by_spec(_PSM[nm])
        _sems = ctx.sems
    Ld = _sems["Ld"]; Lw = _sems["Lw"]; St = _sems["St"]; St1 = _sems["St1"]; Sq = _sems["Sq"]; Ld2 = _sems["Ld2"]; Lp = _sems["Lp"]; T = _sems["T"]; V = _sems["V"]; S = _sems["S"]; s_mm = _sems["s_mm"]; s_pre = _sems["s_pre"]; s_tanh = _sems["s_tanh"]; s_bl = _sems["s_bl"]; s_fl = _sems["s_fl"]; V2 = _sems["V2"]
    TL = B * TLOC
    TLB = TL + 4
    NT = (TLB + 127) // 128
    NTC = TL // 512 if TL >= 512 else 1
    TCW = min(512, TL)

    xa = DIN("xa", [TLB, D])
    mixes = DIN("mixes", [6, 128, JD])
    wts = DIN("wts", [6, D, D])
    identb = DIN("identb", [128, 128], BF16)
    identf = DIN("identf", [128, 128], F32)
    xfm_o = DOUT("xfm", [JD, 128, TL])
    rg_o = DOUT("rg", [2, JD, 128, TL], BF16)
    sp_o = DOUT("sp", [2, JD, 128, TL])
    kva_s = DINT("kva_s", [3, JD, 128, TL], BF16)
    dec_s = DINT("dec_s", [JD, 128, TL])

    xt = SB("xt", [128, D], F32)
    xnt = SB("xnt", [128, D], F32)
    xnb = SB("xnb", [128, D], BF16)
    lnm1 = SB("lnm1", [128, 1], F32)
    lnm2 = SB("lnm2", [128, 1], F32)
    lnsq = SB("lnsq", [128, D], F32)
    idb = SB("idb", [128, 128], BF16)
    idf = SB("idf", [128, 128], F32)
    mix_sb = SB("mix_sb", [128, 6 * JD], F32)
    xne = SB("xne", [128, JD * TLB], BF16)
    dbf = SB("dbf", [128, JD * TL], BF16)
    zbf = SB("zbf", [128, JD * TL], BF16)
    wtf = SB("wtf", [128, D], F32)
    wtb = SB("wtb", [128, 8 * D], BF16)
    acts = SB("acts", [128, TL], F32)
    actb = SB("actb", [128, TL], BF16)
    t_a = SB("t_a", [128, TCW], F32)
    t_r = SB("t_r", [128, TCW], F32)
    xev = SB("xev", [128, 128], F32)
    kj = SB("kj", [128, TL], BF16)
    vj = SB("vj", [128, TL], BF16)
    aj = SB("aj", [128, TL], BF16)
    dj = SB("dj", [128, TL], F32)
    uj = SB("uj", [128, TL], F32)
    slj = SB("slj", [128, TL], F32)
    pcj = SB("pcj", [128, TL], F32)
    ptr = PS("ptr", [128, 128], BF16)
    ptrf = PS("ptrf", [128, 128], F32)
    pp = PS("pp", [128, TCW], F32)
    if True:
        v, s, t, g, sy = nc.vector, nc.scalar, nc.tensor, nc.gpsimd, nc.sync

        sy.dma_start(out=idb[:], in_=identb).then_inc(Ld.h, 16)
        sy.dma_start(out=idf[:], in_=identf).then_inc(Ld.h, 16)
        for m in range(6):
            sy.dma_start(out=mix_sb[:, m * JD:(m + 1) * JD], in_=mixes[m]).then_inc(Ld.h, 16)
        Ld.val += 16 * 8
        onesf = acts

        for i in range(NT):
            r0 = 128 * i
            rows = min(128, TLB - r0)
            if rows < 128:
                ei = V.inc(v.memset(xt[:], 0.0))
                V.wait(sy, ei)
            T.wait(sy)
            V.wait(sy)
            sy.dma_start(out=xt[0:rows, :],
                         in_=xa[r0:r0 + rows, :]).then_inc(Ld.h, 16)
            Ld.val += 16
            Ld.wait(v)
            v.tensor_reduce(lnm1[:], xt[:], AX.X, ALU.add)
            v.drain()
            v.tensor_scalar_mul(lnm1[:], lnm1[:], 1.0 / D)
            v.drain()
            v.tensor_scalar(xnt[:], xt[:], lnm1[:], None, ALU.subtract)
            v.drain()
            v.tensor_mul(lnsq[:], xnt[:], xnt[:])
            v.drain()
            v.tensor_reduce(lnm2[:], lnsq[:], AX.X, ALU.add)
            v.drain()
            ei = V.inc(v.tensor_scalar(lnm2[:], lnm2[:], 1.0 / D, 1e-5, ALU.mult, ALU.add))
            V.wait(s, ei)
            ei = S.inc(s.activation(lnm2[:], lnm2[:], AF.Sqrt))
            S.wait(v, ei)
            v.reciprocal(lnm2[:], lnm2[:])
            v.drain()
            v.tensor_scalar(xnt[:], xnt[:], lnm2[:], None, ALU.mult)
            v.drain()
            ei = V.inc(v.tensor_copy(xnb[:], xnt[:]))
            V.wait(t, ei)
            for j in range(JD):
                ei = T.inc(t.transpose(ptr[:], xnb[:, 128 * j:128 * (j + 1)], idb[:]))
                T.wait(v, ei)
                ei = V.inc(v.tensor_copy(xne[:, j * TLB + r0: j * TLB + r0 + rows],
                                         ptr[0:128, 0:rows]))
                V.wait(t, ei)
                c0 = max(r0, 4)
                n_real = r0 + rows - c0
                if n_real <= 0:
                    continue
                ei = T.inc(t.transpose(ptrf[:], xt[:, 128 * j:128 * (j + 1)], idf[:]))
                T.wait(v, ei)
                ei = V.inc(v.tensor_copy(xev[:, 0:rows], ptrf[:, 0:rows]))
                V.wait(g, ei)
                g.dma_start(out=xfm_o[j, :, c0 - 4: c0 - 4 + n_real],
                            in_=xev[:, c0 - r0: c0 - r0 + n_real]).then_inc(St.h, 16)
                St.val += 16
                V.wait(t, ei)
                St.wait(v)

        if phases < 2:
            St.wait(g)
            if ctx is not None:
                ctx.barrier()
            return nc
        St.wait(v)
        v.drain()
        for j in range(JD):
            v.tensor_tensor(dbf[:, j * TL:(j + 1) * TL],
                            xne[:, j * TLB: j * TLB + TL],
                            xne[:, j * TLB + 4: j * TLB + 4 + TL], ALU.subtract)
        ei = V.inc(v.drain())

        import os as _os
        WNAMES = ["r", "w", "k", "v", "a", "g"][:int(_os.environ.get("NMIX", "6"))]
        for m, wn in enumerate(WNAMES):
            T.wait(sy)
            for k in range(JD):
                if k > 0:
                    V.wait(sy)
                sy.dma_start(out=wtf[:],
                             in_=wts[m, 128 * k:128 * (k + 1), :]).then_inc(Ld.h, 16)
                Ld.val += 16
                Ld.wait(v)
                ei = V.inc(v.tensor_copy(wtb[:, k * D:(k + 1) * D], wtf[:]))
            T.wait(v)
            for j in range(JD):
                v.scalar_tensor_tensor(zbf[:, j * TL:(j + 1) * TL],
                                       dbf[:, j * TL:(j + 1) * TL],
                                       mix_sb[:, m * JD + j: m * JD + j + 1],
                                       xne[:, j * TLB + 4: j * TLB + 4 + TL],
                                       ALU.mult, ALU.add)
            ei = V.inc(v.drain())
            V.wait(t, ei)
            for m8 in range(JD):
                for tc in range(NTC):
                    c0 = tc * TCW
                    for k in range(JD):
                        mm = t.matmul(pp[:, 0:TCW],
                                      wtb[:, k * D + 128 * m8: k * D + 128 * (m8 + 1)],
                                      zbf[:, k * TL + c0: k * TL + c0 + TCW],
                                      start=(k == 0), stop=(k == JD - 1))
                    ei = T.inc(mm)
                    if wn == "r":
                        T.wait(s, ei)
                        ei = S.inc(s.activation(actb[:, c0:c0 + TCW], pp[:, 0:TCW], AF.Sigmoid))
                        S.wait(t, ei)
                        S.wait(g, ei)
                    elif wn == "k":
                        T.wait(s, ei)
                        ei = S.inc(s.activation(actb[:, c0:c0 + TCW], pp[:, 0:TCW], AF.Tanh))
                        S.wait(t, ei)
                        S.wait(g, ei)
                    elif wn in ("a", "g"):
                        T.wait(s, ei)
                        ei = S.inc(s.activation(actb[:, c0:c0 + TCW], pp[:, 0:TCW], AF.Sigmoid))
                        S.wait(t, ei)
                        S.wait(g, ei)
                    elif wn == "v":
                        T.wait(v, ei)
                        ei = V.inc(v.tensor_copy(actb[:, c0:c0 + TCW], pp[:, 0:TCW]))
                        V.wait(t, ei)
                        V.wait(g, ei)
                    elif wn == "w":
                        T.wait(s, ei)
                        ei_abs = S.inc(s.activation(t_a[:], pp[:, 0:TCW], AF.Abs))
                        s.drain()
                        s.activation(t_a[:], t_a[:], AF.Exp, scale=-1.0)
                        s.drain()
                        ei_ln = S.inc(s.activation(t_a[:], t_a[:], AF.Ln, bias=1.0))
                        S.wait(v, ei_abs)
                        v.tensor_relu(t_r[:], pp[:, 0:TCW])
                        v.drain()
                        S.wait(v, ei_ln)
                        ei_add = V.inc(v.tensor_add(t_a[:], t_a[:], t_r[:]))
                        V.wait(s, ei_add)
                        V.wait(t, ei_add)
                        ei = S.inc(s.activation(acts[:, c0:c0 + TCW], t_a[:], AF.Exp, scale=-1.0))
                        S.wait(t, ei)
                        S.wait(g, ei)
                if wn == "r":
                    g.dma_start(out=rg_o[0, m8], in_=actb[:, 0:TL]).then_inc(St.h, 16)
                elif wn == "g":
                    g.dma_start(out=rg_o[1, m8], in_=actb[:, 0:TL]).then_inc(St.h, 16)
                elif wn == "k":
                    g.dma_start(out=kva_s[0, m8], in_=actb[:, 0:TL]).then_inc(St.h, 16)
                elif wn == "v":
                    g.dma_start(out=kva_s[1, m8], in_=actb[:, 0:TL]).then_inc(St.h, 16)
                elif wn == "a":
                    g.dma_start(out=kva_s[2, m8], in_=actb[:, 0:TL]).then_inc(St.h, 16)
                elif wn == "w":
                    g.dma_start(out=dec_s[m8], in_=acts[:, 0:TL]).then_inc(St.h, 16)
                St.val += 16
                St.wait(s)
                St.wait(v)

        if phases < 4:
            St.wait(g)
            if ctx is not None:
                ctx.barrier()
            return nc
        St.wait(v)
        v.memset(onesf[:], 1.0)
        v.drain()
        for j in range(JD):
            St.wait(sy)
            V.wait(sy)
            sy.dma_start(out=kj[:], in_=kva_s[0, j]).then_inc(Ld.h, 16)
            sy.dma_start(out=vj[:], in_=kva_s[1, j]).then_inc(Ld.h, 16)
            sy.dma_start(out=aj[:], in_=kva_s[2, j]).then_inc(Ld.h, 16)
            sy.dma_start(out=dj[:], in_=dec_s[j]).then_inc(Ld.h, 16)
            Ld.val += 64
            Ld.wait(v)
            v.tensor_mul(uj[:], aj[:], kj[:])
            v.drain()
            v.tensor_mul(uj[:], uj[:], vj[:])
            v.drain()
            for b in range(B):
                v.tensor_tensor_scan(slj[:, b:TL:B], dj[:, b:TL:B], uj[:, b:TL:B],
                                     0.0, ALU.mult, ALU.add)
                v.tensor_tensor_scan(pcj[:, b:TL:B], dj[:, b:TL:B], onesf[:, b:TL:B],
                                     1.0, ALU.mult, ALU.mult)
            ei = V.inc(v.drain())
            V.wait(g, ei)
            g.dma_start(out=sp_o[0, j], in_=slj[:]).then_inc(St.h, 16)
            g.dma_start(out=sp_o[1, j], in_=pcj[:]).then_inc(St.h, 16)
            St.val += 32
            St.wait(v)
        St.wait(g)
    if ctx is not None:
        ctx.barrier()
    return nc



def host_prep_A(x_full, mix_list, w_list, core, TLOC):
    TG_ = x_full.shape[1]
    t0 = TLOC * core
    xs = x_full[:, t0:t0 + TLOC, :]
    xs = np.ascontiguousarray(xs.transpose(1, 0, 2).reshape(B * TLOC, D))
    if t0 == 0:
        bnd = np.zeros((B, D), np.float32)
    else:
        bnd = np.ascontiguousarray(x_full[:, t0 - 1, :])
    xa = np.concatenate([bnd, xs], axis=0)
    mixes = np.stack([m.reshape(JD, 128).T for m in mix_list])
    wts = np.stack([np.ascontiguousarray(w.T) for w in w_list])
    return {
        "xa": xa.astype(np.float32),
        "mixes": mixes.astype(np.float32),
        "wts": wts.astype(np.float32),
        "identb": np.eye(128, dtype=BF),
        "identf": np.eye(128, dtype=np.float32),
    }


def kernel(**inputs):
    raise NotImplementedError("assembling")



def build_prog_B(TLOC, ctx=None, x_in=None, x_out=None, xaps=None):
    _TAG = "B"
    if ctx is None:
        nc = bass.Bass("TRN2", target_bir_lowering=False, debug=False)
        DIN = lambda nm, sh, dt=F32: din(nc, nm, sh, dt)
        DOUT = lambda nm, sh, dt=F32: dout(nc, nm, sh, dt)
        DINT = lambda nm, sh, dt=F32: dint(nc, nm, sh, dt)
        SB = lambda nm, sh, dt: sb(nc, _TAG + "_" + nm, sh, dt)
        PS = lambda nm, sh, dt=F32: ps(nc, _TAG + "_" + nm, sh, dt)
        _sems = {n: Sem(nc, n) for n in ['Ld', 'Lw', 'St', 'St1', 'Sq', 'Ld2', 'Lp', 'T', 'V', 'S', 's_mm', 's_pre', 's_tanh', 's_bl', 's_fl', 'V2']}
    else:
        nc = ctx.nc
        ctx.reset()
        def DIN(nm, sh, dt=F32):
            if xaps and nm in xaps:
                return xaps[nm]
            if nm == "x_i" and x_in is not None:
                return x_in
            return ctx.din(_TAG + "_" + nm, sh, dt)
        def DOUT(nm, sh, dt=F32):
            if xaps and nm in xaps:
                return xaps[nm]
            if nm == "xout" and x_out is not None:
                return x_out
            return ctx.dout(_TAG + "_" + nm, sh, dt)
        def DINT(nm, sh, dt=F32):
            if xaps and nm in xaps:
                return xaps[nm]
            return ctx.dint(_TAG + "_" + nm, sh, dt)
        SB = lambda nm, sh, dt: ctx.al(sh, dt)
        _PSM = {'pg': 'ctx.pA', 'pg2': 'ctx.pB', 'pbc': 'ctx.pC', 'pbc2': 'ctx.pD', 'pp': 'ctx.pA'}
        PS = lambda nm, sh, dt=F32: ctx.psum_by_spec(_PSM[nm])
        _sems = ctx.sems
    Ld = _sems["Ld"]; Lw = _sems["Lw"]; St = _sems["St"]; St1 = _sems["St1"]; Sq = _sems["Sq"]; Ld2 = _sems["Ld2"]; Lp = _sems["Lp"]; T = _sems["T"]; V = _sems["V"]; S = _sems["S"]; s_mm = _sems["s_mm"]; s_pre = _sems["s_pre"]; s_tanh = _sems["s_tanh"]; s_bl = _sems["s_bl"]; s_fl = _sems["s_fl"]; V2 = _sems["V2"]
    TL = B * TLOC
    NTC = max(1, TL // 512)
    TCW = min(512, TL)
    GN_EPS = 0.00064

    xfm_i = DIN("xfm_i", [JD, 128, TL])
    rg_i = DIN("rg_i", [2, JD, 128, TL], BF16)
    sp_i = DIN("sp_i", [2, JD, 128, TL])
    pref = DIN("pref", [7, 2, 128, JD * B])
    gnwb = DIN("gnwb", [2, 128, JD])
    woutT = DIN("woutT", [D, D])
    bout = DIN("bout", [1, D])
    gsel = DIN("gsel", [128, 2], BF16)
    sel16 = DIN("sel16", [JD, 16, 128], BF16)
    onesr = DIN("onesr", [1, TL])
    xout_o = DOUT("xout", [JD, 128, TL])

    prf = SB("prf", [128, 7 * 2 * JD * B], F32)
    sin_sb = SB("sin_sb", [128, JD * B], F32)
    tmp_sb = SB("tmp_sb", [128, JD * B], F32)
    gnw_sb = SB("gnw_sb", [128, 2 * JD], F32)
    gsel_sb = SB("gsel_sb", [128, 2], BF16)
    sel16_sb = SB("sel16_sb", [16, JD * 128], BF16)
    ones_bf = SB("ones_bf", [1, TL], BF16)
    onesf_sb = SB("onesf_sb", [1, TL], F32)
    bout_bf = SB("bout_bf", [1, D], BF16)
    bout_f = SB("bout_f", [1, D], F32)
    slj = SB("slj", [128, TL], F32)
    pcj = SB("pcj", [128, TL], F32)
    rj = SB("rj", [128, TL], BF16)
    gj = SB("gj", [128, TL], BF16)
    mxb = SB("mxb", [128, JD * TL], BF16)
    sqb = SB("sqb", [128, TCW], BF16)
    gst = SB("gst", [16, 2 * TL], F32)
    pg_sb = SB("pg_sb", [2, 2 * 512], F32)
    g16 = SB("g16", [16, 2 * TL], BF16)
    wtf = SB("wtf", [128, D], F32)
    wtb = SB("wtb", [128, 8 * D], BF16)
    xfm = SB("xfm", [128, TL], F32)
    t_x = SB("t_x", [128, TCW], F32)
    t_y = SB("t_y", [128, TCW], F32)
    pg = PS("pg", [2, TCW], F32)
    pg2 = PS("pg2", [2, TCW], F32)
    pbc = PS("pbc", [128, TCW], F32)
    pbc2 = PS("pbc2", [128, TCW], F32)
    pp = PS("pp", [128, TCW], F32)

    v, s, t, g, sy = nc.vector, nc.scalar, nc.tensor, nc.gpsimd, nc.sync

    for k in range(7):
        for q in range(2):
            sy.dma_start(out=prf[:, (k * 2 + q) * JD * B:(k * 2 + q + 1) * JD * B],
                         in_=pref[k, q]).then_inc(Ld.h, 16)
    sy.dma_start(out=gnw_sb[:, 0:JD], in_=gnwb[0]).then_inc(Ld.h, 16)
    sy.dma_start(out=gnw_sb[:, JD:2 * JD], in_=gnwb[1]).then_inc(Ld.h, 16)
    sy.dma_start(out=gsel_sb[:], in_=gsel).then_inc(Ld.h, 16)
    for j in range(JD):
        sy.dma_start(out=sel16_sb[0:16, j * 128:(j + 1) * 128],
                     in_=sel16[j]).then_inc(Ld.h, 16)
    sy.dma_start(out=onesf_sb[:], in_=onesr).then_inc(Ld.h, 16)
    sy.dma_start(out=bout_f[:], in_=bout).then_inc(Ld.h, 16)
    Ld.val += 16 * (14 + 2 + 1 + JD + 1 + 1)
    Ld.wait(v)
    v.tensor_copy(ones_bf[:], onesf_sb[:])
    v.drain()
    v.tensor_copy(bout_bf[:], bout_f[:])
    v.drain()
    v.memset(sin_sb[:], 0.0)
    v.drain()
    for k in range(7):
        v.tensor_mul(tmp_sb[:], sin_sb[:], prf[:, (k * 2) * JD * B:(k * 2 + 1) * JD * B])
        v.drain()
        v.tensor_add(sin_sb[:], tmp_sb[:], prf[:, (k * 2 + 1) * JD * B:(k * 2 + 2) * JD * B])
        v.drain()

    for j in range(JD):
        St.wait(sy); V.wait(sy); S.wait(sy)
        Ld2.wait(sy)
        sy.dma_start(out=slj[:], in_=sp_i[0, j]).then_inc(Ld2.h, 16)
        sy.dma_start(out=pcj[:], in_=sp_i[1, j]).then_inc(Ld2.h, 16)
        sy.dma_start(out=rj[:], in_=rg_i[0, j]).then_inc(Ld2.h, 16)
        Ld2.val += 48
        Ld2.wait(v)
        for b in range(B):
            v.scalar_tensor_tensor(slj[:, b:TL:B], pcj[:, b:TL:B],
                                   sin_sb[:, B * j + b: B * j + b + 1],
                                   slj[:, b:TL:B], ALU.mult, ALU.add)
        v.drain()
        ei = V.inc(v.tensor_mul(mxb[:, j * TL:(j + 1) * TL], rj[:], slj[:]))
        v.drain()
        for tc in range(NTC):
            c0 = tc * TCW
            V.wait(s, ei)
            eis = S.inc(s.activation(sqb[:], mxb[:, j * TL + c0: j * TL + c0 + TCW],
                                     AF.Square))
            V.wait(t, ei)
            S.wait(t, eis)
            mm = t.matmul(pg[0:2, 0:TCW], gsel_sb[:], mxb[:, j * TL + c0: j * TL + c0 + TCW],
                          start=True, stop=True)
            mm2 = t.matmul(pg2[0:2, 0:TCW], gsel_sb[:], sqb[:], start=True, stop=True)
            eit = T.inc(mm2)
            T.wait(v, eit)
            v.tensor_copy(pg_sb[0:2, 0:TCW], pg[0:2, 0:TCW])
            v.drain()
            ei2 = V.inc(v.tensor_copy(pg_sb[0:2, TCW:2 * TCW], pg2[0:2, 0:TCW]))
            V.wait(t, ei2)
            V.wait(g, ei2)
            g.dma_start(out=gst[2 * j:2 * j + 2, c0:c0 + TCW], in_=pg_sb[0:2, 0:TCW]).then_inc(Sq.h, 16)
            g.dma_start(out=gst[2 * j:2 * j + 2, TL + c0:TL + c0 + TCW], in_=pg_sb[0:2, TCW:2 * TCW]).then_inc(Sq.h, 16)
            Sq.val += 32
            Sq.wait(v)
            Sq.wait(s)

    Sq.wait(v)
    NG = 64.0
    v.tensor_scalar_mul(gst[0:16, 0:TL], gst[0:16, 0:TL], 1.0 / NG)
    v.drain()
    v.tensor_scalar_mul(gst[0:16, TL:2 * TL], gst[0:16, TL:2 * TL], 1.0 / NG)
    v.drain()
    v.tensor_mul(g16[0:16, 0:TL], gst[0:16, 0:TL], gst[0:16, 0:TL])
    v.drain()
    v.tensor_sub(gst[0:16, TL:2 * TL], gst[0:16, TL:2 * TL], g16[0:16, 0:TL])
    v.drain()
    v.tensor_scalar_add(gst[0:16, TL:2 * TL], gst[0:16, TL:2 * TL], GN_EPS)
    ei = V.inc(v.drain())
    V.wait(s, ei)
    ei = S.inc(s.activation(gst[0:16, TL:2 * TL], gst[0:16, TL:2 * TL], AF.Sqrt))
    S.wait(v, ei)
    v.reciprocal(gst[0:16, TL:2 * TL], gst[0:16, TL:2 * TL])
    v.drain()
    v.tensor_copy(g16[0:16, :], gst[0:16, :])
    ei = V.inc(v.drain())
    V.wait(t, ei)

    for j in range(JD):
        V.wait(sy)
        Ld2.wait(sy)
        sy.dma_start(out=gj[:], in_=rg_i[1, j]).then_inc(Ld2.h, 16)
        Ld2.val += 16
        for tc in range(NTC):
            c0 = tc * TCW
            sl = slice(j * TL + c0, j * TL + c0 + TCW)
            mm = t.matmul(pbc[:, 0:TCW], sel16_sb[0:16, j * 128:(j + 1) * 128],
                          g16[0:16, c0:c0 + TCW], start=True, stop=True)
            mm2 = t.matmul(pbc2[:, 0:TCW], sel16_sb[0:16, j * 128:(j + 1) * 128],
                           g16[0:16, TL + c0:TL + c0 + TCW], start=True, stop=True)
            eit = T.inc(mm2)
            T.wait(v, eit)
            Ld2.wait(v)
            v.tensor_sub(t_x[:], mxb[:, sl], pbc[:, 0:TCW])
            v.drain()
            v.tensor_mul(t_x[:], t_x[:], pbc2[:, 0:TCW])
            v.drain()
            v.tensor_scalar(t_x[:], t_x[:], gnw_sb[:, j:j + 1],
                            gnw_sb[:, JD + j:JD + j + 1], ALU.mult, ALU.add)
            v.drain()
            ei = V.inc(v.tensor_mul(mxb[:, sl], t_x[:], gj[:, c0:c0 + TCW]))
            V.wait(t, ei)

    for k in range(JD):
        if k > 0:
            V.wait(sy)
        T.wait(sy)
        Ld2.wait(sy)
        sy.dma_start(out=wtf[:], in_=woutT[128 * k:128 * (k + 1), :]).then_inc(Ld2.h, 16)
        Ld2.val += 16
        Ld2.wait(v)
        V.inc(v.tensor_copy(wtb[:, k * D:(k + 1) * D], wtf[:]))
    ei = V.inc(v.drain())
    V.wait(t, ei)
    for m8 in range(JD):
        V.wait(sy)
        Ld2.wait(sy)
        sy.dma_start(out=xfm[:], in_=xfm_i[m8]).then_inc(Ld2.h, 16)
        Ld2.val += 16
        for tc in range(NTC):
            c0 = tc * TCW
            for k in range(JD):
                mm = t.matmul(pp[:, 0:TCW],
                              wtb[:, k * D + 128 * m8: k * D + 128 * (m8 + 1)],
                              mxb[:, k * TL + c0: k * TL + c0 + TCW],
                              start=(k == 0), stop=False)
            mm = t.matmul(pp[:, 0:TCW], bout_bf[:, 128 * m8:128 * (m8 + 1)],
                          ones_bf[:, c0:c0 + TCW], start=False, stop=True)
            eit = T.inc(mm)
            T.wait(v, eit)
            Ld2.wait(v)
            ei = V.inc(v.tensor_add(t_y[:], pp[:, 0:TCW],
                                    xfm[:, c0:c0 + TCW]))
            V.wait(t, ei)
            V.wait(g, ei)
            g.wait_ge(St.h, St.val)
            g.dma_start(out=xout_o[m8, :, c0:c0 + TCW], in_=t_y[:]).then_inc(St.h, 16)
            St.val += 16
            St.wait(v)
    St.wait(g)
    if ctx is not None:
        ctx.barrier()
    return nc


def host_prep_B(outA_list, gn_w, gn_b, W_out, b_out, core, TLOC):
    TL = B * TLOC
    myA = outA_list[core]
    pref = np.zeros((7, 2, 128, JD * B), np.float32)
    pref[:, 0, :, :] = 1.0
    for k in range(core):
        sp = outA_list[k]["sp"]
        pf = sp[1][:, :, TL - B:TL]
        sf = sp[0][:, :, TL - B:TL]
        pref[k, 0] = pf.transpose(1, 0, 2).reshape(128, JD * B)
        pref[k, 1] = sf.transpose(1, 0, 2).reshape(128, JD * B)
    gnwb = np.stack([gn_w.reshape(JD, 128).T, gn_b.reshape(JD, 128).T])
    gsel = np.zeros((128, 2), BF)
    gsel[0:64, 0] = 1; gsel[64:128, 1] = 1
    sel16 = np.zeros((JD, 16, 128), BF)
    for j in range(JD):
        sel16[j, 2 * j, 0:64] = 1
        sel16[j, 2 * j + 1, 64:128] = 1
    return {
        "xfm_i": myA["xfm"], "rg_i": myA["rg"], "sp_i": myA["sp"],
        "pref": pref,
        "gnwb": gnwb.astype(np.float32),
        "woutT": np.ascontiguousarray(W_out.T).astype(np.float32),
        "bout": b_out.reshape(1, D).astype(np.float32),
        "gsel": gsel, "sel16": sel16,
        "onesr": np.ones((1, TL), np.float32),
    }



def build_prog_C(TLOC, ctx=None, x_in=None, x_out=None, xaps=None):
    _TAG = "C"
    if ctx is None:
        nc = bass.Bass("TRN2", target_bir_lowering=False, debug=False)
        DIN = lambda nm, sh, dt=F32: din(nc, nm, sh, dt)
        DOUT = lambda nm, sh, dt=F32: dout(nc, nm, sh, dt)
        DINT = lambda nm, sh, dt=F32: dint(nc, nm, sh, dt)
        SB = lambda nm, sh, dt: sb(nc, _TAG + "_" + nm, sh, dt)
        PS = lambda nm, sh, dt=F32: ps(nc, _TAG + "_" + nm, sh, dt)
        _sems = {n: Sem(nc, n) for n in ['Ld', 'Lw', 'St', 'St1', 'Sq', 'Ld2', 'Lp', 'T', 'V', 'S', 's_mm', 's_pre', 's_tanh', 's_bl', 's_fl', 'V2']}
    else:
        nc = ctx.nc
        ctx.reset()
        def DIN(nm, sh, dt=F32):
            if xaps and nm in xaps:
                return xaps[nm]
            if nm == "x_i" and x_in is not None:
                return x_in
            return ctx.din(_TAG + "_" + nm, sh, dt)
        def DOUT(nm, sh, dt=F32):
            if xaps and nm in xaps:
                return xaps[nm]
            if nm == "xout" and x_out is not None:
                return x_out
            return ctx.dout(_TAG + "_" + nm, sh, dt)
        def DINT(nm, sh, dt=F32):
            if xaps and nm in xaps:
                return xaps[nm]
            return ctx.dint(_TAG + "_" + nm, sh, dt)
        SB = lambda nm, sh, dt: ctx.al(sh, dt)
        _PSM = {'pA': 'ctx.pA', 'pB': 'ctx.pB', 'pC': 'ctx.pC', 'pD': 'ctx.pD'}
        PS = lambda nm, sh, dt=F32: ctx.psum_by_spec(_PSM[nm])
        _sems = ctx.sems
    Ld = _sems["Ld"]; Lw = _sems["Lw"]; St = _sems["St"]; St1 = _sems["St1"]; Sq = _sems["Sq"]; Ld2 = _sems["Ld2"]; Lp = _sems["Lp"]; T = _sems["T"]; V = _sems["V"]; S = _sems["S"]; s_mm = _sems["s_mm"]; s_pre = _sems["s_pre"]; s_tanh = _sems["s_tanh"]; s_bl = _sems["s_bl"]; s_fl = _sems["s_fl"]; V2 = _sems["V2"]
    TL = B * TLOC
    NTC = max(1, TL // 512)
    TCW = min(512, TL)
    E = 8
    H = 4 * D
    HC = H // 128
    HQ = HC // 4

    x_i = DIN("x_i", [JD, 128, TL])
    gate_w = DIN("gate_w", [D, E])
    gate_b = DIN("gate_b", [1, E])
    w1_i = DIN("w1", [E, D, H])
    b1_i = DIN("b1", [E, 1, H])
    w2_i = DIN("w2", [E, H, D])
    b2_i = DIN("b2", [E, 1, D])
    identf = DIN("identf", [128, 128], F32)
    xout_o = DOUT("xout", [JD, 128, TL])

    uni = SB("uni", [128, JD * TCW], F32)
    xjs = uni
    xnb = SB("xnb", [128, JD * TL], BF16)
    xb_c = SB("xb_c", [128, TCW], BF16)
    sq_c = SB("sq_c", [128, TCW], BF16)
    onescol = SB("onescol", [128, 1], BF16)
    ones_row = SB("ones_row", [1, TL], BF16)
    st_f = SB("st_f", [1, 4 * TCW], F32)
    st_b = SB("st_b", [1, 2 * TCW], BF16)
    gw_sb = SB("gw_sb", [128, JD * E], BF16)
    gb_sb = SB("gb_sb", [1, E], BF16)
    gb_f = SB("gb_f", [1, E], F32)
    idf = SB("idf", [128, 128], F32)
    lg = SB("lg", [8, TCW], F32)
    lgt = SB("lgt", [128, E], F32)
    gm1 = SB("gm1", [128, 1], F32)
    gm8 = SB("gm8", [128, E], F32)
    wmap = SB("wmap", [128, E], F32)
    wfm = SB("wfm", [8, NTC * TCW], F32)
    wfmb = SB("wfmb", [8, NTC * TCW], BF16)
    wrow = SB("wrow", [1, TCW], F32)
    wrow_b = SB("wrow_b", [1, TCW], BF16)
    wbc = SB("wbc", [128, TCW], BF16)
    w1b = SB("w1b", [128, JD * 1024], BF16)
    w2b = SB("w2b", [128, HQ * D], BF16)
    wtf = SB("wtf", [128, 1024], F32)
    b1b = SB("b1b", [1, H], BF16)
    b2b = SB("b2b", [8, D], BF16)
    b2f = SB("b2f", [8, D], F32)
    hbuf = uni.bitcast(BF16)
    htmp = SB("htmp", [128, TCW], BF16)
    htmp2 = SB("htmp2", [128, TCW], BF16)
    oacc = SB("oacc", [128, JD * TL], F32)
    t_y = SB("t_y", [128, TCW], F32)
    pA = PS("pA", [128, TCW], F32)
    pB = PS("pB", [128, TCW], F32)
    pC = PS("pC", [128, TCW], F32)
    pD = PS("pD", [128, TCW], F32)
    pb1 = pA; pb2 = pB; pbc = pC; pbc2 = pD
    pl = pA; ptt = pB; pth = pC
    ph = pB; po = pC

    v, s, t, g, sy = nc.vector, nc.scalar, nc.tensor, nc.gpsimd, nc.sync

    sy.dma_start(out=idf[:], in_=identf).then_inc(Ld.h, 16)
    Ld.val += 16
    for j in range(JD):
        sy.dma_start(out=wtf[:, 0:E], in_=gate_w[128 * j:128 * (j + 1), :]).then_inc(Ld.h, 16)
        Ld.val += 16
        Ld.wait(v)
        V.inc(v.tensor_copy(gw_sb[:, j * E:(j + 1) * E], wtf[:, 0:E]))
        V.wait(sy)
    sy.dma_start(out=gb_f[:], in_=gate_b).then_inc(Ld.h, 16)
    Ld.val += 16
    Ld.wait(v)
    v.memset(onescol[:], 1.0)
    v.memset(ones_row[:], 1.0)
    v.tensor_copy(gb_sb[:], gb_f[:])
    v.memset(oacc[:], 0.0)
    ei = V.inc(v.drain())
    V.wait(t, ei)

    for tc in range(NTC):
        c0 = tc * TCW
        V.wait(sy); S.wait(sy); T.wait(sy)
        Ld.wait(sy)
        for j in range(JD):
            sy.dma_start(out=xjs[:, j * TCW:(j + 1) * TCW],
                         in_=x_i[j][:, c0:c0 + TCW]).then_inc(Ld.h, 16)
        Ld.val += 16 * JD
        Ld.wait(v); Ld.wait(s)
        for j in range(JD):
            xsl = xjs[:, j * TCW:(j + 1) * TCW]
            ei = V.inc(v.tensor_copy(xb_c[:], xsl))
            eis = S.inc(s.activation(sq_c[:], xsl, AF.Square))
            V.wait(t, ei); S.wait(t, eis)
            mm1 = t.matmul(pb1[0:1, 0:TCW], onescol[:], xb_c[:], start=(j == 0), stop=(j == JD - 1))
            mm2 = t.matmul(pb2[0:1, 0:TCW], onescol[:], sq_c[:], start=(j == 0), stop=(j == JD - 1))
            eit = T.inc(mm2)
            T.wait(v, eit)
            T.wait(s, eit)
        v.tensor_scalar_mul(st_f[:, 0:TCW], pb1[0:1, 0:TCW], 1.0 / D)
        v.drain()
        v.tensor_scalar_mul(st_f[:, TCW:2 * TCW], pb2[0:1, 0:TCW], 1.0 / D)
        v.drain()
        v.tensor_mul(st_f[:, 3 * TCW:4 * TCW], st_f[:, 0:TCW], st_f[:, 0:TCW])
        v.drain()
        v.tensor_sub(st_f[:, TCW:2 * TCW], st_f[:, TCW:2 * TCW], st_f[:, 3 * TCW:4 * TCW])
        v.drain()
        v.tensor_scalar_add(st_f[:, TCW:2 * TCW], st_f[:, TCW:2 * TCW], 1e-5)
        ei = V.inc(v.drain())
        V.wait(s, ei)
        ei = S.inc(s.activation(st_f[:, 2 * TCW:3 * TCW], st_f[:, TCW:2 * TCW], AF.Sqrt))
        S.wait(v, ei)
        v.reciprocal(st_f[:, 2 * TCW:3 * TCW], st_f[:, 2 * TCW:3 * TCW])
        v.drain()
        v.tensor_copy(st_b[:, 0:TCW], st_f[:, 0:TCW])
        v.drain()
        ei = V.inc(v.tensor_copy(st_b[:, TCW:2 * TCW], st_f[:, 2 * TCW:3 * TCW]))
        V.wait(t, ei)
        mm1 = t.matmul(pbc[:, 0:TCW], onescol[0:1, 0:1].bitcast(BF16) if False else ones_row[:, 0:128],
                       st_b[:, 0:TCW], start=True, stop=True) if False else None
        mm1 = t.matmul(pbc[:, 0:TCW], ones_row[:, 0:128], st_b[:, 0:TCW], start=True, stop=True)
        mm2 = t.matmul(pbc2[:, 0:TCW], ones_row[:, 0:128], st_b[:, TCW:2 * TCW], start=True, stop=True)
        eit = T.inc(mm2)
        T.wait(v, eit)
        for j in range(JD):
            v.tensor_sub(t_y[:], xjs[:, j * TCW:(j + 1) * TCW], pbc[:, 0:TCW])
            v.drain()
            v.tensor_mul(xnb[:, j * TL + c0: j * TL + c0 + TCW], t_y[:], pbc2[:, 0:TCW])
            v.drain()
        ei = V.inc(v.drain())
        V.wait(t, ei)
        for j in range(JD):
            mm = t.matmul(pl[0:8, 0:TCW], gw_sb[:, j * E:(j + 1) * E],
                          xnb[:, j * TL + c0: j * TL + c0 + TCW],
                          start=(j == 0), stop=False)
        mm = t.matmul(pl[0:8, 0:TCW], gb_sb[:], ones_row[:, 0:TCW], start=False, stop=True)
        eit = T.inc(mm)
        T.wait(v, eit)
        ei = V.inc(v.tensor_copy(lg[0:8, 0:TCW], pl[0:8, 0:TCW]))
        V.wait(t, ei)
        for sbt in range(TCW // 128):
            mm = t.matmul(ptt[:, 0:E], lg[0:8, 128 * sbt:128 * (sbt + 1)], idf[0:8, 0:E],
                          start=True, stop=True)
            eit = T.inc(mm)
            T.wait(v, eit)
            v.tensor_copy(lgt[:], ptt[:, 0:E])
            v.drain()
            v.reduce_max(gm1[:], lgt[:], AX.X)
            v.drain()
            ei = V.inc(v.tensor_scalar(lgt[:], lgt[:], gm1[:], None, ALU.subtract))
            V.wait(s, ei)
            ei = S.inc(s.activation(lgt[:], lgt[:], AF.Exp))
            S.wait(v, ei)
            v.tensor_reduce(gm1[:], lgt[:], AX.X, ALU.add)
            v.drain()
            v.reciprocal(gm1[:], gm1[:])
            v.drain()
            v.tensor_scalar(lgt[:], lgt[:], gm1[:], None, ALU.mult)
            v.drain()
            v.max(out=gm8[:], in_=lgt[:])
            v.drain()
            v.tensor_scalar(wmap[:], lgt[:], gm8[:, 1:2], None, ALU.is_ge)
            v.drain()
            v.tensor_mul(wmap[:], wmap[:], lgt[:])
            v.drain()
            v.tensor_reduce(gm1[:], wmap[:], AX.X, ALU.add)
            v.drain()
            v.reciprocal(gm1[:], gm1[:])
            v.drain()
            ei = V.inc(v.tensor_scalar(wmap[:], wmap[:], gm1[:], None, ALU.mult))
            V.wait(t, ei)
            mm = t.transpose(pth[0:128, 0:8][0:128, 0:8] if False else pth[:, :],
                             wmap[:], idf[:]) if False else None
            mm = t.transpose(pth[:, :], wmap[:, 0:E].broadcast_to([128, E]) if False else wmap[:],
                             idf[:]) if False else None
            mm = t.transpose(pth[0:8, 0:128], wmap[:], idf[:])
            eit = T.inc(mm)
            T.wait(v, eit)
            ei = V.inc(v.tensor_copy(wfm[0:8, c0 + 128 * sbt: c0 + 128 * (sbt + 1)],
                                     pth[0:8, 0:128]))
            V.wait(t, ei)
        v.drain()
    St.wait(v)
    ei = V.inc(v.tensor_copy(wfmb[0:8, :], wfm[0:8, :]))
    V.wait(t, ei); V.wait(g, ei)

    for e in range(E):
        sy.dma_start(out=b2f[e:e + 1, :] if e == 0 else b2f[e:e + 1, :],
                     in_=b2_i[e]).then_inc(Lw.h, 16)
        Lw.val += 16
    Lw.wait(v)
    v.drain()
    ei = V.inc(v.tensor_copy(b2b[0:8, :], b2f[0:8, :]))
    V.wait(t, ei)

    for e in range(E):
        for half in range(4):
            h0 = half * 1024
            T.wait(sy)
            for k in range(JD):
                V.wait(sy)
                Lw.wait(sy)
                sy.dma_start(out=wtf[:], in_=w1_i[e, 128 * k:128 * (k + 1), h0:h0 + 1024]).then_inc(Lw.h, 16)
                Lw.val += 16
                Lw.wait(v)
                V.inc(v.tensor_copy(w1b[:, k * 1024:(k + 1) * 1024], wtf[:]))
            for hk in range(HQ):
                V.wait(sy)
                Lw.wait(sy)
                sy.dma_start(out=wtf[:, 0:D],
                             in_=w2_i[e, 128 * (HQ * half + hk):128 * (HQ * half + hk + 1), :]).then_inc(Lw.h, 16)
                Lw.val += 16
                Lw.wait(v)
                V.inc(v.tensor_copy(w2b[:, hk * D:(hk + 1) * D], wtf[:, 0:D]))
            if half == 0:
                for bh in range(4):
                    V.wait(sy)
                    Lw.wait(sy)
                    sy.dma_start(out=wtf[0:1, 0:1024],
                                 in_=b1_i[e][:, 1024 * bh:1024 * (bh + 1)]).then_inc(Lw.h, 16)
                    Lw.val += 16
                    Lw.wait(v)
                    V.inc(v.tensor_copy(b1b[:, 1024 * bh:1024 * (bh + 1)], wtf[0:1, 0:1024]))
            ei = V.inc(v.drain())
            V.wait(t, ei)
            for tc in range(NTC):
                c0 = tc * TCW
                V.wait(sy); Lw.wait(sy)
                sy.dma_start(out=wrow[:], in_=wfm[e:e + 1, c0:c0 + TCW]).then_inc(Lw.h, 16)
                Lw.val += 16
                Lw.wait(v)
                v.tensor_copy(wrow_b[:], wrow[:])
                ei = V.inc(v.drain())
                V.wait(t, ei)
                mm = t.matmul(pbc[:, 0:TCW], ones_row[:, 0:128], wrow_b[:], start=True, stop=True)
                eit = T.inc(mm)
                T.wait(v, eit)
                ei = V.inc(v.tensor_copy(wbc[:], pbc[:, 0:TCW]))
                V.wait(t, ei)
                for hk in range(HQ):
                    hcol = 128 * hk
                    php = ph if hk % 2 == 0 else pD
                    htp = htmp if hk % 2 == 0 else htmp2
                    if hk >= 2:
                        S.wait(t, S.val - 1)
                    for k in range(JD):
                        mm = t.matmul(php[:, 0:TCW],
                                      w1b[:, k * 1024 + hcol: k * 1024 + hcol + 128],
                                      xnb[:, k * TL + c0: k * TL + c0 + TCW],
                                      start=(k == 0), stop=False)
                    mm = t.matmul(php[:, 0:TCW], b1b[:, h0 + hcol: h0 + hcol + 128],
                                  ones_row[:, 0:TCW], start=False, stop=True)
                    eit = T.inc(mm)
                    T.wait(s, eit)
                    ei = S.inc(s.activation(htp[:], php[:, 0:TCW], AF.Gelu))
                    S.wait(v, ei)
                    v.tensor_mul(hbuf[:, hk * TCW:(hk + 1) * TCW], htp[:], wbc[:])
                    eiv = V.inc(v.drain())
                    V.wait(s, eiv)
                ei = V.inc(v.drain())
                V.wait(t, ei)
                for m8 in range(JD):
                    pop = po if m8 % 2 == 0 else ph
                    for hk in range(HQ):
                        mm = t.matmul(pop[:, 0:TCW],
                                      w2b[:, hk * D + 128 * m8: hk * D + 128 * (m8 + 1)],
                                      hbuf[:, hk * TCW:(hk + 1) * TCW],
                                      start=(hk == 0), stop=(hk == HQ - 1))
                    eit = T.inc(mm)
                    T.wait(v, eit)
                    osl = oacc[:, m8 * TL + c0: m8 * TL + c0 + TCW]
                    v.tensor_add(osl, osl, pop[:, 0:TCW])
                    ei = V.inc(v.drain())
                    if m8 >= 1:
                        V.wait(t, ei - 1)
                if True:
                    V.wait(t, V.val)

    for m8 in range(JD):
        for tc in range(NTC):
            c0 = tc * TCW
            mm = t.matmul(po[:, 0:TCW], b2b[0:8, 128 * m8:128 * (m8 + 1)],
                          wfmb[0:8, c0:c0 + TCW], start=True, stop=True)
            eit = T.inc(mm)
            T.wait(v, eit)
            osl = oacc[:, m8 * TL + c0: m8 * TL + c0 + TCW]
            v.tensor_add(osl, osl, po[:, 0:TCW])
            ei = V.inc(v.drain())
            V.wait(t, ei)

    for m8 in range(JD):
        V.wait(sy)
        Ld.wait(sy)
        sy.dma_start(out=xjs[:, 0:TL] if TL <= JD * TCW else xjs[:, 0:TL],
                     in_=x_i[m8]).then_inc(Ld.h, 16)
        Ld.val += 16
        Ld.wait(v)
        St.wait(v)
        v.tensor_add(oacc[:, m8 * TL:(m8 + 1) * TL], oacc[:, m8 * TL:(m8 + 1) * TL],
                     xjs[:, 0:TL])
        ei = V.inc(v.drain())
        V.wait(g, ei)
        g.wait_ge(St.h, St.val)
        g.dma_start(out=xout_o[m8], in_=oacc[:, m8 * TL:(m8 + 1) * TL]).then_inc(St.h, 16)
        St.val += 16
    St.wait(g)
    if ctx is not None:
        ctx.barrier()
    return nc


def host_prep_C(x_fm_list_unused, gate_w, gate_b, w1, b1, w2, b2, TLOC):
    return {
        "gate_w": np.ascontiguousarray(gate_w).astype(np.float32),
        "gate_b": gate_b.reshape(1, -1).astype(np.float32),
        "w1": np.ascontiguousarray(w1).astype(np.float32),
        "b1": np.ascontiguousarray(b1.reshape(b1.shape[0], 1, -1)).astype(np.float32),
        "w2": np.ascontiguousarray(w2).astype(np.float32),
        "b2": np.ascontiguousarray(b2.reshape(b2.shape[0], 1, -1)).astype(np.float32),
        "identf": np.eye(128, dtype=np.float32),
    }



def build_prog_D(TLOC, ctx=None, x_in=None, x_out=None, xaps=None):
    _TAG = "D"
    if ctx is None:
        nc = bass.Bass("TRN2", target_bir_lowering=False, debug=False)
        DIN = lambda nm, sh, dt=F32: din(nc, nm, sh, dt)
        DOUT = lambda nm, sh, dt=F32: dout(nc, nm, sh, dt)
        DINT = lambda nm, sh, dt=F32: dint(nc, nm, sh, dt)
        SB = lambda nm, sh, dt: sb(nc, _TAG + "_" + nm, sh, dt)
        PS = lambda nm, sh, dt=F32: ps(nc, _TAG + "_" + nm, sh, dt)
        _sems = {n: Sem(nc, n) for n in ['Ld', 'Lw', 'St', 'St1', 'Sq', 'Ld2', 'Lp', 'T', 'V', 'S', 's_mm', 's_pre', 's_tanh', 's_bl', 's_fl', 'V2']}
    else:
        nc = ctx.nc
        ctx.reset()
        def DIN(nm, sh, dt=F32):
            if xaps and nm in xaps:
                return xaps[nm]
            if nm == "x_i" and x_in is not None:
                return x_in
            return ctx.din(_TAG + "_" + nm, sh, dt)
        def DOUT(nm, sh, dt=F32):
            if xaps and nm in xaps:
                return xaps[nm]
            if nm == "xout" and x_out is not None:
                return x_out
            return ctx.dout(_TAG + "_" + nm, sh, dt)
        def DINT(nm, sh, dt=F32):
            if xaps and nm in xaps:
                return xaps[nm]
            return ctx.dint(_TAG + "_" + nm, sh, dt)
        SB = lambda nm, sh, dt: ctx.al(sh, dt)
        _PSM = {'pA': 'ctx.pA', 'pB': 'ctx.pB', 'pC': 'ctx.pC', 'pD': 'ctx.pD'}
        PS = lambda nm, sh, dt=F32: ctx.psum_by_spec(_PSM[nm])
        _sems = ctx.sems
    Ld = _sems["Ld"]; Lw = _sems["Lw"]; St = _sems["St"]; St1 = _sems["St1"]; Sq = _sems["Sq"]; Ld2 = _sems["Ld2"]; Lp = _sems["Lp"]; T = _sems["T"]; V = _sems["V"]; S = _sems["S"]; s_mm = _sems["s_mm"]; s_pre = _sems["s_pre"]; s_tanh = _sems["s_tanh"]; s_bl = _sems["s_bl"]; s_fl = _sems["s_fl"]; V2 = _sems["V2"]
    TL = B * TLOC
    NTC = max(1, TL // 512)
    TCW = min(512, TL)
    EH = 2 * D

    x_i = DIN("x_i", [JD, 128, TL])
    winT = DIN("winT", [D, EH])
    b_in = DIN("b_in", [1, EH])
    b_h = DIN("b_h", [1, D])
    lnwb = DIN("lnwb", [2, 128, JD])
    g_o = DOUT("g_o", [JD, 128, TL], BF16)
    vb_o = DOUT("vb_o", [JD, 128, TL], BF16)
    xn2_o = DOUT("xn2_o", [JD, 128, TL])

    xjs = SB("xjs", [128, JD * TCW], F32)
    yjs = SB("yjs", [128, JD * TCW], F32)
    xnb = SB("xnb", [128, JD * TL], BF16)
    xb_c = SB("xb_c", [128, TCW], BF16)
    sq_c = SB("sq_c", [128, TCW], BF16)
    onescol = SB("onescol", [128, 1], BF16)
    ones_row = SB("ones_row", [1, TL], BF16)
    st_f = SB("st_f", [1, 4 * TCW], F32)
    st_b = SB("st_b", [1, 2 * TCW], BF16)
    lnw_sb = SB("lnw_sb", [128, 2 * JD], F32)
    wtf = SB("wtf", [128, 1024], F32)
    wtb = SB("wtb", [128, JD * EH], BF16)
    brow = SB("brow", [1, EH + D], BF16)
    t_y = SB("t_y", [128, TCW], F32)
    t_z = SB("t_z", [128, TCW], F32)
    pA = PS("pA", [128, TCW], F32)
    pB = PS("pB", [128, TCW], F32)
    pC = PS("pC", [128, TCW], F32)
    pD = PS("pD", [128, TCW], F32)

    v, s, t, g, sy = nc.vector, nc.scalar, nc.tensor, nc.gpsimd, nc.sync

    sy.dma_start(out=lnw_sb[:, 0:JD], in_=lnwb[0]).then_inc(Ld.h, 16)
    Ld.val += 16
    sy.dma_start(out=lnw_sb[:, JD:2 * JD], in_=lnwb[1]).then_inc(Ld.h, 16)
    Ld.val += 16
    sy.dma_start(out=wtf[0:1, 0:1024], in_=b_in[:, 0:1024]).then_inc(Ld.h, 16)
    Ld.val += 16
    Ld.wait(v)
    V.inc(v.tensor_copy(brow[:, 0:1024], wtf[0:1, 0:1024]))
    V.wait(sy)
    sy.dma_start(out=wtf[0:1, 0:1024], in_=b_in[:, 1024:2048]).then_inc(Ld.h, 16)
    Ld.val += 16
    Ld.wait(v)
    V.inc(v.tensor_copy(brow[:, 1024:2048], wtf[0:1, 0:1024]))
    V.wait(sy)
    sy.dma_start(out=wtf[0:1, 0:1024], in_=b_h).then_inc(Ld.h, 16)
    Ld.val += 16
    Ld.wait(v)
    V.inc(v.tensor_copy(brow[:, 2048:2048 + D], wtf[0:1, 0:1024]))
    v.memset(onescol[:], 1.0)
    v.memset(ones_row[:], 1.0)
    ei = V.inc(v.drain())
    V.wait(t, ei); V.wait(sy)
    for k in range(JD):
        for hh in range(2):
            Lw.wait(sy)
            V.wait(sy)
            sy.dma_start(out=wtf[:], in_=winT[128 * k:128 * (k + 1),
                                            1024 * hh:1024 * (hh + 1)]).then_inc(Lw.h, 16)
            Lw.val += 16
            Lw.wait(v)
            V.inc(v.tensor_copy(wtb[:, k * EH + 1024 * hh: k * EH + 1024 * (hh + 1)], wtf[:]))
    ei = V.inc(v.drain())
    V.wait(t, ei)

    def ln_stats(xsrc_sb, off_fn, eps):
        for j in range(JD):
            xsl = xsrc_sb[:, off_fn(j):off_fn(j) + TCW]
            ei = V.inc(v.tensor_copy(xb_c[:], xsl))
            eis = S.inc(s.activation(sq_c[:], xsl, AF.Square))
            V.wait(t, ei); S.wait(t, eis)
            t.matmul(pA[0:1, 0:TCW], onescol[:], xb_c[:], start=(j == 0), stop=(j == JD - 1))
            mm2 = t.matmul(pB[0:1, 0:TCW], onescol[:], sq_c[:], start=(j == 0), stop=(j == JD - 1))
            eit = T.inc(mm2)
            T.wait(v, eit); T.wait(s, eit)
        v.tensor_scalar_mul(st_f[:, 0:TCW], pA[0:1, 0:TCW], 1.0 / D)
        v.drain()
        v.tensor_scalar_mul(st_f[:, TCW:2 * TCW], pB[0:1, 0:TCW], 1.0 / D)
        v.drain()
        v.tensor_mul(st_f[:, 3 * TCW:4 * TCW], st_f[:, 0:TCW], st_f[:, 0:TCW])
        v.drain()
        v.tensor_sub(st_f[:, TCW:2 * TCW], st_f[:, TCW:2 * TCW], st_f[:, 3 * TCW:4 * TCW])
        v.drain()
        v.tensor_scalar_add(st_f[:, TCW:2 * TCW], st_f[:, TCW:2 * TCW], eps)
        ei = V.inc(v.drain())
        V.wait(s, ei)
        ei = S.inc(s.activation(st_f[:, 2 * TCW:3 * TCW], st_f[:, TCW:2 * TCW], AF.Sqrt))
        S.wait(v, ei)
        v.reciprocal(st_f[:, 2 * TCW:3 * TCW], st_f[:, 2 * TCW:3 * TCW])
        v.drain()
        v.tensor_copy(st_b[:, 0:TCW], st_f[:, 0:TCW])
        v.drain()
        ei = V.inc(v.tensor_copy(st_b[:, TCW:2 * TCW], st_f[:, 2 * TCW:3 * TCW]))
        V.wait(t, ei)
        mm1 = t.matmul(pC[:, 0:TCW], ones_row[:, 0:128], st_b[:, 0:TCW], start=True, stop=True)
        mm2 = t.matmul(pD[:, 0:TCW], ones_row[:, 0:128], st_b[:, TCW:2 * TCW], start=True, stop=True)
        eit = T.inc(mm2)
        T.wait(v, eit)

    for tc in range(NTC):
        c0 = tc * TCW
        V.wait(sy); S.wait(sy); T.wait(sy); Ld.wait(sy)
        for j in range(JD):
            sy.dma_start(out=xjs[:, j * TCW:(j + 1) * TCW],
                         in_=x_i[j][:, c0:c0 + TCW]).then_inc(Ld.h, 16)
        Ld.val += 16 * JD
        Ld.wait(v); Ld.wait(s)
        ln_stats(xjs, lambda j: j * TCW, 1e-5)
        for j in range(JD):
            v.tensor_sub(t_y[:], xjs[:, j * TCW:(j + 1) * TCW], pC[:, 0:TCW])
            v.drain()
            ei = V.inc(v.tensor_mul(yjs[:, j * TCW:(j + 1) * TCW], t_y[:], pD[:, 0:TCW]))
            v.drain()
            V.wait(g, ei)
            g.wait_ge(St.h, St.val)
            g.dma_start(out=xn2_o[j][:, c0:c0 + TCW],
                        in_=yjs[:, j * TCW:(j + 1) * TCW]).then_inc(St.h, 16)
            St.val += 16
        St.wait(v)
        ei = V.inc(v.drain())
        V.wait(t, ei)
        V.wait(s, ei)
        ln_stats(yjs, lambda j: j * TCW, 1e-5)
        for j in range(JD):
            v.tensor_sub(t_y[:], yjs[:, j * TCW:(j + 1) * TCW], pC[:, 0:TCW])
            v.drain()
            v.tensor_mul(t_z[:], t_y[:], pD[:, 0:TCW])
            v.drain()
            v.tensor_scalar(t_z[:], t_z[:], lnw_sb[:, j:j + 1], lnw_sb[:, JD + j:JD + j + 1],
                            ALU.mult, ALU.add)
            v.drain()
            v.tensor_copy(xnb[:, j * TL + c0: j * TL + c0 + TCW], t_z[:])
            v.drain()
        V.wait(t, V.inc(v.drain()))

    for m16 in range(16):
        for tc in range(NTC):
            c0 = tc * TCW
            for k in range(JD):
                mm = t.matmul(pA[:, 0:TCW],
                              wtb[:, k * EH + 128 * m16: k * EH + 128 * (m16 + 1)],
                              xnb[:, k * TL + c0: k * TL + c0 + TCW],
                              start=(k == 0), stop=False)
            mm = t.matmul(pA[:, 0:TCW], brow[:, 128 * m16:128 * (m16 + 1)],
                          ones_row[:, 0:TCW], start=False, stop=(m16 < 8))
            if m16 >= 8:
                mm = t.matmul(pA[:, 0:TCW], brow[:, 2048 + 128 * (m16 - 8):2048 + 128 * (m16 - 7)],
                              ones_row[:, 0:TCW], start=False, stop=True)
            eit = T.inc(mm)
            if m16 < 8:
                T.wait(s, eit)
                ei = S.inc(s.activation(t_y[:].bitcast(BF16)[:, 0:TCW], pA[:, 0:TCW], AF.Sigmoid))
                S.wait(t, ei); S.wait(g, ei)
                g.wait_ge(St.h, St.val)
                g.dma_start(out=g_o[m16][:, c0:c0 + TCW],
                            in_=t_y[:].bitcast(BF16)[:, 0:TCW]).then_inc(St.h, 16)
                St.val += 16
                St.wait(s); St.wait(v)
            else:
                T.wait(v, eit)
                ei = V.inc(v.tensor_copy(t_y[:].bitcast(BF16)[:, 0:TCW], pA[:, 0:TCW]))
                V.wait(t, ei); V.wait(g, ei)
                g.wait_ge(St.h, St.val)
                g.dma_start(out=vb_o[m16 - 8][:, c0:c0 + TCW],
                            in_=t_y[:].bitcast(BF16)[:, 0:TCW]).then_inc(St.h, 16)
                St.val += 16
                St.wait(v); St.wait(s)
    St.wait(g)
    if ctx is not None:
        ctx.barrier()
    return nc



def build_prog_E(TG_, NCORES_, ctx=None, x_in=None, x_out=None, xaps=None):
    _TAG = "E"
    if ctx is None:
        nc = bass.Bass("TRN2", target_bir_lowering=False, debug=False)
        DIN = lambda nm, sh, dt=F32: din(nc, nm, sh, dt)
        DOUT = lambda nm, sh, dt=F32: dout(nc, nm, sh, dt)
        DINT = lambda nm, sh, dt=F32: dint(nc, nm, sh, dt)
        SB = lambda nm, sh, dt: sb(nc, _TAG + "_" + nm, sh, dt)
        PS = lambda nm, sh, dt=F32: ps(nc, _TAG + "_" + nm, sh, dt)
        _sems = {n: Sem(nc, n) for n in ['Ld', 'Lw', 'St', 'St1', 'Sq', 'Ld2', 'Lp', 'T', 'V', 'S', 's_mm', 's_pre', 's_tanh', 's_bl', 's_fl', 'V2']}
    else:
        nc = ctx.nc
        ctx.reset()
        def DIN(nm, sh, dt=F32):
            if xaps and nm in xaps:
                return xaps[nm]
            if nm == "x_i" and x_in is not None:
                return x_in
            return ctx.din(_TAG + "_" + nm, sh, dt)
        def DOUT(nm, sh, dt=F32):
            if xaps and nm in xaps:
                return xaps[nm]
            if nm == "xout" and x_out is not None:
                return x_out
            return ctx.dout(_TAG + "_" + nm, sh, dt)
        def DINT(nm, sh, dt=F32):
            if xaps and nm in xaps:
                return xaps[nm]
            return ctx.dint(_TAG + "_" + nm, sh, dt)
        SB = lambda nm, sh, dt: ctx.al(sh, dt)
        _PSM = {'pz0': 'ctx.pA[:, 0:32]', 'pz1': 'ctx.pB[:, 0:32]'}
        PS = lambda nm, sh, dt=F32: ctx.psum_by_spec(_PSM[nm])
        _sems = ctx.sems
    Ld = _sems["Ld"]; Lw = _sems["Lw"]; St = _sems["St"]; St1 = _sems["St1"]; Sq = _sems["Sq"]; Ld2 = _sems["Ld2"]; Lp = _sems["Lp"]; T = _sems["T"]; V = _sems["V"]; S = _sems["S"]; s_mm = _sems["s_mm"]; s_vb = _sems["s_pre"]; s_tanh = _sems["s_tanh"]; s_bl = _sems["s_bl"]; s_fl = _sems["s_fl"]; V2 = _sems["V2"]
    CH = 64
    NCH = TG_ // CH
    COLS = JD * B
    TLL = B * (TG_ // NCORES_)
    CB = CH * B

    g_i = DIN("g_i", [NCORES_, JD, 128, TLL], BF16)
    vb_i = DIN("vb_i", [NCORES_, JD, 128, TLL], BF16)
    whT = DIN("whT", [D, D])
    hs_o = DOUT("hs_o", [JD, 128, B * TG_], BF16)

    wtf = SB("wtf", [128, D], F32)
    wtb = SB("wtb", [128, JD * D], BF16)
    gb0 = SB("gb0", [128, JD * CB], BF16)
    gb1 = SB("gb1", [128, JD * CB], BF16)
    vbb0 = SB("vbb0", [128, JD * CB], BF16)
    vbb1 = SB("vbb1", [128, JD * CB], BF16)
    hs0 = SB("hs0", [128, JD * CB], BF16)
    hs1 = SB("hs1", [128, JD * CB], BF16)
    h = SB("h", [128, COLS], F32)
    hbf = SB("hbf", [128, COLS], BF16)
    dd = SB("dd", [128, COLS], F32)
    cc = SB("cc", [128, COLS], F32)
    pz0 = PS("pz0", [128, 512])[:, 0:COLS]

    v, s, t, g, sy = nc.vector, nc.scalar, nc.tensor, nc.gpsimd, nc.sync

    gbuf = [gb0, gb1]; vbuf = [vbb0, vbb1]; hsb = [hs0, hs1]
    gv = [b_.rearrange("q (j a) -> q j a", j=JD) for b_ in gbuf]
    vv = [b_.rearrange("q (j a) -> q j a", j=JD) for b_ in vbuf]
    hv = [b_.rearrange("q (j a) -> q j a", j=JD) for b_ in hsb]

    for k in range(JD):
        if k > 0:
            Ld.wait(sy)
            V2.wait(sy)
        sy.dma_start(out=wtf[:], in_=whT[128 * k:128 * (k + 1), :]).then_inc(Ld.h, 16)
        Ld.val += 16
        Ld.wait(v)
        V2.inc(v.tensor_copy(wtb[:, k * D:(k + 1) * D], wtf[:]))
    v.drain()
    v.memset(h[:], 0.0)
    v.drain()
    v.memset(hbf[:], 0.0)
    v.drain().then_inc(s_bl.h, 1)
    s_bl.val += 1

    def chunk_src(c):
        r = c // (NCH // NCORES_)
        tl0 = (c % (NCH // NCORES_)) * CB
        return r, tl0

    PFC = 16 * 2 * JD
    NC2 = NCH // 2

    for c in range(NCH + 1):
        r, tl0 = chunk_src(min(c, NCH - 1))
        if c >= 2:
            sy.wait_ge(s_bl.h, 1 + (c - 1) * CH)
        if c >= 1:
            Lp.wait(sy)
        for j in range(JD):
            sy.dma_start(out=gbuf[c % 2][:, j * CB:(j + 1) * CB],
                         in_=g_i[r, j][:, tl0:tl0 + CB]).then_inc(Lp.h, 16)
            sy.dma_start(out=vbuf[c % 2][:, j * CB:(j + 1) * CB],
                         in_=vb_i[r, j][:, tl0:tl0 + CB]).then_inc(Lp.h, 16)
        Lp.val += PFC

    s.wait_ge(Lp.h, PFC)
    s.activation(pz0, vv[0][:, :, 0:4], AF.Copy).then_inc(s_vb.h, 1)
    s.drain()
    s_vb.val += 1
    with s.Fori(0, NC2) as c2:
        for p01 in (0, 1):
            base = c2 * (2 * CH) + p01 * CH
            s.wait_ge(Lp.h, c2 * (2 * PFC) + (p01 + 1) * PFC)
            with s.Fori(0, CH - 1) as i_:
                s.wait_ge(s_mm.h, base + i_ + 1)
                s.activation(cc[:], pz0, AF.Tanh).then_inc(s_tanh.h, 1)
                s.drain()
                s.activation(pz0, vv[p01][:, :, ds(i_ * 4 + 4, 4)],
                             AF.Copy).then_inc(s_vb.h, 1)
                s.drain()
            s.wait_ge(s_mm.h, base + CH)
            s.activation(cc[:], pz0, AF.Tanh).then_inc(s_tanh.h, 1)
            s.drain()
            s.wait_ge(Lp.h, c2 * (2 * PFC) + (p01 + 2) * PFC)
            s.activation(pz0, vv[1 - p01][:, :, 0:4], AF.Copy).then_inc(s_vb.h, 1)
            s.drain()
    s_tanh.val += CH * NCH
    s_vb.val += CH * NCH

    t.wait_ge(Ld.h, Ld.val)
    t.wait_ge(V2.h, V2.val)
    with t.Fori(0, TG_) as i:
        t.wait_ge(s_vb.h, i + 1)
        t.wait_ge(s_bl.h, i + 1)
        for o in range(JD):
            for k in range(JD):
                mm = t.matmul(pz0[:, 4 * o:4 * (o + 1)],
                              wtb[:, k * D + 128 * o: k * D + 128 * (o + 1)],
                              hbf[:, 4 * k:4 * (k + 1)],
                              start=False, stop=(k == JD - 1),
                              skip_group_check=True)
        mm.then_inc(s_mm.h, 1)
    s_mm.val += TG_

    g.engine_nop().then_inc(s_fl.h, 1)
    s_fl.val += 1

    with v.Fori(0, NC2) as c2:
        for p01 in (0, 1):
            base = c2 * (2 * CH) + p01 * CH
            v.wait_ge(s_fl.h, c2 * 2 + p01)
            with v.Fori(0, CH) as i_:
                v.wait_ge(s_tanh.h, base + i_ + 1)
                v.tensor_sub(dd[:], cc[:], h[:])
                v.drain()
                v.tensor_mul(dd[:], dd[:], gv[p01][:, :, ds(i_ * 4, 4)])
                v.drain()
                v.tensor_add(h[:], h[:], dd[:])
                v.drain()
                v.tensor_copy(hbf[:], h[:]).then_inc(s_bl.h, 1)
                v.drain()
                v.tensor_copy(hv[p01][:, :, ds(i_ * 4, 4)],
                              h[:]).then_inc(T.h, 1)
                v.drain()
    s_bl.val += CH * NCH
    T.val += CH * NCH

    for c in range(NCH):
        par = c % 2
        Sc = St if par == 0 else St1
        g.wait_ge(T.h, CH * (c + 1))
        g.wait_ge(Sc.h, Sc.val)
        for j in range(JD):
            g.dma_start(out=hs_o[j][:, c * CB:(c + 1) * CB],
                        in_=hsb[par][:, j * CB:(j + 1) * CB]).then_inc(Sc.h, 16)
        Sc.val += 16 * JD
        g.wait_ge(Sc.h, Sc.val)
        g.engine_nop().then_inc(s_fl.h, 1)
        s_fl.val += 1
    g.wait_ge(St.h, St.val)
    g.wait_ge(St1.h, St1.val)
    if ctx is not None:
        ctx.barrier()
    return nc



def build_prog_F(TLOC, ctx=None, x_in=None, x_out=None, xaps=None, hs_off=None):
    _TAG = "F"
    if ctx is None:
        nc = bass.Bass("TRN2", target_bir_lowering=False, debug=False)
        DIN = lambda nm, sh, dt=F32: din(nc, nm, sh, dt)
        DOUT = lambda nm, sh, dt=F32: dout(nc, nm, sh, dt)
        DINT = lambda nm, sh, dt=F32: dint(nc, nm, sh, dt)
        SB = lambda nm, sh, dt: sb(nc, _TAG + "_" + nm, sh, dt)
        PS = lambda nm, sh, dt=F32: ps(nc, _TAG + "_" + nm, sh, dt)
        _sems = {n: Sem(nc, n) for n in ['Ld', 'Lw', 'St', 'St1', 'Sq', 'Ld2', 'Lp', 'T', 'V', 'S', 's_mm', 's_pre', 's_tanh', 's_bl', 's_fl', 'V2']}
    else:
        nc = ctx.nc
        ctx.reset()
        def DIN(nm, sh, dt=F32):
            if xaps and nm in xaps:
                return xaps[nm]
            if nm == "x_i" and x_in is not None:
                return x_in
            return ctx.din(_TAG + "_" + nm, sh, dt)
        def DOUT(nm, sh, dt=F32):
            if xaps and nm in xaps:
                return xaps[nm]
            if nm == "xout" and x_out is not None:
                return x_out
            return ctx.dout(_TAG + "_" + nm, sh, dt)
        def DINT(nm, sh, dt=F32):
            if xaps and nm in xaps:
                return xaps[nm]
            return ctx.dint(_TAG + "_" + nm, sh, dt)
        SB = lambda nm, sh, dt: ctx.al(sh, dt)
        _PSM = {'pA': 'ctx.pA'}
        PS = lambda nm, sh, dt=F32: ctx.psum_by_spec(_PSM[nm])
        _sems = ctx.sems
    Ld = _sems["Ld"]; Lw = _sems["Lw"]; St = _sems["St"]; St1 = _sems["St1"]; Sq = _sems["Sq"]; Ld2 = _sems["Ld2"]; Lp = _sems["Lp"]; T = _sems["T"]; V = _sems["V"]; S = _sems["S"]; s_mm = _sems["s_mm"]; s_pre = _sems["s_pre"]; s_tanh = _sems["s_tanh"]; s_bl = _sems["s_bl"]; s_fl = _sems["s_fl"]; V2 = _sems["V2"]
    TL = B * TLOC
    NTC = max(1, TL // 512)
    TCW = min(512, TL)

    x_i = DIN("x_i", [JD, 128, TL])
    xn2_i = DIN("xn2_i", [JD, 128, TL])
    hs_i = DIN("hs_i", [JD, 128, TL], BF16)
    woutT = DIN("woutT", [D, D])
    bout = DIN("bout", [1, D])
    xout_o = DOUT("xout", [JD, 128, TL])

    hsb = SB("hsb", [128, JD * TL], BF16)
    wtf = SB("wtf", [128, D], F32)
    wtb = SB("wtb", [128, JD * D], BF16)
    brow = SB("brow", [1, D], BF16)
    ones_row = SB("ones_row", [1, TL], BF16)
    xs = SB("xs", [128, TL], F32)
    xn2s = SB("xn2s", [128, TL], F32)
    t_y = SB("t_y", [128, TCW], F32)
    pA = PS("pA", [128, TCW], F32)

    v, s, t, g, sy = nc.vector, nc.scalar, nc.tensor, nc.gpsimd, nc.sync

    for j in range(JD):
        hsrc = hs_i[j] if hs_off is None else hs_i[j][:, ts(hs_off, TL)]
        sy.dma_start(out=hsb[:, j * TL:(j + 1) * TL], in_=hsrc).then_inc(Ld.h, 16)
        Ld.val += 16
    sy.dma_start(out=wtf[0:1, :], in_=bout).then_inc(Ld.h, 16)
    Ld.val += 16
    Ld.wait(v)
    v.tensor_copy(brow[:], wtf[0:1, :])
    v.memset(ones_row[:], 1.0)
    ei = V.inc(v.drain())
    V.wait(sy)
    for k in range(JD):
        Lw.wait(sy)
        V.wait(sy)
        sy.dma_start(out=wtf[:], in_=woutT[128 * k:128 * (k + 1), :]).then_inc(Lw.h, 16)
        Lw.val += 16
        Lw.wait(v)
        V.inc(v.tensor_copy(wtb[:, k * D:(k + 1) * D], wtf[:]))
    ei = V.inc(v.drain())
    V.wait(t, ei)

    for m8 in range(JD):
        V.wait(sy); Lw.wait(sy)
        sy.dma_start(out=xs[:], in_=x_i[m8]).then_inc(Lw.h, 16)
        sy.dma_start(out=xn2s[:], in_=xn2_i[m8]).then_inc(Lw.h, 16)
        Lw.val += 32
        for tc in range(NTC):
            c0 = tc * TCW
            for k in range(JD):
                mm = t.matmul(pA[:, 0:TCW],
                              wtb[:, k * D + 128 * m8: k * D + 128 * (m8 + 1)],
                              hsb[:, k * TL + c0: k * TL + c0 + TCW],
                              start=(k == 0), stop=False)
            mm = t.matmul(pA[:, 0:TCW], brow[:, 128 * m8:128 * (m8 + 1)],
                          ones_row[:, 0:TCW], start=False, stop=True)
            eit = T.inc(mm)
            T.wait(v, eit)
            Lw.wait(v)
            v.tensor_add(t_y[:], pA[:, 0:TCW], xs[:, c0:c0 + TCW])
            v.drain()
            ei = V.inc(v.tensor_add(t_y[:], t_y[:], xn2s[:, c0:c0 + TCW]))
            V.wait(t, ei)
            V.wait(g, ei)
            g.wait_ge(St.h, St.val)
            g.dma_start(out=xout_o[m8][:, c0:c0 + TCW], in_=t_y[:]).then_inc(St.h, 16)
            St.val += 16
            St.wait(v)
    St.wait(g)
    if ctx is not None:
        ctx.barrier()
    return nc



def build_prog_I(TLOC, ctx=None, x_in=None, x_out=None, xaps=None):
    _TAG = "I"
    if ctx is None:
        nc = bass.Bass("TRN2", target_bir_lowering=False, debug=False)
        DIN = lambda nm, sh, dt=F32: din(nc, nm, sh, dt)
        DOUT = lambda nm, sh, dt=F32: dout(nc, nm, sh, dt)
        DINT = lambda nm, sh, dt=F32: dint(nc, nm, sh, dt)
        SB = lambda nm, sh, dt: sb(nc, _TAG + "_" + nm, sh, dt)
        PS = lambda nm, sh, dt=F32: ps(nc, _TAG + "_" + nm, sh, dt)
        _sems = {n: Sem(nc, n) for n in ['Ld', 'Lw', 'St', 'St1', 'Sq', 'Ld2', 'Lp', 'T', 'V', 'S', 's_mm', 's_pre', 's_tanh', 's_bl', 's_fl', 'V2']}
    else:
        nc = ctx.nc
        ctx.reset()
        def DIN(nm, sh, dt=F32):
            if xaps and nm in xaps:
                return xaps[nm]
            if nm == "x_i" and x_in is not None:
                return x_in
            return ctx.din(_TAG + "_" + nm, sh, dt)
        def DOUT(nm, sh, dt=F32):
            if xaps and nm in xaps:
                return xaps[nm]
            if nm == "xout" and x_out is not None:
                return x_out
            return ctx.dout(_TAG + "_" + nm, sh, dt)
        def DINT(nm, sh, dt=F32):
            if xaps and nm in xaps:
                return xaps[nm]
            return ctx.dint(_TAG + "_" + nm, sh, dt)
        SB = lambda nm, sh, dt: ctx.al(sh, dt)
        _PSM = {'pA': 'ctx.pA', 'pB': 'ctx.pB', 'pC': 'ctx.pC', 'pD': 'ctx.pD'}
        PS = lambda nm, sh, dt=F32: ctx.psum_by_spec(_PSM[nm])
        _sems = ctx.sems
    Ld = _sems["Ld"]; Lw = _sems["Lw"]; St = _sems["St"]; St1 = _sems["St1"]; Sq = _sems["Sq"]; Ld2 = _sems["Ld2"]; Lp = _sems["Lp"]; T = _sems["T"]; V = _sems["V"]; S = _sems["S"]; s_mm = _sems["s_mm"]; s_pre = _sems["s_pre"]; s_tanh = _sems["s_tanh"]; s_bl = _sems["s_bl"]; s_fl = _sems["s_fl"]; V2 = _sems["V2"]
    TL = B * TLOC
    NTC = max(1, TL // 512)
    TCW = min(512, TL)
    E = 32
    HI = 512
    HCE = HI // 128
    HS = 1024
    HCS = HS // 128

    x_i = DIN("x_i", [JD, 128, TL])
    gw_i = DIN("gw", [D, E])
    gb_i = DIN("gb", [1, E])
    w1_i = DIN("w1", [E, D, HI])
    b1_i = DIN("b1", [E, HI])
    w3_i = DIN("w3", [E, D, HI])
    b3_i = DIN("b3", [E, HI])
    w2_i = DIN("w2", [E, HI, D])
    b2_i = DIN("b2", [E, D])
    sw1_i = DIN("sw1", [D, HS])
    sb1_i = DIN("sb1", [1, HS])
    sw3_i = DIN("sw3", [D, HS])
    sb3_i = DIN("sb3", [1, HS])
    sw2_i = DIN("sw2", [HS, D])
    sb2_i = DIN("sb2", [1, D])
    identf = DIN("identf", [128, 128], F32)
    xout_o = DOUT("xout", [JD, 128, TL])

    uni = SB("uni", [128, JD * TCW], F32)
    xjs = uni
    xnb = SB("xnb", [128, JD * TL], BF16)
    xb_c = SB("xb_c", [128, TCW], BF16)
    sq_c = SB("sq_c", [128, TCW], BF16)
    onescol = SB("onescol", [128, 1], BF16)
    ones_row = SB("ones_row", [1, TL], BF16)
    st_f = SB("st_f", [1, 4 * TCW], F32)
    st_b = SB("st_b", [1, 2 * TCW], BF16)
    gw_sb = SB("gw_sb", [128, JD * E], BF16)
    gb_sb = SB("gb_sb", [1, E], BF16)
    idf = SB("idf", [128, 128], F32)
    lg = SB("lg", [32, TCW], F32)
    lgt = SB("lgt", [128, E], F32)
    gm1 = SB("gm1", [128, 1], F32)
    gm8 = SB("gm8", [128, 8], F32)
    gsc = SB("gsc", [128, 8], F32)
    gtmp = SB("gtmp", [128, 8], F32)
    kbc = SB("kbc", [128, E], F32)
    wmap = SB("wmap", [128, E], F32)
    wfmb = SB("wfmb", [32, NTC * TCW], BF16)
    wrow_b = SB("wrow_b", [1, TCW], BF16)
    wbc = SB("wbc", [128, TCW], BF16)
    w1b = SB("w1b", [128, JD * HS], BF16)
    w3b = SB("w3b", [128, JD * HS], BF16)
    w2b = SB("w2b", [128, HCS * D], BF16)
    wtf = SB("wtf", [128, 1024], F32)
    brow = SB("brow", [1, 2 * HI], BF16)
    b2b = SB("b2b", [32, D], BF16)
    b2f = SB("b2f", [32, D], F32)
    sbrow = SB("sbrow", [1, 2 * HS + D], BF16)
    hbuf = uni.bitcast(BF16)
    htmp = SB("htmp", [128, TCW], BF16)
    htmp2 = SB("htmp2", [128, TCW], BF16)
    oacc = SB("oacc", [128, JD * TL], F32)
    t_y = SB("t_y", [128, TCW], F32)
    pA = PS("pA", [128, TCW], F32)
    pB = PS("pB", [128, TCW], F32)
    pC = PS("pC", [128, TCW], F32)
    pD = PS("pD", [128, TCW], F32)
    pb1 = pA; pb2 = pB; pbc = pC; pbc2 = pD
    pl = pA; ptt = pB; pth = pC
    pha = pA; phb = pB; po = pC; pw = pD

    v, s, t, g, sy = nc.vector, nc.scalar, nc.tensor, nc.gpsimd, nc.sync

    sy.dma_start(out=idf[:], in_=identf).then_inc(Ld.h, 16)
    Ld.val += 16
    for j in range(JD):
        Ld.wait(sy); V.wait(sy)
        sy.dma_start(out=wtf[:, 0:E], in_=gw_i[128 * j:128 * (j + 1), :]).then_inc(Ld.h, 16)
        Ld.val += 16
        Ld.wait(v)
        V.inc(v.tensor_copy(gw_sb[:, j * E:(j + 1) * E], wtf[:, 0:E]))
    Ld.wait(sy); V.wait(sy)
    sy.dma_start(out=wtf[0:1, 0:E], in_=gb_i).then_inc(Ld.h, 16)
    Ld.val += 16
    Ld.wait(v)
    v.tensor_copy(gb_sb[:], wtf[0:1, 0:E])
    v.memset(onescol[:], 1.0)
    v.memset(ones_row[:], 1.0)
    v.memset(oacc[:], 0.0)
    ei = V.inc(v.drain())
    V.wait(t, ei); V.wait(sy)
    for (name, src, off, w) in [("sb1", sb1_i, 0, HS), ("sb3", sb3_i, HS, HS), ("sb2", sb2_i, 2 * HS, D)]:
        Ld.wait(sy); V.wait(sy)
        sy.dma_start(out=wtf[0:1, 0:w], in_=src).then_inc(Ld.h, 16)
        Ld.val += 16
        Ld.wait(v)
        V.inc(v.tensor_copy(sbrow[:, off:off + w], wtf[0:1, 0:w]))
    Ld.wait(sy); V.wait(sy)
    sy.dma_start(out=b2f[0:32, :], in_=b2_i).then_inc(Ld.h, 16)
    Ld.val += 16
    Ld.wait(v)
    ei = V.inc(v.tensor_copy(b2b[0:32, :], b2f[0:32, :]))
    V.wait(t, ei)

    for tc in range(NTC):
        c0 = tc * TCW
        V.wait(sy); S.wait(sy); T.wait(sy); Ld.wait(sy)
        for j in range(JD):
            sy.dma_start(out=xjs[:, j * TCW:(j + 1) * TCW],
                         in_=x_i[j][:, c0:c0 + TCW]).then_inc(Ld.h, 16)
        Ld.val += 16 * JD
        Ld.wait(v); Ld.wait(s)
        for j in range(JD):
            xsl = xjs[:, j * TCW:(j + 1) * TCW]
            ei = V.inc(v.tensor_copy(xb_c[:], xsl))
            eis = S.inc(s.activation(sq_c[:], xsl, AF.Square))
            V.wait(t, ei); S.wait(t, eis)
            t.matmul(pb1[0:1, 0:TCW], onescol[:], xb_c[:], start=(j == 0), stop=(j == JD - 1))
            mm2 = t.matmul(pb2[0:1, 0:TCW], onescol[:], sq_c[:], start=(j == 0), stop=(j == JD - 1))
            eit = T.inc(mm2)
            T.wait(v, eit); T.wait(s, eit)
        v.tensor_scalar_mul(st_f[:, 0:TCW], pb1[0:1, 0:TCW], 1.0 / D)
        v.drain()
        v.tensor_scalar_mul(st_f[:, TCW:2 * TCW], pb2[0:1, 0:TCW], 1.0 / D)
        v.drain()
        v.tensor_mul(st_f[:, 3 * TCW:4 * TCW], st_f[:, 0:TCW], st_f[:, 0:TCW])
        v.drain()
        v.tensor_sub(st_f[:, TCW:2 * TCW], st_f[:, TCW:2 * TCW], st_f[:, 3 * TCW:4 * TCW])
        v.drain()
        v.tensor_scalar_add(st_f[:, TCW:2 * TCW], st_f[:, TCW:2 * TCW], 1e-5)
        ei = V.inc(v.drain())
        V.wait(s, ei)
        ei = S.inc(s.activation(st_f[:, 2 * TCW:3 * TCW], st_f[:, TCW:2 * TCW], AF.Sqrt))
        S.wait(v, ei)
        v.reciprocal(st_f[:, 2 * TCW:3 * TCW], st_f[:, 2 * TCW:3 * TCW])
        v.drain()
        v.tensor_copy(st_b[:, 0:TCW], st_f[:, 0:TCW])
        v.drain()
        ei = V.inc(v.tensor_copy(st_b[:, TCW:2 * TCW], st_f[:, 2 * TCW:3 * TCW]))
        V.wait(t, ei)
        mm1 = t.matmul(pbc[:, 0:TCW], ones_row[:, 0:128], st_b[:, 0:TCW], start=True, stop=True)
        mm2 = t.matmul(pbc2[:, 0:TCW], ones_row[:, 0:128], st_b[:, TCW:2 * TCW], start=True, stop=True)
        eit = T.inc(mm2)
        T.wait(v, eit)
        for j in range(JD):
            v.tensor_sub(t_y[:], xjs[:, j * TCW:(j + 1) * TCW], pbc[:, 0:TCW])
            v.drain()
            v.tensor_mul(xnb[:, j * TL + c0: j * TL + c0 + TCW], t_y[:], pbc2[:, 0:TCW])
            v.drain()
        ei = V.inc(v.drain())
        V.wait(t, ei)
        for j in range(JD):
            t.matmul(pl[0:32, 0:TCW], gw_sb[:, j * E:(j + 1) * E],
                     xnb[:, j * TL + c0: j * TL + c0 + TCW], start=(j == 0), stop=False)
        mm = t.matmul(pl[0:32, 0:TCW], gb_sb[:], ones_row[:, 0:TCW], start=False, stop=True)
        eit = T.inc(mm)
        T.wait(v, eit)
        ei = V.inc(v.tensor_copy(lg[0:32, 0:TCW], pl[0:32, 0:TCW]))
        V.wait(t, ei)
        for sbt in range(TCW // 128):
            mm = t.matmul(ptt[:, 0:E], lg[0:32, 128 * sbt:128 * (sbt + 1)], idf[0:32, 0:E],
                          start=True, stop=True)
            eit = T.inc(mm)
            T.wait(v, eit)
            v.tensor_copy(lgt[:], ptt[:, 0:E])
            v.drain()
            v.reduce_max(gm1[:], lgt[:], AX.X)
            v.drain()
            ei = V.inc(v.tensor_scalar(lgt[:], lgt[:], gm1[:], None, ALU.subtract))
            V.wait(s, ei)
            ei = S.inc(s.activation(lgt[:], lgt[:], AF.Exp))
            S.wait(v, ei)
            v.tensor_reduce(gm1[:], lgt[:], AX.X, ALU.add)
            v.drain()
            v.reciprocal(gm1[:], gm1[:])
            v.drain()
            v.tensor_scalar(lgt[:], lgt[:], gm1[:], None, ALU.mult)
            v.drain()
            pairs = [(0, 1), (0, 2), (0, 3), (1, 2), (1, 3), (2, 3)]
            first = True
            for (a, b_) in pairs:
                v.tensor_add(gtmp[:], lgt[:, a:E:4], lgt[:, b_:E:4])
                v.drain()
                if first:
                    v.tensor_copy(gsc[:], gtmp[:])
                    first = False
                else:
                    v.tensor_max(gsc[:], gsc[:], gtmp[:])
                v.drain()
            v.max(out=gm8[:], in_=gsc[:])
            v.drain()
            v.tensor_scalar(gsc[:], gsc[:], gm8[:, 3:4], None, ALU.is_ge)
            v.drain()
            v.tensor_copy(kbc[:].rearrange("p (g i) -> p g i", g=8),
                          gsc[:].unsqueeze(2).to_broadcast([128, 8, 4]))
            v.drain()
            v.tensor_mul(wmap[:], lgt[:], kbc[:])
            v.drain()
            v.tensor_scalar(kbc[:], kbc[:], -1.0, 1e30, ALU.add, ALU.mult)
            v.drain()
            v.tensor_add(wmap[:], wmap[:], kbc[:])
            v.drain()
            v.max(out=gm8[:], in_=wmap[:])
            v.drain()
            v.tensor_scalar(wmap[:], wmap[:], gm8[:, 3:4], None, ALU.is_ge)
            v.drain()
            ei = V.inc(v.tensor_mul(wmap[:], wmap[:], lgt[:]))
            V.wait(t, ei)
            mm = t.transpose(pth[0:32, 0:128], wmap[:], idf[:])
            eit = T.inc(mm)
            T.wait(v, eit)
            ei = V.inc(v.tensor_copy(wfmb[0:32, c0 + 128 * sbt: c0 + 128 * (sbt + 1)],
                                     pth[0:32, 0:128]))
            V.wait(t, ei)
        v.drain()
    St.wait(v)
    ei = V.inc(v.drain())
    V.wait(t, ei); V.wait(g, ei); V.wait(sy)

    for e in range(-1, E):
        HCX = HCS if e < 0 else HCE
        T.wait(sy)
        for k in range(JD):
            for (wsrc, dstb, hw) in ([(sw1_i, w1b, HS), (sw3_i, w3b, HS)] if e < 0
                                     else [(w1_i[e], w1b, HI), (w3_i[e], w3b, HI)]):
                V.wait(sy); Lw.wait(sy)
                sy.dma_start(out=wtf[:, 0:hw], in_=wsrc[128 * k:128 * (k + 1), :]).then_inc(Lw.h, 16)
                Lw.val += 16
                Lw.wait(v)
                V.inc(v.tensor_copy(dstb[:, k * hw:(k + 1) * hw], wtf[:, 0:hw]))
        for hk in range(HCX):
            V.wait(sy); Lw.wait(sy)
            src = sw2_i[128 * hk:128 * (hk + 1), :] if e < 0 else w2_i[e, 128 * hk:128 * (hk + 1), :]
            sy.dma_start(out=wtf[:, 0:D], in_=src).then_inc(Lw.h, 16)
            Lw.val += 16
            Lw.wait(v)
            V.inc(v.tensor_copy(w2b[:, hk * D:(hk + 1) * D], wtf[:, 0:D]))
        if e >= 0:
            for (bsrc, off) in [(b1_i[e:e + 1, :], 0), (b3_i[e:e + 1, :], HI)]:
                V.wait(sy); Lw.wait(sy)
                sy.dma_start(out=wtf[0:1, 0:HI], in_=bsrc).then_inc(Lw.h, 16)
                Lw.val += 16
                Lw.wait(v)
                V.inc(v.tensor_copy(brow[:, off:off + HI], wtf[0:1, 0:HI]))
        ei = V.inc(v.drain())
        V.wait(t, ei)
        for tc in range(NTC):
            c0 = tc * TCW
            if e >= 0:
                V.wait(sy); Lw.wait(sy)
                T.wait(sy)
                sy.dma_start(out=wrow_b[:], in_=wfmb[e:e + 1, c0:c0 + TCW]).then_inc(Lw.h, 16)
                Lw.val += 16
                Lw.wait(t)
                mm = t.matmul(pw[:, 0:TCW], ones_row[:, 0:128], wrow_b[:], start=True, stop=True)
                eit = T.inc(mm)
                T.wait(v, eit)
                ei = V.inc(v.tensor_copy(wbc[:], pw[:, 0:TCW]))
                V.wait(t, ei)
            for hk in range(HCX):
                hcol = 128 * hk
                HW = HS if e < 0 else HI
                pa_ = pha if hk % 2 == 0 else po
                pb_ = phb if hk % 2 == 0 else pw
                ht_ = htmp if hk % 2 == 0 else htmp2
                if hk >= 2:
                    V.wait(t, V.val - 1)
                for k in range(JD):
                    t.matmul(pa_[:, 0:TCW], w1b[:, k * HW + hcol: k * HW + hcol + 128],
                             xnb[:, k * TL + c0: k * TL + c0 + TCW],
                             start=(k == 0), stop=False)
                brsrc = sbrow[:, hcol:hcol + 128] if e < 0 else brow[:, hcol:hcol + 128]
                t.matmul(pa_[:, 0:TCW], brsrc, ones_row[:, 0:TCW], start=False, stop=True)
                for k in range(JD):
                    t.matmul(pb_[:, 0:TCW], w3b[:, k * HW + hcol: k * HW + hcol + 128],
                             xnb[:, k * TL + c0: k * TL + c0 + TCW],
                             start=(k == 0), stop=False)
                brsrc = sbrow[:, HS + hcol:HS + hcol + 128] if e < 0 else brow[:, HI + hcol:HI + hcol + 128]
                mm = t.matmul(pb_[:, 0:TCW], brsrc, ones_row[:, 0:TCW], start=False, stop=True)
                eit = T.inc(mm)
                T.wait(s, eit)
                ei = S.inc(s.activation(ht_[:], pa_[:, 0:TCW], AF.Silu))
                S.wait(v, ei)
                v.tensor_mul(hbuf[:, hk * TCW:(hk + 1) * TCW], ht_[:], pb_[:, 0:TCW])
                v.drain()
                if e >= 0:
                    v.tensor_mul(hbuf[:, hk * TCW:(hk + 1) * TCW],
                                 hbuf[:, hk * TCW:(hk + 1) * TCW], wbc[:])
                    v.drain()
                eiv = V.inc(v.drain())
                V.wait(s, eiv - 2 if hk >= 1 else 0)
            V.wait(t, V.val)
            for m8 in range(JD):
                pop = po if m8 % 2 == 0 else pw
                for hk in range(HCX):
                    mm = t.matmul(pop[:, 0:TCW],
                                  w2b[:, hk * D + 128 * m8: hk * D + 128 * (m8 + 1)],
                                  hbuf[:, hk * TCW:(hk + 1) * TCW],
                                  start=(hk == 0), stop=(hk == HCX - 1))
                eit = T.inc(mm)
                T.wait(v, eit)
                osl = oacc[:, m8 * TL + c0: m8 * TL + c0 + TCW]
                v.tensor_add(osl, osl, pop[:, 0:TCW])
                ei = V.inc(v.drain())
                if m8 >= 1:
                    V.wait(t, ei - 1)
            V.wait(t, V.val)

    for m8 in range(JD):
        for tc in range(NTC):
            c0 = tc * TCW
            t.matmul(po[:, 0:TCW], b2b[0:32, 128 * m8:128 * (m8 + 1)],
                     wfmb[0:32, c0:c0 + TCW], start=True, stop=False)
            mm = t.matmul(po[:, 0:TCW], sbrow[:, 2 * HS + 128 * m8:2 * HS + 128 * (m8 + 1)],
                          ones_row[:, 0:TCW], start=False, stop=True)
            eit = T.inc(mm)
            T.wait(v, eit)
            osl = oacc[:, m8 * TL + c0: m8 * TL + c0 + TCW]
            v.tensor_add(osl, osl, po[:, 0:TCW])
            ei = V.inc(v.drain())
            V.wait(t, ei)
    for m8 in range(JD):
        V.wait(sy); Ld.wait(sy)
        sy.dma_start(out=xjs[:, 0:TL], in_=x_i[m8]).then_inc(Ld.h, 16)
        Ld.val += 16
        Ld.wait(v)
        St.wait(v)
        v.tensor_add(oacc[:, m8 * TL:(m8 + 1) * TL], oacc[:, m8 * TL:(m8 + 1) * TL],
                     xjs[:, 0:TL])
        ei = V.inc(v.drain())
        V.wait(g, ei)
        g.wait_ge(St.h, St.val)
        g.dma_start(out=xout_o[m8], in_=oacc[:, m8 * TL:(m8 + 1) * TL]).then_inc(St.h, 16)
        St.val += 16
    St.wait(g)
    if ctx is not None:
        ctx.barrier()
    return nc



def build_prog_J(TLOC, ctx=None, x_in=None, x_out=None, xaps=None):
    _TAG = "J"
    if ctx is None:
        nc = bass.Bass("TRN2", target_bir_lowering=False, debug=False)
        DIN = lambda nm, sh, dt=F32: din(nc, nm, sh, dt)
        DOUT = lambda nm, sh, dt=F32: dout(nc, nm, sh, dt)
        DINT = lambda nm, sh, dt=F32: dint(nc, nm, sh, dt)
        SB = lambda nm, sh, dt: sb(nc, _TAG + "_" + nm, sh, dt)
        PS = lambda nm, sh, dt=F32: ps(nc, _TAG + "_" + nm, sh, dt)
        _sems = {n: Sem(nc, n) for n in ['Ld', 'Lw', 'St', 'St1', 'Sq', 'Ld2', 'Lp', 'T', 'V', 'S', 's_mm', 's_pre', 's_tanh', 's_bl', 's_fl', 'V2']}
    else:
        nc = ctx.nc
        ctx.reset()
        def DIN(nm, sh, dt=F32):
            if xaps and nm in xaps:
                return xaps[nm]
            if nm == "x_i" and x_in is not None:
                return x_in
            return ctx.din(_TAG + "_" + nm, sh, dt)
        def DOUT(nm, sh, dt=F32):
            if xaps and nm in xaps:
                return xaps[nm]
            if nm == "xout" and x_out is not None:
                return x_out
            return ctx.dout(_TAG + "_" + nm, sh, dt)
        def DINT(nm, sh, dt=F32):
            if xaps and nm in xaps:
                return xaps[nm]
            return ctx.dint(_TAG + "_" + nm, sh, dt)
        SB = lambda nm, sh, dt: ctx.al(sh, dt)
        _PSM = {'pA': 'ctx.pA', 'pB': 'ctx.pB', 'pC': 'ctx.pC', 'pD': 'ctx.pD'}
        PS = lambda nm, sh, dt=F32: ctx.psum_by_spec(_PSM[nm])
        _sems = ctx.sems
    Ld = _sems["Ld"]; Lw = _sems["Lw"]; St = _sems["St"]; St1 = _sems["St1"]; Sq = _sems["Sq"]; Ld2 = _sems["Ld2"]; Lp = _sems["Lp"]; T = _sems["T"]; V = _sems["V"]; S = _sems["S"]; s_mm = _sems["s_mm"]; s_pre = _sems["s_pre"]; s_tanh = _sems["s_tanh"]; s_bl = _sems["s_bl"]; s_fl = _sems["s_fl"]; V2 = _sems["V2"]
    TL = B * TLOC
    NTC = max(1, TL // 512)
    TCW = min(512, TL)
    CH = 64
    NCHJ = TLOC // CH
    PI = float(np.pi)
    x_i = DIN("x_i", [JD, 128, TL])
    Wk_i = DIN("Wk", [D, D]); bk_i = DIN("bk", [1, D])
    Wv_i = DIN("Wv", [D, D]); bv_i = DIN("bv", [1, D])
    Wq_i = DIN("Wq", [D, D]); bq_i = DIN("bq", [1, D])
    Wkp_i = DIN("Wkp", [D, D]); bkp_i = DIN("bkp", [1, D])
    Wqp_i = DIN("Wqp", [D, D]); bqp_i = DIN("bqp", [1, D])
    Wo_i = DIN("Wo", [D, D]); bo_i = DIN("bo", [1, D])
    ps_i = DIN("phs", [128, JD])
    lnwb = DIN("lnwb", [2, 128, JD])
    xout_o = DOUT("xout", [JD, 128, TL])
    xn1_s = DINT("xn1_s", [JD, 128, TL])

    xnb = SB("xnb", [128, JD * TL], BF16)
    kqb = SB("kqb", [128, JD * TL], BF16)
    vbuf = SB("vbuf", [128, JD * TL], BF16)
    mrb = SB("mrb", [128, JD * TL], BF16)
    mib = SB("mib", [128, JD * TL], BF16)
    wtf = SB("wtf", [128, 512], F32)
    wtb = SB("wtb", [128, JD * D], BF16)
    brow = SB("brow", [65, 2 * D], BF16)

    def bro(q, lo, w_):
        p0 = 32 * (q % 3)
        c0_ = D * (q // 3)
        return brow[p0:p0 + 1, c0_ + lo:c0_ + lo + w_]

    def onr(q, w_):
        p0 = 32 * (q % 3)
        return ones_row[p0:p0 + 1, 0:w_]
    lnw_sb = SB("lnw_sb", [128, 2 * JD], F32)
    ps_sb = SB("ps_sb", [128, JD], F32)
    onescol = SB("onescol", [128, 1], BF16)
    ones_row = SB("ones_row", [65, TCW], BF16)
    st_f = SB("st_f", [1, 4 * TCW], F32)
    st_b = SB("st_b", [1, 2 * TCW], BF16)
    xb_c = SB("xb_c", [128, TCW], BF16)
    sq_c = SB("sq_c", [128, TCW], BF16)
    t_y = SB("t_y", [128, TCW], F32)
    t_z = SB("t_z", [128, TCW], F32)
    t_w = SB("t_w", [128, TCW], F32)
    pA = PS("pA", [128, TCW], F32)
    pB = PS("pB", [128, TCW], F32)
    pC = PS("pC", [128, TCW], F32)
    pD = PS("pD", [128, TCW], F32)

    v, s, t, g, sy = nc.vector, nc.scalar, nc.tensor, nc.gpsimd, nc.sync

    sy.dma_start(out=lnw_sb[:, 0:JD], in_=lnwb[0]).then_inc(Ld.h, 16); Ld.val += 16
    sy.dma_start(out=lnw_sb[:, JD:2 * JD], in_=lnwb[1]).then_inc(Ld.h, 16); Ld.val += 16
    sy.dma_start(out=ps_sb[:], in_=ps_i).then_inc(Ld.h, 16); Ld.val += 16
    for q, src in enumerate([bk_i, bv_i, bq_i, bkp_i, bqp_i, bo_i]):
        for hh in range(2):
            Ld.wait(sy); V.wait(sy)
            sy.dma_start(out=wtf[0:1, :], in_=src[:, 512 * hh:512 * (hh + 1)]).then_inc(Ld.h, 16)
            Ld.val += 16
            Ld.wait(v)
            V.inc(v.tensor_copy(bro(q, 512 * hh, 512), wtf[0:1, :]))
    v.memset(onescol[:], 1.0)
    v.memset(ones_row[:], 1.0)
    ei = V.inc(v.drain())
    V.wait(t, ei); V.wait(sy)

    def load_w(wsrc):
        for k in range(JD):
            for hh in range(2):
                Lw.wait(sy); V.wait(sy); T.wait(sy)
                sy.dma_start(out=wtf[:], in_=wsrc[128 * k:128 * (k + 1),
                                                  512 * hh:512 * (hh + 1)]).then_inc(Lw.h, 16)
                Lw.val += 16
                Lw.wait(v)
                V.inc(v.tensor_copy(wtb[:, k * D + 512 * hh: k * D + 512 * (hh + 1)], wtf[:]))
        ei = V.inc(v.drain())
        V.wait(t, ei)

    def proj(src_sb, dst_sb, brow_off):
        for m8 in range(JD):
            for tc in range(NTC):
                c0 = tc * TCW
                for k in range(JD):
                    t.matmul(pA[:, 0:TCW], wtb[:, k * D + 128 * m8: k * D + 128 * (m8 + 1)],
                             src_sb[:, k * TL + c0: k * TL + c0 + TCW],
                             start=(k == 0), stop=False)
                mm = t.matmul(pA[:, 0:TCW], bro(brow_off, 128 * m8, 128),
                              onr(brow_off, TCW), start=False, stop=True)
                eit = T.inc(mm)
                T.wait(v, eit)
                ei = V.inc(v.tensor_copy(dst_sb[:, m8 * TL + c0: m8 * TL + c0 + TCW],
                                         pA[:, 0:TCW]))
                V.wait(t, ei)

    def emit_trig(phase_src_psum, m8, tc):
        c0 = tc * TCW
        eit = T.inc(phase_src_psum)
        T.wait(s, eit)
        ei = S.inc(s.activation(t_y[:], pA[:, 0:TCW], AF.Tanh))
        S.wait(t, ei)
        S.wait(v, ei)
        v.tensor_scalar(t_y[:], t_y[:], ps_sb[:, m8:m8 + 1], None, ALU.mult)
        v.drain()
        v.tensor_scalar_mul(t_z[:], t_y[:], 1.0 / (2 * PI))
        v.drain()
        v.tensor_copy(t_w[:].bitcast(I32), t_z[:])
        v.drain()
        v.tensor_copy(t_z[:], t_w[:].bitcast(I32))
        v.drain()
        v.scalar_tensor_tensor(t_y[:], t_z[:], -2 * PI, t_y[:], ALU.mult, ALU.add)
        v.drain()
        v.tensor_scalar(t_z[:], t_y[:], PI, -2 * PI, ALU.is_gt, ALU.mult)
        v.drain()
        v.tensor_add(t_y[:], t_y[:], t_z[:])
        v.drain()
        v.tensor_scalar(t_z[:], t_y[:], -PI, 2 * PI, ALU.is_lt, ALU.mult)
        v.drain()
        v.tensor_add(t_y[:], t_y[:], t_z[:])
        v.drain()

    def sin_of(src_ap, dst_ap):
        ei = V.inc(v.drain())
        V.wait(s, ei)
        ei = S.inc(s.activation(dst_ap, src_ap, AF.Sin))
        S.wait(v, ei)

    for tc in range(NTC):
        c0 = tc * TCW
        V.wait(sy); S.wait(sy); T.wait(sy); Ld.wait(sy)
        for j in range(JD):
            sy.dma_start(out=t_w[:], in_=x_i[j][:, c0:c0 + TCW]).then_inc(Ld.h, 16)
            Ld.val += 16
            Ld.wait(v); Ld.wait(s)
            ei = V.inc(v.tensor_copy(xb_c[:], t_w[:]))
            eis = S.inc(s.activation(sq_c[:], t_w[:], AF.Square))
            V.wait(t, ei); S.wait(t, eis)
            t.matmul(pA[0:1, 0:TCW], onescol[:], xb_c[:], start=(j == 0), stop=(j == JD - 1))
            mm2 = t.matmul(pB[0:1, 0:TCW], onescol[:], sq_c[:], start=(j == 0), stop=(j == JD - 1))
            eit = T.inc(mm2)
            T.wait(v, eit); T.wait(s, eit); T.wait(sy, eit)
            V.wait(sy)
        v.tensor_scalar_mul(st_f[:, 0:TCW], pA[0:1, 0:TCW], 1.0 / D)
        v.drain()
        v.tensor_scalar_mul(st_f[:, TCW:2 * TCW], pB[0:1, 0:TCW], 1.0 / D)
        v.drain()
        v.tensor_mul(st_f[:, 3 * TCW:4 * TCW], st_f[:, 0:TCW], st_f[:, 0:TCW])
        v.drain()
        v.tensor_sub(st_f[:, TCW:2 * TCW], st_f[:, TCW:2 * TCW], st_f[:, 3 * TCW:4 * TCW])
        v.drain()
        v.tensor_scalar_add(st_f[:, TCW:2 * TCW], st_f[:, TCW:2 * TCW], 1e-5)
        ei = V.inc(v.drain())
        V.wait(s, ei)
        ei = S.inc(s.activation(st_f[:, 2 * TCW:3 * TCW], st_f[:, TCW:2 * TCW], AF.Sqrt))
        S.wait(v, ei)
        v.reciprocal(st_f[:, 2 * TCW:3 * TCW], st_f[:, 2 * TCW:3 * TCW])
        v.drain()
        v.tensor_copy(st_b[:, 0:TCW], st_f[:, 0:TCW])
        v.drain()
        ei = V.inc(v.tensor_copy(st_b[:, TCW:2 * TCW], st_f[:, 2 * TCW:3 * TCW]))
        V.wait(t, ei)
        mm1 = t.matmul(pC[:, 0:TCW], ones_row[0:1, 0:128], st_b[:, 0:TCW], start=True, stop=True)
        mm2 = t.matmul(pD[:, 0:TCW], ones_row[0:1, 0:128], st_b[:, TCW:2 * TCW], start=True, stop=True)
        eit = T.inc(mm2)
        T.wait(v, eit)
        T.wait(sy, eit)
        for j in range(JD):
            sy.dma_start(out=t_w[:], in_=x_i[j][:, c0:c0 + TCW]).then_inc(Ld.h, 16)
            Ld.val += 16
            Ld.wait(v)
            v.tensor_sub(t_y[:], t_w[:], pC[:, 0:TCW])
            v.drain()
            v.tensor_mul(t_z[:], t_y[:], pD[:, 0:TCW])
            v.drain()
            ei = V.inc(v.tensor_copy(xnb[:, j * TL + c0: j * TL + c0 + TCW], t_z[:]))
            V.wait(g, ei)
            g.wait_ge(St.h, St.val)
            g.dma_start(out=xn1_s[j][:, c0:c0 + TCW], in_=t_z[:]).then_inc(St.h, 16)
            St.val += 16
            St.wait(v)
            V.wait(sy)
        V.wait(t, V.val)

    load_w(Wv_i)
    proj(xnb, vbuf, 1)
    load_w(Wk_i)
    proj(xnb, kqb, 0)
    load_w(Wkp_i)
    for m8 in range(JD):
        for tc in range(NTC):
            c0 = tc * TCW
            for k in range(JD):
                t.matmul(pA[:, 0:TCW], wtb[:, k * D + 128 * m8: k * D + 128 * (m8 + 1)],
                         kqb[:, k * TL + c0: k * TL + c0 + TCW],
                         start=(k == 0), stop=False)
            mm = t.matmul(pA[:, 0:TCW], bro(3, 128 * m8, 128),
                          onr(3, TCW), start=False, stop=True)
            emit_trig(mm, m8, tc)
            vsl = vbuf[:, m8 * TL + c0: m8 * TL + c0 + TCW]
            sin_of(t_y[:], t_w[:])
            v.tensor_mul(mib[:, m8 * TL + c0: m8 * TL + c0 + TCW], vsl, t_w[:])
            v.drain()
            v.tensor_scalar_add(t_y[:], t_y[:], PI / 2)
            v.drain()
            v.tensor_scalar(t_z[:], t_y[:], PI, -2 * PI, ALU.is_gt, ALU.mult)
            v.drain()
            v.tensor_add(t_y[:], t_y[:], t_z[:])
            sin_of(t_y[:], t_w[:])
            ei = V.inc(v.tensor_mul(mrb[:, m8 * TL + c0: m8 * TL + c0 + TCW], vsl, t_w[:]))
            V.wait(s, ei)

    for j in range(JD):
        jo = j * TL
        for b_ in range(B):
            for ch in range(NCHJ):
                off = jo + ch * CH * B + b_
                e_ = off + (CH - 1) * B + 1
                v.tensor_tensor_scan(mrb[:, off:e_:B], mrb[:, off:e_:B],
                                     mrb[:, off:e_:B], 0.0, ALU.add, ALU.bypass)
                v.tensor_tensor_scan(mib[:, off:e_:B], mib[:, off:e_:B],
                                     mib[:, off:e_:B], 0.0, ALU.add, ALU.bypass)
        v.drain()

    load_w(Wq_i)
    proj(xnb, kqb, 2)
    load_w(Wqp_i)
    for m8 in range(JD):
        for tc in range(NTC):
            c0 = tc * TCW
            sl = slice(m8 * TL + c0, m8 * TL + c0 + TCW)
            for k in range(JD):
                t.matmul(pA[:, 0:TCW], wtb[:, k * D + 128 * m8: k * D + 128 * (m8 + 1)],
                         kqb[:, k * TL + c0: k * TL + c0 + TCW],
                         start=(k == 0), stop=False)
            mm = t.matmul(pA[:, 0:TCW], bro(4, 128 * m8, 128),
                          onr(4, TCW), start=False, stop=True)
            emit_trig(mm, m8, tc)
            sin_of(t_y[:], t_w[:])
            v.tensor_mul(t_z[:], mib[:, sl], t_w[:])
            v.drain()
            v.tensor_scalar_add(t_y[:], t_y[:], PI / 2)
            v.drain()
            v.tensor_scalar(t_w[:], t_y[:], PI, -2 * PI, ALU.is_gt, ALU.mult)
            v.drain()
            v.tensor_add(t_y[:], t_y[:], t_w[:])
            sin_of(t_y[:], t_w[:])
            v.tensor_mul(t_y[:], mrb[:, sl], t_w[:])
            v.drain()
            ei = V.inc(v.tensor_add(mrb[:, sl], t_y[:], t_z[:]))
            V.wait(s, ei)

    for tc in range(NTC):
        c0 = tc * TCW
        S.wait(v); T.wait(v)
        for j in range(JD):
            xsl = mrb[:, j * TL + c0: j * TL + c0 + TCW]
            eis = S.inc(s.activation(sq_c[:], xsl, AF.Square))
            S.wait(t, eis)
            t.matmul(pA[0:1, 0:TCW], onescol[:], xsl, start=(j == 0), stop=(j == JD - 1))
            mm2 = t.matmul(pB[0:1, 0:TCW], onescol[:], sq_c[:], start=(j == 0), stop=(j == JD - 1))
            eit = T.inc(mm2)
            T.wait(s, eit)
        T.wait(v)
        v.tensor_scalar_mul(st_f[:, 0:TCW], pA[0:1, 0:TCW], 1.0 / D)
        v.drain()
        v.tensor_scalar_mul(st_f[:, TCW:2 * TCW], pB[0:1, 0:TCW], 1.0 / D)
        v.drain()
        v.tensor_mul(st_f[:, 3 * TCW:4 * TCW], st_f[:, 0:TCW], st_f[:, 0:TCW])
        v.drain()
        v.tensor_sub(st_f[:, TCW:2 * TCW], st_f[:, TCW:2 * TCW], st_f[:, 3 * TCW:4 * TCW])
        v.drain()
        v.tensor_scalar_add(st_f[:, TCW:2 * TCW], st_f[:, TCW:2 * TCW], float(D) * 1e-5)
        ei = V.inc(v.drain())
        V.wait(s, ei)
        ei = S.inc(s.activation(st_f[:, 2 * TCW:3 * TCW], st_f[:, TCW:2 * TCW], AF.Sqrt))
        S.wait(v, ei)
        v.reciprocal(st_f[:, 2 * TCW:3 * TCW], st_f[:, 2 * TCW:3 * TCW])
        v.drain()
        v.tensor_copy(st_b[:, 0:TCW], st_f[:, 0:TCW])
        v.drain()
        ei = V.inc(v.tensor_copy(st_b[:, TCW:2 * TCW], st_f[:, 2 * TCW:3 * TCW]))
        V.wait(t, ei)
        mm1 = t.matmul(pC[:, 0:TCW], ones_row[0:1, 0:128], st_b[:, 0:TCW], start=True, stop=True)
        mm2 = t.matmul(pD[:, 0:TCW], ones_row[0:1, 0:128], st_b[:, TCW:2 * TCW], start=True, stop=True)
        eit = T.inc(mm2)
        T.wait(v, eit)
        for j in range(JD):
            sl = slice(j * TL + c0, j * TL + c0 + TCW)
            v.tensor_sub(t_y[:], mrb[:, sl], pC[:, 0:TCW])
            v.drain()
            v.tensor_mul(t_y[:], t_y[:], pD[:, 0:TCW])
            v.drain()
            v.tensor_scalar(t_y[:], t_y[:], lnw_sb[:, j:j + 1], lnw_sb[:, JD + j:JD + j + 1],
                            ALU.mult, ALU.add)
            v.drain()
            v.tensor_copy(mrb[:, sl], t_y[:])
            v.drain()
        ei = V.inc(v.drain())
        V.wait(t, ei)

    load_w(Wo_i)
    for m8 in range(JD):
        for tc in range(NTC):
            c0 = tc * TCW
            for k in range(JD):
                t.matmul(pA[:, 0:TCW], wtb[:, k * D + 128 * m8: k * D + 128 * (m8 + 1)],
                         mrb[:, k * TL + c0: k * TL + c0 + TCW],
                         start=(k == 0), stop=False)
            mm = t.matmul(pA[:, 0:TCW], bro(5, 128 * m8, 128),
                          onr(5, TCW), start=False, stop=True)
            eit = T.inc(mm)
            T.wait(sy, eit)
            V.wait(sy)
            sy.dma_start(out=t_w[:], in_=x_i[m8][:, c0:c0 + TCW]).then_inc(Ld.h, 16)
            Ld.val += 16
            sy.dma_start(out=t_z[:], in_=xn1_s[m8][:, c0:c0 + TCW]).then_inc(Ld.h, 16)
            Ld.val += 16
            T.wait(v, eit)
            Ld.wait(v)
            v.tensor_add(t_y[:], pA[:, 0:TCW], t_w[:])
            v.drain()
            ei = V.inc(v.tensor_add(t_y[:], t_y[:], t_z[:]))
            V.wait(t, ei)
            V.wait(g, ei)
            g.wait_ge(St.h, St.val)
            g.dma_start(out=xout_o[m8][:, c0:c0 + TCW], in_=t_y[:]).then_inc(St.h, 16)
            St.val += 16
            St.wait(v)
    St.wait(g)
    if ctx is not None:
        ctx.barrier()
    return nc



def build_prog_H(TLOC, ctx=None, x_in=None, x_out=None, xaps=None):
    _TAG = "H"
    if ctx is None:
        nc = bass.Bass("TRN2", target_bir_lowering=False, debug=False)
        DIN = lambda nm, sh, dt=F32: din(nc, nm, sh, dt)
        DOUT = lambda nm, sh, dt=F32: dout(nc, nm, sh, dt)
        DINT = lambda nm, sh, dt=F32: dint(nc, nm, sh, dt)
        SB = lambda nm, sh, dt: sb(nc, _TAG + "_" + nm, sh, dt)
        PS = lambda nm, sh, dt=F32: ps(nc, _TAG + "_" + nm, sh, dt)
        _sems = {n: Sem(nc, n) for n in ['Ld', 'Lw', 'St', 'St1', 'Sq', 'Ld2', 'Lp', 'T', 'V', 'S', 's_mm', 's_pre', 's_tanh', 's_bl', 's_fl', 'V2']}
    else:
        nc = ctx.nc
        ctx.reset()
        def DIN(nm, sh, dt=F32):
            if xaps and nm in xaps:
                return xaps[nm]
            if nm == "x_i" and x_in is not None:
                return x_in
            return ctx.din(_TAG + "_" + nm, sh, dt)
        def DOUT(nm, sh, dt=F32):
            if xaps and nm in xaps:
                return xaps[nm]
            if nm == "xout" and x_out is not None:
                return x_out
            return ctx.dout(_TAG + "_" + nm, sh, dt)
        def DINT(nm, sh, dt=F32):
            if xaps and nm in xaps:
                return xaps[nm]
            return ctx.dint(_TAG + "_" + nm, sh, dt)
        SB = lambda nm, sh, dt: ctx.al(sh, dt)
        _PSM = {'pA': 'ctx.pA', 'pB': 'ctx.pB', 'pC': 'ctx.pC', 'pD': 'ctx.pD', 'pT': 'ctx.pT'}
        PS = lambda nm, sh, dt=F32: ctx.psum_by_spec(_PSM[nm])
        _sems = ctx.sems
    Ld = _sems["Ld"]; Lw = _sems["Lw"]; St = _sems["St"]; St1 = _sems["St1"]; Sq = _sems["Sq"]; Ld2 = _sems["Ld2"]; Lp = _sems["Lp"]; T = _sems["T"]; V = _sems["V"]; S = _sems["S"]; s_mm = _sems["s_mm"]; s_pre = _sems["s_pre"]; s_tanh = _sems["s_tanh"]; s_bl = _sems["s_bl"]; s_fl = _sems["s_fl"]; V2 = _sems["V2"]
    import os as _os
    TL = B * TLOC
    NTC = max(1, TL // 512)
    TCW = min(512, TL)
    EH = 2 * D
    NPT = TL // 128
    NG = NPT // 4

    x_i = DIN("x_i", [JD, 128, TL])
    lnwb = DIN("lnwb", [2, 128, JD])
    wuT = DIN("wuT", [D, EH])
    bu_i = DIN("bu", [1, EH])
    wvT = DIN("wvT", [D, EH])
    bv_i = DIN("bv", [1, EH])
    wbT = DIN("wbT", [D, 128])
    bb_i = DIN("bb", [128, 1])
    qaff = DIN("qaff", [128, 4])
    woT = DIN("woT", [EH, D])
    bo_i = DIN("bo", [1, D])
    wrel = DIN("wrel", [1, 1023])
    costab = DIN("costab", [128, TL], BF16)
    sintab = DIN("sintab", [128, TL], BF16)
    maskt = DIN("maskt", [128, 128], BF16)
    aind = DIN("aind", [4, 128], BF16)
    identb = DIN("identb", [128, 128], BF16)
    xout_o = DOUT("xout", [JD, 128, TL])
    xn1_s = DINT("xn1_s", [JD, 128, TL])

    xnb = SB("xnb", [128, JD * TL], BF16)
    wub = SB("wub", [128, JD * EH], BF16)
    wvb = SB("wvb", [128, JD * EH], BF16)
    wob = SB("wob", [128, 16 * D], BF16)
    wtf = SB("wtf", [128, 1024], F32)
    base_b = SB("base_b", [128, TL], BF16)
    qq_b = SB("qq_b", [128, TL], BF16)
    qk_b = SB("qk_b", [128, TL], BF16)
    cs_sb = SB("cs_sb", [128, max(TL, EH)], BF16)
    sn_sb = SB("sn_sb", [128, max(TL, EH)], BF16)
    u_tm = cs_sb
    v_tm = sn_sb
    prod_fm = SB("prod_fm", [128, 16 * 512], BF16)
    mask_sb = SB("mask_sb", [128, 128], BF16)
    aind_sb = SB("aind_sb", [4, 128], BF16)
    b4_sb = SB("b4_sb", [4, 128], BF16)
    idb = SB("idb", [128, 128], BF16)
    w7 = SB("w7", [1, 16], F32)
    w7b = SB("w7b", [1, 128], BF16)
    lnw_sb = SB("lnw_sb", [128, 2 * JD], F32)
    bbp = SB("bbp", [128, 1], F32)
    qaf = SB("qaf", [128, 8], F32)
    brow = SB("brow", [1, 2 * EH + D], BF16)
    onescol = SB("onescol", [128, 1], BF16)
    ones_row = SB("ones_row", [1, TCW], BF16)
    st_f = SB("st_f", [1, 4 * TCW], F32)
    st_b = SB("st_b", [1, 2 * TCW], BF16)
    xb_c = SB("xb_c", [128, TCW], BF16)
    sq_c = SB("sq_c", [128, TCW], BF16)
    t_y = SB("t_y", [128, TCW], F32)
    t_z = SB("t_z", [128, TCW], F32)
    t_w = SB("t_w", [128, TCW], F32)
    sG = SB("sG", [128, 128], F32)
    sGb = SB("sGb", [128, 128], BF16)
    sS = SB("sS", [128, 128], BF16)
    pA = PS("pA", [128, TCW], F32)
    pB = PS("pB", [128, TCW], F32)
    pC = PS("pC", [128, TCW], F32)
    pD = PS("pD", [128, TCW], F32)
    pT = PS("pT", [128, 128], BF16)

    v, s, t, g, sy = nc.vector, nc.scalar, nc.tensor, nc.gpsimd, nc.sync

    for dst, src in [(cs_sb[:, 0:TL], costab), (sn_sb[:, 0:TL], sintab), (mask_sb, maskt),
                     (aind_sb, aind), (idb, identb), (lnw_sb[:, 0:JD], lnwb[0]),
                     (lnw_sb[:, JD:2 * JD], lnwb[1]), (bbp, bb_i), (qaf[:, 0:4], qaff)]:
        sy.dma_start(out=dst if not hasattr(dst, "shape") else dst[:] if hasattr(dst, "ap") else dst,
                     in_=src).then_inc(Ld.h, 16)
        Ld.val += 16
    sy.dma_start(out=w7[0:1, 0:7], in_=wrel[:, 508:515]).then_inc(Ld.h, 16)
    Ld.val += 16
    Ld.wait(v)
    v.tensor_scalar_mul(qaf[:, 4:8], qaf[:, 0:4], 0.125)
    v.drain()
    v.memset(onescol[:], 1.0)
    v.memset(ones_row[:], 1.0)
    for r in range(4):
        for b_ in range(4):
            v.tensor_copy(w7b[0:1, b_:b_ + 125:4],
                          w7[0:1, 3 - r + b_:3 - r + b_ + 1].to_broadcast([1, 32]))
        ei = V.inc(v.drain())
        V.wait(g, ei)
        g.wait_ge(St.h, St.val)
        g.dma_start(out=b4_sb[r:r + 1, :].bitcast(BF16) if False else b4_sb[r:r + 1, :],
                    in_=w7b[0:1, :]).then_inc(St.h, 16)
        St.val += 16
        St.wait(v)
    ei = V.inc(v.drain())
    V.wait(t, ei)

    def load_cast(dst, src, cols, koff=0):
        for k in range(JD if cols != 16 else 16):
            pass
    for k in range(JD):
        for hh in range(2):
            Lw.wait(sy); V.wait(sy)
            sy.dma_start(out=wtf[:], in_=wuT[128 * k:128 * (k + 1),
                                            1024 * hh:1024 * (hh + 1)]).then_inc(Lw.h, 16)
            Lw.val += 16
            Lw.wait(v)
            V.inc(v.tensor_copy(wub[:, k * EH + 1024 * hh: k * EH + 1024 * (hh + 1)], wtf[:]))
    for k in range(JD):
        for hh in range(2):
            Lw.wait(sy); V.wait(sy)
            sy.dma_start(out=wtf[:], in_=wvT[128 * k:128 * (k + 1),
                                            1024 * hh:1024 * (hh + 1)]).then_inc(Lw.h, 16)
            Lw.val += 16
            Lw.wait(v)
            V.inc(v.tensor_copy(wvb[:, k * EH + 1024 * hh: k * EH + 1024 * (hh + 1)], wtf[:]))
    for k in range(16):
        Lw.wait(sy); V.wait(sy)
        sy.dma_start(out=wtf[:], in_=woT[128 * k:128 * (k + 1), :]).then_inc(Lw.h, 16)
        Lw.val += 16
        Lw.wait(v)
        V.inc(v.tensor_copy(wob[:, k * D:(k + 1) * D], wtf[:]))
    wbb = SB("wbb", [128, JD * 128], BF16)
    for k in range(JD):
        Lw.wait(sy); V.wait(sy)
        sy.dma_start(out=wtf[:, 0:128], in_=wbT[128 * k:128 * (k + 1), :]).then_inc(Lw.h, 16)
        Lw.val += 16
        Lw.wait(v)
        V.inc(v.tensor_copy(wbb[:, k * 128:(k + 1) * 128], wtf[:, 0:128]))
    for q, (src, w_) in enumerate([(bu_i, EH), (bv_i, EH), (bo_i, D)]):
        for hh in range(w_ // 1024):
            Lw.wait(sy); V.wait(sy)
            sy.dma_start(out=wtf[0:1, :], in_=src[:, 1024 * hh:1024 * (hh + 1)]).then_inc(Lw.h, 16)
            Lw.val += 16
            Lw.wait(v)
            V.inc(v.tensor_copy(brow[0:1, q * EH + 1024 * hh: q * EH + 1024 * (hh + 1)],
                                wtf[0:1, :]))
    ei = V.inc(v.drain())
    V.wait(t, ei)

    yjs = prod_fm.bitcast(F32)
    for tc in range(NTC):
        c0 = tc * TCW

        def ln_pass(src_getter, eps):
            for j in range(JD):
                xsl = src_getter(j)
                ei = V.inc(v.tensor_copy(xb_c[:], xsl))
                eis = S.inc(s.activation(sq_c[:], xsl, AF.Square))
                V.wait(t, ei); S.wait(t, eis)
                t.matmul(pA[0:1, 0:TCW], onescol[:], xb_c[:], start=(j == 0), stop=(j == JD - 1))
                mm2 = t.matmul(pB[0:1, 0:TCW], onescol[:], sq_c[:], start=(j == 0), stop=(j == JD - 1))
                eit = T.inc(mm2)
                T.wait(v, eit); T.wait(s, eit)
            v.tensor_scalar_mul(st_f[:, 0:TCW], pA[0:1, 0:TCW], 1.0 / D)
            v.drain()
            v.tensor_scalar_mul(st_f[:, TCW:2 * TCW], pB[0:1, 0:TCW], 1.0 / D)
            v.drain()
            v.tensor_mul(st_f[:, 3 * TCW:4 * TCW], st_f[:, 0:TCW], st_f[:, 0:TCW])
            v.drain()
            v.tensor_sub(st_f[:, TCW:2 * TCW], st_f[:, TCW:2 * TCW], st_f[:, 3 * TCW:4 * TCW])
            v.drain()
            v.tensor_scalar_add(st_f[:, TCW:2 * TCW], st_f[:, TCW:2 * TCW], eps)
            ei = V.inc(v.drain())
            V.wait(s, ei)
            ei = S.inc(s.activation(st_f[:, 2 * TCW:3 * TCW], st_f[:, TCW:2 * TCW], AF.Sqrt))
            S.wait(v, ei)
            v.reciprocal(st_f[:, 2 * TCW:3 * TCW], st_f[:, 2 * TCW:3 * TCW])
            v.drain()
            v.tensor_copy(st_b[:, 0:TCW], st_f[:, 0:TCW])
            v.drain()
            ei = V.inc(v.tensor_copy(st_b[:, TCW:2 * TCW], st_f[:, 2 * TCW:3 * TCW]))
            V.wait(t, ei)
            mm1 = t.matmul(pC[:, 0:TCW], ones_row[0:1, 0:128], st_b[:, 0:TCW], start=True, stop=True)
            mm2 = t.matmul(pD[:, 0:TCW], ones_row[0:1, 0:128], st_b[:, TCW:2 * TCW], start=True, stop=True)
            eit = T.inc(mm2)
            T.wait(v, eit)

        V.wait(sy); S.wait(sy); T.wait(sy); Ld.wait(sy)
        for j in range(JD):
            sy.dma_start(out=yjs[:, j * TCW:(j + 1) * TCW],
                         in_=x_i[j][:, c0:c0 + TCW]).then_inc(Ld.h, 16)
        Ld.val += 16 * JD
        Ld.wait(v); Ld.wait(s)
        S.wait(v)
        ln_pass(lambda j: yjs[:, j * TCW:(j + 1) * TCW], 1e-5)
        for j in range(JD):
            v.tensor_sub(t_y[:], yjs[:, j * TCW:(j + 1) * TCW], pC[:, 0:TCW])
            v.drain()
            ei = V.inc(v.tensor_mul(yjs[:, j * TCW:(j + 1) * TCW], t_y[:], pD[:, 0:TCW]))
            v.drain()
            V.wait(g, ei)
            g.wait_ge(St.h, St.val)
            g.dma_start(out=xn1_s[j][:, c0:c0 + TCW],
                        in_=yjs[:, j * TCW:(j + 1) * TCW]).then_inc(St.h, 16)
            St.val += 16
        St.wait(v)
        v.drain()
        V.wait(s); V.wait(t)
        ln_pass(lambda j: yjs[:, j * TCW:(j + 1) * TCW], 1e-5)
        for j in range(JD):
            v.tensor_sub(t_y[:], yjs[:, j * TCW:(j + 1) * TCW], pC[:, 0:TCW])
            v.drain()
            v.tensor_mul(t_z[:], t_y[:], pD[:, 0:TCW])
            v.drain()
            v.tensor_scalar(t_z[:], t_z[:], lnw_sb[:, j:j + 1], lnw_sb[:, JD + j:JD + j + 1],
                            ALU.mult, ALU.add)
            v.drain()
            v.tensor_copy(xnb[:, j * TL + c0: j * TL + c0 + TCW], t_z[:])
            v.drain()
        ei = V.inc(v.drain())
        V.wait(t, ei)

    if _os.environ.get("HDBG") == "1":
        for j in range(JD):
            V.wait(g, V.inc(v.tensor_copy(t_y[:, 0:TCW], xnb[:, j * TL: j * TL + TCW])))
            g.wait_ge(St.h, St.val)
            g.dma_start(out=xout_o[j][:, 0:TCW], in_=t_y[:, 0:TCW]).then_inc(St.h, 16)
            St.val += 16
            St.wait(v)
        St.wait(g)
        if ctx is not None:
            ctx.barrier()
        return nc
    for tc in range(NTC):
        c0 = tc * TCW
        for k in range(JD):
            mm = t.matmul(pA[:, 0:TCW], wbb[:, k * 128:(k + 1) * 128],
                          xnb[:, k * TL + c0: k * TL + c0 + TCW],
                          start=(k == 0), stop=(k == JD - 1))
        eit = T.inc(mm)
        T.wait(s, eit)
        ei = S.inc(s.activation(base_b[:, c0:c0 + TCW], pA[:, 0:TCW], AF.Silu, bias=bbp[:]))
        S.wait(t, ei); S.wait(v, ei)
    for (dst, wq, bq2) in [(qq_b, 4, 5), (qk_b, 6, 7)]:
        v.tensor_scalar(t_y[:, 0:128] if False else dst[:], base_b[:],
                        qaf[:, wq:wq + 1], qaf[:, bq2:bq2 + 1], ALU.mult, ALU.add)
        v.drain()
    for (dst,) in [(qq_b,), (qk_b,)]:
        for tc in range(NTC):
            c0 = tc * TCW
            v.tensor_copy(t_w[0:64, 0:TCW], dst[64:128, c0:c0 + TCW])
            v.drain()
            v.tensor_copy(t_w[64:128, 0:TCW], dst[0:64, c0:c0 + TCW])
            v.drain()
            v.tensor_mul(t_y[0:64, 0:TCW], dst[0:64, c0:c0 + TCW], cs_sb[0:64, c0:c0 + TCW])
            v.drain()
            v.tensor_mul(t_z[0:64, 0:TCW], t_w[0:64, 0:TCW], sn_sb[0:64, c0:c0 + TCW])
            v.drain()
            v.tensor_mul(t_y[64:128, 0:TCW], dst[64:128, c0:c0 + TCW], cs_sb[64:128, c0:c0 + TCW])
            v.drain()
            v.tensor_mul(t_z[64:128, 0:TCW], t_w[64:128, 0:TCW], sn_sb[64:128, c0:c0 + TCW])
            v.drain()
            v.tensor_sub(dst[0:64, c0:c0 + TCW], t_y[0:64, 0:TCW], t_z[0:64, 0:TCW])
            v.drain()
            v.tensor_add(dst[64:128, c0:c0 + TCW], t_y[64:128, 0:TCW], t_z[64:128, 0:TCW])
            v.drain()
    ei = V.inc(v.drain())
    V.wait(t, ei)

    if _os.environ.get("HDBG") == "2":
        for (nm, buf) in [(0, base_b), (1, qq_b), (2, qk_b)]:
            V.wait(g, V.inc(v.tensor_copy(t_y[:, 0:TCW], buf[:, 0:TCW])))
            g.wait_ge(St.h, St.val)
            g.dma_start(out=xout_o[nm][:, 0:TCW], in_=t_y[:, 0:TCW]).then_inc(St.h, 16)
            St.val += 16
            St.wait(v)
        St.wait(g)
        if ctx is not None:
            ctx.barrier()
        return nc
    for tg in range(NG):
        for ti in range(4):
            pt0 = (tg * 4 + ti) * 128
            for (dstb, wsrc, bq) in [(u_tm, wub, 0), (v_tm, wvb, 1)]:
                for ec in range(4):
                    for k in range(JD):
                        t.matmul(pA[:, 0:TCW],
                                 xnb[:, k * TL + pt0: k * TL + pt0 + 128],
                                 wsrc[:, k * EH + 512 * ec: k * EH + 512 * (ec + 1)],
                                 start=(k == 0), stop=False)
                    mm = t.matmul(pA[:, 0:TCW], ones_row[0:1, 0:128],
                                  brow[0:1, bq * EH + 512 * ec: bq * EH + 512 * (ec + 1)],
                                  start=False, stop=True)
                    eit = T.inc(mm)
                    T.wait(s, eit)
                    ei = S.inc(s.activation(dstb[:, 512 * ec:512 * (ec + 1)], pA[:, 0:TCW], AF.Silu))
                    S.wait(t, ei)
            t.matmul(pA[:, 0:128], qq_b[:, pt0:pt0 + 128], qk_b[:, pt0:pt0 + 128],
                     start=True, stop=False)
            mm = t.matmul(pA[:, 0:128], aind_sb[0:4, :], b4_sb[0:4, :], start=False, stop=True)
            eit = T.inc(mm)
            T.wait(v, eit)
            v.tensor_relu(sG[:], pA[:, 0:128])
            ei = V.inc(v.drain())
            V.wait(s, ei); V.wait(t, ei)
            ei = S.inc(s.activation(sG[:], sG[:], AF.Square))
            S.wait(v, ei)
            ei = V.inc(v.tensor_mul(sGb[:], sG[:], mask_sb[:]))
            V.wait(t, ei)
            mm = t.transpose(pT[:, 0:128], sGb[:], idb[:])
            eit = T.inc(mm)
            T.wait(v, eit)
            ei = V.inc(v.tensor_copy(sS[:], pT[:, 0:128]))
            V.wait(t, ei)
            for ec in range(4):
                mm = t.matmul(pB[:, 0:TCW], sS[:], v_tm[:, 512 * ec:512 * (ec + 1)],
                              start=True, stop=True)
                eit = T.inc(mm)
                T.wait(v, eit)
                ei = V.inc(v.tensor_mul(xb_c[:], u_tm[:, 512 * ec:512 * (ec + 1)], pB[:, 0:TCW]))
                V.wait(t, ei)
                for q4 in range(4):
                    mm = t.transpose(pT[:, 0:128], xb_c[:, 128 * q4:128 * (q4 + 1)], idb[:])
                    eit = T.inc(mm)
                    T.wait(v, eit)
                    ei = V.inc(v.tensor_copy(
                        prod_fm[:, (ec * 4 + q4) * 512 + ti * 128:(ec * 4 + q4) * 512 + ti * 128 + 128],
                        pT[:, 0:128]))
                    V.wait(t, ei)
                    V.wait(s, ei) if False else None
        if _os.environ.get("HDBG") == "3" and tg == 0:
            for (nm, buf, wd) in [(0, u_tm, EH), (1, v_tm, EH), (2, sS, 128), (3, prod_fm, 512)]:
                wd2 = min(wd, TCW)
                V.wait(g, V.inc(v.tensor_copy(t_y[:, 0:wd2], buf[:, 0:wd2])))
                g.wait_ge(St.h, St.val)
                g.dma_start(out=xout_o[nm][:, 0:wd2], in_=t_y[:, 0:wd2]).then_inc(St.h, 16)
                St.val += 16
                St.wait(v)
            St.wait(g)
            if ctx is not None:
                ctx.barrier()
            return nc
        gc0 = tg * 512
        for m8 in range(JD):
            for ke in range(16):
                t.matmul(pC[:, 0:TCW], wob[:, ke * D + 128 * m8: ke * D + 128 * (m8 + 1)],
                         prod_fm[:, ke * 512:(ke + 1) * 512],
                         start=(ke == 0), stop=False)
            mm = t.matmul(pC[:, 0:TCW], ones_row[0:1, 0:TCW] if False else brow[0:1, 2 * EH + 128 * m8: 2 * EH + 128 * (m8 + 1)],
                          ones_row[0:1, 0:TCW], start=False, stop=True)
            eit = T.inc(mm)
            T.wait(sy, eit)
            V.wait(sy)
            sy.dma_start(out=t_w[:], in_=x_i[m8][:, gc0:gc0 + TCW]).then_inc(Ld.h, 16)
            Ld.val += 16
            sy.dma_start(out=t_z[:], in_=xn1_s[m8][:, gc0:gc0 + TCW]).then_inc(Ld.h, 16)
            Ld.val += 16
            T.wait(v, eit)
            Ld.wait(v)
            v.tensor_add(t_y[:], pC[:, 0:TCW], t_w[:])
            v.drain()
            ei = V.inc(v.tensor_add(t_y[:], t_y[:], t_z[:]))
            V.wait(t, ei)
            V.wait(g, ei)
            g.wait_ge(St.h, St.val)
            g.dma_start(out=xout_o[m8][:, gc0:gc0 + TCW], in_=t_y[:]).then_inc(St.h, 16)
            St.val += 16
            St.wait(v)
    St.wait(g)
    if ctx is not None:
        ctx.barrier()
    return nc



_PROGS = {}


def _get_progs(TLOC, only=None):
    names = only or ["A", "B", "C", "D", "E", "F", "H", "I", "J"]
    builders = {"A": lambda: build_prog_A(TLOC), "B": lambda: build_prog_B(TLOC),
                "C": lambda: build_prog_C(TLOC), "D": lambda: build_prog_D(TLOC),
                "E": lambda: build_prog_E(NCORES * TLOC, NCORES),
                "F": lambda: build_prog_F(TLOC), "H": lambda: build_prog_H(TLOC),
                "I": lambda: build_prog_I(TLOC), "J": lambda: build_prog_J(TLOC)}
    for n in names:
        if n not in _PROGS:
            _PROGS[n] = builders[n]()
    return _PROGS


def _from_fm(xo, TLOC):
    return np.asarray(xo).reshape(D, TLOC, B).transpose(2, 1, 0)


def kernel_unfused(**inp):
    TLOC = 512
    TL = B * TLOC
    P = _get_progs(TLOC)
    run = lambda nc_, ins: run_bass_kernel_spmd(nc_, ins, core_ids=list(range(NCORES))).results

    vec = lambda a: np.ascontiguousarray(np.asarray(a, np.float32).reshape(-1))
    x0 = np.asarray(inp["L0_attn_a_x"], np.float32)

    mixes = [vec(inp[f"L0_attn_a_x_{m}"]) for m in "rwkvag"]
    wlist = [np.asarray(inp[f"L0_attn_a_W_{m}"], np.float32) for m in "rwkvag"]
    insA = [host_prep_A(x0, mixes, wlist, c, TLOC) for c in range(NCORES)]
    outA = run(P["A"], insA)
    insB = [host_prep_B(outA, vec(inp["L0_attn_a_gn_w"]), vec(inp["L0_attn_a_gn_b"]),
                        np.asarray(inp["L0_attn_a_W_out"], np.float32),
                        vec(inp["L0_attn_a_b_out"]), c, TLOC) for c in range(NCORES)]
    outB = run(P["B"], insB)
    X = [np.asarray(o["xout"]) for o in outB]

    statC = host_prep_C(None, np.asarray(inp["L0_ffn_gate_w"], np.float32),
                        vec(inp["L0_ffn_gate_b"]),
                        np.asarray(inp["L0_ffn_w1"], np.float32),
                        np.asarray(inp["L0_ffn_b1"], np.float32),
                        np.asarray(inp["L0_ffn_w2"], np.float32),
                        np.asarray(inp["L0_ffn_b2"], np.float32), TLOC)
    insC = []
    for c in range(NCORES):
        d = dict(statC); d["x_i"] = X[c]; insC.append(d)
    outC = run(P["C"], insC)
    X = [np.asarray(o["xout"]) for o in outC]

    insD = [{
        "x_i": X[c],
        "winT": np.ascontiguousarray(np.asarray(inp["L0_attn_b_W_in"], np.float32).T),
        "b_in": vec(inp["L0_attn_b_b_in"]).reshape(1, -1),
        "b_h": vec(inp["L0_attn_b_b_h"]).reshape(1, -1),
        "lnwb": np.stack([vec(inp["L0_attn_b_ln_w"]).reshape(JD, 128).T,
                          vec(inp["L0_attn_b_ln_b"]).reshape(JD, 128).T]).astype(np.float32),
    } for c in range(NCORES)]
    outD = run(P["D"], insD)
    g_all = np.stack([np.asarray(o["g_o"]) for o in outD])
    vb_all = np.stack([np.asarray(o["vb_o"]) for o in outD])
    insE = {"g_i": g_all, "vb_i": vb_all,
            "whT": np.ascontiguousarray(np.asarray(inp["L0_attn_b_W_h"], np.float32).T)}
    outE = run(P["E"], [insE] * NCORES)
    hs_full = np.asarray(outE[0]["hs_o"])
    insF = [{
        "x_i": X[c], "xn2_i": np.asarray(outD[c]["xn2_o"]),
        "hs_i": np.ascontiguousarray(hs_full[:, :, c * TL:(c + 1) * TL]),
        "woutT": np.ascontiguousarray(np.asarray(inp["L0_attn_b_W_out"], np.float32).T),
        "bout": vec(inp["L0_attn_b_b_out"]).reshape(1, -1),
    } for c in range(NCORES)]
    outF = run(P["F"], insF)
    X = [np.asarray(o["xout"]) for o in outF]

    insC2 = []
    for c in range(NCORES):
        d = dict(statC); d["x_i"] = X[c]; insC2.append(d)
    outC2 = run(P["C"], insC2)
    X = [np.asarray(o["xout"]) for o in outC2]

    invf = 10000.0 ** (-np.arange(64, dtype=np.float32) / 64)
    pos_col = np.tile(np.arange(B, dtype=np.float32), TLOC)
    sinus = pos_col[None, :] * np.concatenate([invf, invf])[:, None]
    costab = np.cos(sinus).astype(BF)
    sintab = np.sin(sinus).astype(BF)
    maskt = np.zeros((128, 128), BF)
    for p_ in range(128):
        for q_ in range(128):
            if p_ // 4 == q_ // 4 and q_ % 4 <= p_ % 4:
                maskt[p_, q_] = 1
    aind = np.zeros((4, 128), BF)
    for r_ in range(4):
        aind[r_, r_::4] = 1
    statH = {
        "lnwb": np.stack([vec(inp["L1_attn_a_ln_g"]).reshape(JD, 128).T,
                          vec(inp["L1_attn_a_ln_b"]).reshape(JD, 128).T]).astype(np.float32),
        "wuT": np.ascontiguousarray(np.asarray(inp["L1_attn_a_Wu"], np.float32).T),
        "bu": vec(inp["L1_attn_a_bu"]).reshape(1, -1),
        "wvT": np.ascontiguousarray(np.asarray(inp["L1_attn_a_Wv"], np.float32).T),
        "bv": vec(inp["L1_attn_a_bv"]).reshape(1, -1),
        "wbT": np.ascontiguousarray(np.asarray(inp["L1_attn_a_Wb"], np.float32).T),
        "bb": vec(inp["L1_attn_a_bb"]).reshape(128, 1),
        "qaff": np.stack([vec(inp["L1_attn_a_qqw"]), vec(inp["L1_attn_a_qqb"]),
                          vec(inp["L1_attn_a_qkw"]), vec(inp["L1_attn_a_qkb"])],
                         axis=1).astype(np.float32),
        "woT": np.ascontiguousarray(np.asarray(inp["L1_attn_a_Wo"], np.float32).T),
        "bo": vec(inp["L1_attn_a_bo"]).reshape(1, -1),
        "wrel": vec(inp["L1_attn_a_w_rel"]).reshape(1, -1),
        "costab": costab, "sintab": sintab, "maskt": maskt, "aind": aind,
        "identb": np.eye(128, dtype=BF),
    }
    insH = []
    for c in range(NCORES):
        d = dict(statH); d["x_i"] = X[c]; insH.append(d)
    outH = run(P["H"], insH)
    X = [np.asarray(o["xout"]) for o in outH]

    statI = {
        "gw": np.asarray(inp["L1_ffn_gw"], np.float32),
        "gb": vec(inp["L1_ffn_gb"]).reshape(1, -1),
        "w1": np.asarray(inp["L1_ffn_w1"], np.float32),
        "b1": np.asarray(inp["L1_ffn_b1"], np.float32),
        "w3": np.asarray(inp["L1_ffn_w3"], np.float32),
        "b3": np.asarray(inp["L1_ffn_b3"], np.float32),
        "w2": np.asarray(inp["L1_ffn_w2"], np.float32),
        "b2": np.asarray(inp["L1_ffn_b2"], np.float32),
        "sw1": np.asarray(inp["L1_ffn_sw1"], np.float32),
        "sb1": vec(inp["L1_ffn_sb1"]).reshape(1, -1),
        "sw3": np.asarray(inp["L1_ffn_sw3"], np.float32),
        "sb3": vec(inp["L1_ffn_sb3"]).reshape(1, -1),
        "sw2": np.asarray(inp["L1_ffn_sw2"], np.float32),
        "sb2": vec(inp["L1_ffn_sb2"]).reshape(1, -1),
        "identf": np.eye(128, dtype=np.float32),
    }
    insI = []
    for c in range(NCORES):
        d = dict(statI); d["x_i"] = X[c]; insI.append(d)
    outI = run(P["I"], insI)
    X = [np.asarray(o["xout"]) for o in outI]

    statJ = {
        "Wk": np.asarray(inp["L1_attn_b_Wk"], np.float32), "bk": vec(inp["L1_attn_b_bk"]).reshape(1, -1),
        "Wv": np.asarray(inp["L1_attn_b_Wv"], np.float32), "bv": vec(inp["L1_attn_b_bv"]).reshape(1, -1),
        "Wq": np.asarray(inp["L1_attn_b_Wq"], np.float32), "bq": vec(inp["L1_attn_b_bq"]).reshape(1, -1),
        "Wkp": np.asarray(inp["L1_attn_b_Wkp"], np.float32), "bkp": vec(inp["L1_attn_b_bkp"]).reshape(1, -1),
        "Wqp": np.asarray(inp["L1_attn_b_Wqp"], np.float32), "bqp": vec(inp["L1_attn_b_bqp"]).reshape(1, -1),
        "Wo": np.asarray(inp["L1_attn_b_Wo"], np.float32), "bo": vec(inp["L1_attn_b_bo"]).reshape(1, -1),
        "phs": np.ascontiguousarray(vec(inp["L1_attn_b_phase_scale"]).reshape(JD, 128).T),
        "lnwb": np.stack([vec(inp["L1_attn_b_ln_g"]).reshape(JD, 128).T,
                          vec(inp["L1_attn_b_ln_b"]).reshape(JD, 128).T]).astype(np.float32),
    }
    insJ = []
    for c in range(NCORES):
        d = dict(statJ); d["x_i"] = X[c]; insJ.append(d)
    outJ = run(P["J"], insJ)
    X = [np.asarray(o["xout"]) for o in outJ]

    insI2 = []
    for c in range(NCORES):
        d = dict(statI); d["x_i"] = X[c]; insI2.append(d)
    outI2 = run(P["I"], insI2)
    X = [np.asarray(o["xout"]) for o in outI2]

    out = np.empty((B, NCORES * TLOC, D), np.float32)
    for c in range(NCORES):
        out[:, c * TLOC:(c + 1) * TLOC, :] = _from_fm(X[c], TLOC)
    return out



class Ctx:
    def __init__(self, nc, ncores):
        self.nc = nc
        self.ncores = ncores
        self.dram = {}
        self._live = []
        self._ctr = 0
        self.sems = {n: Sem(nc, n) for n in
                     ["Ld", "Lw", "St", "St1", "Sq", "Ld2", "Lp", "T", "V", "S",
                      "s_mm", "s_pre", "s_tanh", "s_bl", "s_fl", "V2", "CC", "BAR"]}
        self.pA = ps(nc, "pA", [128, 512], F32)
        self.pB = ps(nc, "pB", [128, 512], F32)
        self.pC = ps(nc, "pC", [128, 512], F32)
        self.pD = ps(nc, "pD", [128, 512], F32)
        self.pT = ps(nc, "pTT", [128, 128], BF16)
        self.pz = ps(nc, "pz", [128, 32], F32)

    def reset(self):
        for cm in reversed(self._live):
            cm.__exit__(None, None, None)
        self._live = []

    def psum_by_spec(self, spec):
        spec = spec.replace("ctx.", "")
        if "[" in spec:
            nm, rest = spec.split("[", 1)
            w = int(rest.split("0:")[1].split("]")[0])
            return getattr(self, nm)[:, 0:w]
        return getattr(self, spec)

    def al(self, shape, dtype):
        self._ctr += 1
        cm = self.nc.sbuf_tensor(f"ar{self._ctr}", shape, dtype)
        h = cm.__enter__()
        self._live.append(cm)
        return h

    def din(self, name, shape, dtype=F32):
        if name not in self.dram:
            self.dram[name] = self.nc.dram_tensor(name, shape, dtype,
                                                  kind="ExternalInput").ap()
        return self.dram[name]

    def dint(self, name, shape, dtype=F32):
        if name not in self.dram:
            self.dram[name] = self.nc.dram_tensor(name, shape, dtype,
                                                  kind="Internal").ap()
        return self.dram[name]

    def dout(self, name, shape, dtype=F32):
        if name not in self.dram:
            self.dram[name] = self.nc.dram_tensor(name, shape, dtype,
                                                  kind="ExternalOutput").ap()
        return self.dram[name]

    def barrier(self):
        nc = self.nc
        B_ = self.sems["BAR"]
        base = B_.val
        engs = [nc.tensor, nc.vector, nc.scalar, nc.gpsimd, nc.sync]
        for e in engs:
            e.drain().then_inc(B_.h, 1)
        B_.val += 5
        for e in engs:
            e.wait_ge(B_.h, base + 5)



def build_fused(TLOC, NC_=NCORES):
    TL = B * TLOC
    TG_ = NC_ * TLOC
    nc = bass.Bass("TRN2", target_bir_lowering=False, debug=False)
    ctx = Ctx(nc, NC_)
    v, s, t, g, sy = nc.vector, nc.scalar, nc.tensor, nc.gpsimd, nc.sync
    SE = ctx.sems
    rg = [list(range(NC_))]

    xbufA = ctx.dint("xbufA", [JD, 128, TL])
    xbufB = ctx.dint("xbufB", [JD, 128, TL])
    rg_s = ctx.dint("rg_s", [2, JD, 128, TL], BF16)
    sp_s = ctx.dint("sp_s", [2, JD, 128, TL])
    cst_in = ctx.dint("cst_in", [128, 2 * JD * B])
    cst_out = ctx.dint("cst_out", [NC_ * 128, 2 * JD * B])
    pref_s = ctx.dint("pref_s", [7, 2, 128, JD * B])
    g_cin = ctx.dint("g_cin", [JD * 128, TL], BF16)
    vb_cin = ctx.dint("vb_cin", [JD * 128, TL], BF16)
    g_cout = ctx.dint("g_cout", [NC_ * JD * 128, TL], BF16)
    vb_cout = ctx.dint("vb_cout", [NC_ * JD * 128, TL], BF16)
    hs_s = ctx.dint("hs_s", [JD, 128, B * TG_], BF16)
    xn2_s = ctx.dint("xn2_s", [JD, 128, TL])
    rmask = ctx.din("FU_rmask", [128, 16])
    xout_final = ctx.dout("xout", [JD, 128, TL])

    build_prog_A(TLOC, ctx=ctx, xaps={"xfm": xbufA, "rg": rg_s, "sp": sp_s})

    ctx.reset()
    cstb = ctx.al([128, 2 * JD * B], F32)
    tmp1 = ctx.al([128, JD * B], F32)
    tmp2 = ctx.al([128, JD * B], F32)
    rm_sb = ctx.al([128, 16], F32)
    Ld = SE["Ld"]; St = SE["St"]; V = SE["V"]; CC = SE["CC"]
    sy.dma_start(out=rm_sb[:, :], in_=rmask).then_inc(Ld.h, 16)
    Ld.val += 16
    for q in range(2):
        for j in range(JD):
            sy.dma_start(out=cstb[:, q * JD * B + j * B: q * JD * B + (j + 1) * B],
                         in_=sp_s[1 - q, j][:, TL - B:TL]).then_inc(Ld.h, 16)
            Ld.val += 16
    Ld.wait(g)
    g.dma_start(out=cst_in[:, :], in_=cstb[:, :]).then_inc(St.h, 16)
    St.val += 16
    g.wait_ge(St.h, St.val)
    g.collective_compute("AllGather", ALU.bypass, replica_groups=rg,
                         ins=[cst_in[:, :]], outs=[cst_out[:, :]]).then_inc(CC.h, 16)
    CC.val += 16
    CC.wait(sy)
    for k in range(7):
        V.wait(sy)
        St.wait(v)
        if k < NC_ - 1:
            sy.dma_start(out=cstb[:, :],
                         in_=cst_out[128 * k:128 * (k + 1), :]).then_inc(Ld.h, 16)
            Ld.val += 16
            Ld.wait(v)
            v.tensor_scalar(tmp1[:, :], cstb[:, 0:JD * B], rm_sb[:, k:k + 1],
                            rm_sb[:, 8 + k:8 + k + 1], ALU.mult, ALU.add)
            v.drain()
            v.tensor_scalar(tmp2[:, :], cstb[:, JD * B:2 * JD * B], rm_sb[:, k:k + 1],
                            None, ALU.mult)
        else:
            v.memset(tmp1[:, :], 1.0)
            v.drain()
            v.memset(tmp2[:, :], 0.0)
        ei = V.inc(v.drain())
        V.wait(g, ei)
        g.wait_ge(St.h, St.val)
        g.dma_start(out=pref_s[k, 0], in_=tmp1[:, :]).then_inc(St.h, 16)
        g.dma_start(out=pref_s[k, 1], in_=tmp2[:, :]).then_inc(St.h, 16)
        St.val += 32
    g.wait_ge(St.h, St.val)
    ctx.barrier()

    build_prog_B(TLOC, ctx=ctx, x_out=xbufB,
                 xaps={"xfm_i": xbufA, "rg_i": rg_s, "sp_i": sp_s, "pref": pref_s})

    build_prog_C(TLOC, ctx=ctx, x_in=xbufB, x_out=xbufA)

    build_prog_D(TLOC, ctx=ctx, x_in=xbufA,
                 xaps={"g_o": g_cin.rearrange("(j p) t -> j p t", p=128),
                       "vb_o": vb_cin.rearrange("(j p) t -> j p t", p=128),
                       "xn2_o": xn2_s})

    ctx.barrier()
    CC = SE["CC"]
    g.collective_compute("AllGather", ALU.bypass, replica_groups=rg,
                         ins=[g_cin[:, :]], outs=[g_cout[:, :]]).then_inc(CC.h, 16)
    CC.val += 16
    g.collective_compute("AllGather", ALU.bypass, replica_groups=rg,
                         ins=[vb_cin[:, :]], outs=[vb_cout[:, :]]).then_inc(CC.h, 16)
    CC.val += 16
    for e_ in [nc.tensor, nc.vector, nc.scalar, nc.sync, nc.gpsimd]:
        e_.wait_ge(CC.h, CC.val)

    build_prog_E(TG_, NC_, ctx=ctx,
                 xaps={"g_i": g_cout.rearrange("(r j p) t -> r j p t", j=JD, p=128),
                       "vb_i": vb_cout.rearrange("(r j p) t -> r j p t", j=JD, p=128),
                       "hs_o": hs_s})

    ctx.reset()
    hs_loc = ctx.dint("hs_loc", [JD, 128, TL], BF16)
    cmask = ctx.din("FU_cmask", [128, NC_])
    cm_sb = ctx.al([128, NC_], F32)
    acc_f = ctx.al([128, TL], F32)
    tmp_b = ctx.al([128, TL], BF16)
    Ld = SE["Ld"]; St = SE["St"]; V = SE["V"]
    sy.dma_start(out=cm_sb[:, :], in_=cmask).then_inc(Ld.h, 16)
    Ld.val += 16
    for j in range(JD):
        V.wait(sy)
        St.wait(sy)
        v.memset(acc_f[:, :], 0.0)
        v.drain()
        for r in range(NC_):
            Ld.wait(sy)
            sy.dma_start(out=tmp_b[:, :],
                         in_=hs_s[j][:, r * TL:(r + 1) * TL]).then_inc(Ld.h, 16)
            Ld.val += 16
            Ld.wait(v)
            v.scalar_tensor_tensor(acc_f[:, :], tmp_b[:, :], cm_sb[:, r:r + 1],
                                   acc_f[:, :], ALU.mult, ALU.add)
            ei = V.inc(v.drain())
            V.wait(sy, ei)
        v.tensor_copy(tmp_b[:, :], acc_f[:, :])
        ei = V.inc(v.drain())
        V.wait(g, ei)
        g.wait_ge(St.h, St.val)
        g.dma_start(out=hs_loc[j], in_=tmp_b[:, :]).then_inc(St.h, 16)
        St.val += 16
        St.wait(v)
    ctx.barrier()
    build_prog_F(TLOC, ctx=ctx, x_in=xbufA, x_out=xbufB,
                 xaps={"xn2_i": xn2_s, "hs_i": hs_loc})

    build_prog_C(TLOC, ctx=ctx, x_in=xbufB, x_out=xbufA)

    build_prog_H(TLOC, ctx=ctx, x_in=xbufA, x_out=xbufB)

    build_prog_I(TLOC, ctx=ctx, x_in=xbufB, x_out=xbufA)

    build_prog_J(TLOC, ctx=ctx, x_in=xbufA, x_out=xbufB)

    build_prog_I(TLOC, ctx=ctx, x_in=xbufB, x_out=xout_final)
    return nc


def _mk_common(inp, TLOC):
    TL = B * TLOC
    vec = lambda a: np.ascontiguousarray(np.asarray(a, np.float32).reshape(-1))
    mixes = [vec(inp[f"L0_attn_a_x_{m}"]) for m in "rwkvag"]
    wlist = [np.asarray(inp[f"L0_attn_a_W_{m}"], np.float32) for m in "rwkvag"]
    invf = 10000.0 ** (-np.arange(64, dtype=np.float32) / 64)
    pos_col = np.tile(np.arange(B, dtype=np.float32), TLOC)
    sinus = pos_col[None, :] * np.concatenate([invf, invf])[:, None]
    maskt = np.zeros((128, 128), BF)
    for p_ in range(128):
        for q_ in range(128):
            if p_ // 4 == q_ // 4 and q_ % 4 <= p_ % 4:
                maskt[p_, q_] = 1
    aind = np.zeros((4, 128), BF)
    for r_ in range(4):
        aind[r_, r_::4] = 1
    gsel = np.zeros((128, 2), BF)
    gsel[0:64, 0] = 1; gsel[64:128, 1] = 1
    sel16 = np.zeros((JD, 16, 128), BF)
    for j in range(JD):
        sel16[j, 2 * j, 0:64] = 1
        sel16[j, 2 * j + 1, 64:128] = 1

    common = {
        "A_mixes": np.stack([m.reshape(JD, 128).T for m in mixes]).astype(np.float32),
        "A_wts": np.stack([np.ascontiguousarray(w.T) for w in wlist]).astype(np.float32),
        "A_identb": np.eye(128, dtype=BF),
        "A_identf": np.eye(128, dtype=np.float32),
        "B_gnwb": np.stack([vec(inp["L0_attn_a_gn_w"]).reshape(JD, 128).T,
                            vec(inp["L0_attn_a_gn_b"]).reshape(JD, 128).T]).astype(np.float32),
        "B_woutT": np.ascontiguousarray(np.asarray(inp["L0_attn_a_W_out"], np.float32).T),
        "B_bout": vec(inp["L0_attn_a_b_out"]).reshape(1, D),
        "B_gsel": gsel, "B_sel16": sel16,
        "B_onesr": np.ones((1, TL), np.float32),
        "C_gate_w": np.ascontiguousarray(np.asarray(inp["L0_ffn_gate_w"], np.float32)),
        "C_gate_b": vec(inp["L0_ffn_gate_b"]).reshape(1, -1),
        "C_w1": np.ascontiguousarray(np.asarray(inp["L0_ffn_w1"], np.float32)),
        "C_b1": np.ascontiguousarray(np.asarray(inp["L0_ffn_b1"], np.float32).reshape(8, 1, -1)),
        "C_w2": np.ascontiguousarray(np.asarray(inp["L0_ffn_w2"], np.float32)),
        "C_b2": np.ascontiguousarray(np.asarray(inp["L0_ffn_b2"], np.float32).reshape(8, 1, -1)),
        "C_identf": np.eye(128, dtype=np.float32),
        "D_winT": np.ascontiguousarray(np.asarray(inp["L0_attn_b_W_in"], np.float32).T),
        "D_b_in": vec(inp["L0_attn_b_b_in"]).reshape(1, -1),
        "D_b_h": vec(inp["L0_attn_b_b_h"]).reshape(1, -1),
        "D_lnwb": np.stack([vec(inp["L0_attn_b_ln_w"]).reshape(JD, 128).T,
                            vec(inp["L0_attn_b_ln_b"]).reshape(JD, 128).T]).astype(np.float32),
        "E_whT": np.ascontiguousarray(np.asarray(inp["L0_attn_b_W_h"], np.float32).T),
        "F_woutT": np.ascontiguousarray(np.asarray(inp["L0_attn_b_W_out"], np.float32).T),
        "F_bout": vec(inp["L0_attn_b_b_out"]).reshape(1, D),
        "H_lnwb": np.stack([vec(inp["L1_attn_a_ln_g"]).reshape(JD, 128).T,
                            vec(inp["L1_attn_a_ln_b"]).reshape(JD, 128).T]).astype(np.float32),
        "H_wuT": np.ascontiguousarray(np.asarray(inp["L1_attn_a_Wu"], np.float32).T),
        "H_bu": vec(inp["L1_attn_a_bu"]).reshape(1, -1),
        "H_wvT": np.ascontiguousarray(np.asarray(inp["L1_attn_a_Wv"], np.float32).T),
        "H_bv": vec(inp["L1_attn_a_bv"]).reshape(1, -1),
        "H_wbT": np.ascontiguousarray(np.asarray(inp["L1_attn_a_Wb"], np.float32).T),
        "H_bb": vec(inp["L1_attn_a_bb"]).reshape(128, 1),
        "H_qaff": np.stack([vec(inp["L1_attn_a_qqw"]), vec(inp["L1_attn_a_qqb"]),
                            vec(inp["L1_attn_a_qkw"]), vec(inp["L1_attn_a_qkb"])],
                           axis=1).astype(np.float32),
        "H_woT": np.ascontiguousarray(np.asarray(inp["L1_attn_a_Wo"], np.float32).T),
        "H_bo": vec(inp["L1_attn_a_bo"]).reshape(1, -1),
        "H_wrel": vec(inp["L1_attn_a_w_rel"]).reshape(1, -1),
        "H_costab": np.cos(sinus).astype(BF), "H_sintab": np.sin(sinus).astype(BF),
        "H_maskt": maskt, "H_aind": aind, "H_identb": np.eye(128, dtype=BF),
        "I_gw": np.ascontiguousarray(np.asarray(inp["L1_ffn_gw"], np.float32)),
        "I_gb": vec(inp["L1_ffn_gb"]).reshape(1, -1),
        "I_w1": np.ascontiguousarray(np.asarray(inp["L1_ffn_w1"], np.float32)),
        "I_b1": np.ascontiguousarray(np.asarray(inp["L1_ffn_b1"], np.float32)),
        "I_w3": np.ascontiguousarray(np.asarray(inp["L1_ffn_w3"], np.float32)),
        "I_b3": np.ascontiguousarray(np.asarray(inp["L1_ffn_b3"], np.float32)),
        "I_w2": np.ascontiguousarray(np.asarray(inp["L1_ffn_w2"], np.float32)),
        "I_b2": np.ascontiguousarray(np.asarray(inp["L1_ffn_b2"], np.float32)),
        "I_sw1": np.ascontiguousarray(np.asarray(inp["L1_ffn_sw1"], np.float32)),
        "I_sb1": vec(inp["L1_ffn_sb1"]).reshape(1, -1),
        "I_sw3": np.ascontiguousarray(np.asarray(inp["L1_ffn_sw3"], np.float32)),
        "I_sb3": vec(inp["L1_ffn_sb3"]).reshape(1, -1),
        "I_sw2": np.ascontiguousarray(np.asarray(inp["L1_ffn_sw2"], np.float32)),
        "I_sb2": vec(inp["L1_ffn_sb2"]).reshape(1, -1),
        "I_identf": np.eye(128, dtype=np.float32),
        "J_Wk": np.ascontiguousarray(np.asarray(inp["L1_attn_b_Wk"], np.float32)),
        "J_bk": vec(inp["L1_attn_b_bk"]).reshape(1, -1),
        "J_Wv": np.ascontiguousarray(np.asarray(inp["L1_attn_b_Wv"], np.float32)),
        "J_bv": vec(inp["L1_attn_b_bv"]).reshape(1, -1),
        "J_Wq": np.ascontiguousarray(np.asarray(inp["L1_attn_b_Wq"], np.float32)),
        "J_bq": vec(inp["L1_attn_b_bq"]).reshape(1, -1),
        "J_Wkp": np.ascontiguousarray(np.asarray(inp["L1_attn_b_Wkp"], np.float32)),
        "J_bkp": vec(inp["L1_attn_b_bkp"]).reshape(1, -1),
        "J_Wqp": np.ascontiguousarray(np.asarray(inp["L1_attn_b_Wqp"], np.float32)),
        "J_bqp": vec(inp["L1_attn_b_bqp"]).reshape(1, -1),
        "J_Wo": np.ascontiguousarray(np.asarray(inp["L1_attn_b_Wo"], np.float32)),
        "J_bo": vec(inp["L1_attn_b_bo"]).reshape(1, -1),
        "J_phs": np.ascontiguousarray(vec(inp["L1_attn_b_phase_scale"]).reshape(JD, 128).T),
        "J_lnwb": np.stack([vec(inp["L1_attn_b_ln_g"]).reshape(JD, 128).T,
                            vec(inp["L1_attn_b_ln_b"]).reshape(JD, 128).T]).astype(np.float32),
    }
    return common, mixes, wlist


def kernel_fused(**inp):
    TLOC = 512
    TL = B * TLOC
    if "FUSED" not in _PROGS:
        _PROGS["FUSED"] = build_fused(TLOC, NCORES)
    ncF = _PROGS["FUSED"]
    vec = lambda a: np.ascontiguousarray(np.asarray(a, np.float32).reshape(-1))
    x0 = np.asarray(inp["L0_attn_a_x"], np.float32)

    common, mixes, wlist = _mk_common(inp, TLOC)
    in_maps = []
    for c in range(NCORES):
        d = dict(common)
        ax = host_prep_A(x0, mixes, wlist, c, TLOC)
        d["A_xa"] = ax["xa"]
        rm = np.zeros((128, 16), np.float32)
        rm[:, 0:c] = 1.0
        rm[:, 8:16] = 1.0 - rm[:, 0:8]
        d["FU_rmask"] = rm
        cm = np.zeros((128, NCORES), np.float32)
        cm[:, c] = 1.0
        d["FU_cmask"] = cm
        in_maps.append(d)
    res = run_bass_kernel_spmd(ncF, in_maps, core_ids=list(range(NCORES)))
    out = np.empty((B, NCORES * TLOC, D), np.float32)
    for c in range(NCORES):
        out[:, c * TLOC:(c + 1) * TLOC, :] = _from_fm(np.asarray(res.results[c]["xout"]), TLOC)
    return out


def build_merged_2(TLOC, NC_=NCORES):
    TL = B * TLOC
    nc = bass.Bass("TRN2", target_bir_lowering=False, debug=False)
    ctx = Ctx(nc, NC_)
    xbufB = ctx.dint("xbufB", [JD, 128, TL])
    x2out = ctx.dout("x2out", [JD, 128, TL])
    g_out = ctx.dout("g_out", [JD, 128, TL], BF16)
    vb_out = ctx.dout("vb_out", [JD, 128, TL], BF16)
    xn2_out = ctx.dout("xn2_out", [JD, 128, TL])
    build_prog_B(TLOC, ctx=ctx, x_out=xbufB)
    build_prog_C(TLOC, ctx=ctx, x_in=xbufB, x_out=x2out)
    build_prog_D(TLOC, ctx=ctx, x_in=x2out,
                 xaps={"g_o": g_out, "vb_o": vb_out, "xn2_o": xn2_out})
    return nc


def build_merged_3(TLOC, NC_=NCORES):
    TL = B * TLOC
    TG_ = NC_ * TLOC
    nc = bass.Bass("TRN2", target_bir_lowering=False, debug=False)
    ctx = Ctx(nc, NC_)
    v, s, t, g, sy = nc.vector, nc.scalar, nc.tensor, nc.gpsimd, nc.sync
    SE = ctx.sems
    xbufA = ctx.dint("xbufA", [JD, 128, TL])
    xbufB = ctx.dint("xbufB", [JD, 128, TL])
    hs_s = ctx.dint("hs_s", [JD, 128, B * TG_], BF16)
    x2_i = ctx.din("M_x2", [JD, 128, TL])
    xn2_i = ctx.din("M_xn2", [JD, 128, TL])
    xout_final = ctx.dout("xout", [JD, 128, TL])
    build_prog_E(TG_, NC_, ctx=ctx, xaps={"hs_o": hs_s})
    ctx.reset()
    hs_loc = ctx.dint("hs_loc", [JD, 128, TL], BF16)
    cmask = ctx.din("FU_cmask", [128, NC_])
    cm_sb = ctx.al([128, NC_], F32)
    acc_f = ctx.al([128, TL], F32)
    tmp_b = ctx.al([128, TL], BF16)
    Ld = SE["Ld"]; St = SE["St"]; V = SE["V"]
    sy.dma_start(out=cm_sb[:, :], in_=cmask).then_inc(Ld.h, 16)
    Ld.val += 16
    for j in range(JD):
        V.wait(sy)
        St.wait(sy)
        v.memset(acc_f[:, :], 0.0)
        v.drain()
        for r in range(NC_):
            Ld.wait(sy)
            sy.dma_start(out=tmp_b[:, :],
                         in_=hs_s[j][:, r * TL:(r + 1) * TL]).then_inc(Ld.h, 16)
            Ld.val += 16
            Ld.wait(v)
            v.scalar_tensor_tensor(acc_f[:, :], tmp_b[:, :], cm_sb[:, r:r + 1],
                                   acc_f[:, :], ALU.mult, ALU.add)
            ei = V.inc(v.drain())
            V.wait(sy, ei)
        v.tensor_copy(tmp_b[:, :], acc_f[:, :])
        ei = V.inc(v.drain())
        V.wait(g, ei)
        g.wait_ge(St.h, St.val)
        g.dma_start(out=hs_loc[j], in_=tmp_b[:, :]).then_inc(St.h, 16)
        St.val += 16
        St.wait(v)
    ctx.barrier()
    build_prog_F(TLOC, ctx=ctx, x_in=x2_i, x_out=xbufB,
                 xaps={"xn2_i": xn2_i, "hs_i": hs_loc})
    build_prog_C(TLOC, ctx=ctx, x_in=xbufB, x_out=xbufA)
    build_prog_H(TLOC, ctx=ctx, x_in=xbufA, x_out=xbufB)
    build_prog_I(TLOC, ctx=ctx, x_in=xbufB, x_out=xbufA)
    build_prog_J(TLOC, ctx=ctx, x_in=xbufA, x_out=xbufB)
    build_prog_I(TLOC, ctx=ctx, x_in=xbufB, x_out=xout_final)
    return nc


def kernel(**inp):
    TLOC = 512
    TL = B * TLOC
    if "M2" not in _PROGS:
        _PROGS["M2"] = build_merged_2(TLOC, NCORES)
    if "M3" not in _PROGS:
        _PROGS["M3"] = build_merged_3(TLOC, NCORES)
    P = _get_progs(TLOC, only=["A"])
    common, mixes, wlist = _mk_common(inp, TLOC)
    x0 = np.asarray(inp["L0_attn_a_x"], np.float32)
    run = lambda nc_, ins: run_bass_kernel_spmd(nc_, ins, core_ids=list(range(NCORES))).results

    insA = [host_prep_A(x0, mixes, wlist, c, TLOC) for c in range(NCORES)]
    outA = run(P["A"], insA)

    vec = lambda a: np.ascontiguousarray(np.asarray(a, np.float32).reshape(-1))
    k2_stat = {k: v for k, v in common.items()
               if k.split("_")[0] in ("B", "C", "D")}
    insM2 = []
    for c in range(NCORES):
        hb = host_prep_B(outA, vec(inp["L0_attn_a_gn_w"]), vec(inp["L0_attn_a_gn_b"]),
                         np.asarray(inp["L0_attn_a_W_out"], np.float32),
                         vec(inp["L0_attn_a_b_out"]), c, TLOC)
        d = dict(k2_stat)
        for k in ("xfm_i", "rg_i", "sp_i", "pref"):
            d["B_" + k] = hb[k]
        insM2.append(d)
    outM2 = run(_PROGS["M2"], insM2)
    g_all = np.stack([np.asarray(o["g_out"]) for o in outM2])
    vb_all = np.stack([np.asarray(o["vb_out"]) for o in outM2])

    k3_stat = {k: v for k, v in common.items()
               if k.split("_")[0] in ("E", "F", "C", "H", "I", "J")}
    k3_stat["E_g_i"] = g_all
    k3_stat["E_vb_i"] = vb_all
    insM3 = []
    for c in range(NCORES):
        d = dict(k3_stat)
        d["M_x2"] = np.asarray(outM2[c]["x2out"])
        d["M_xn2"] = np.asarray(outM2[c]["xn2_out"])
        cm = np.zeros((128, NCORES), np.float32)
        cm[:, c] = 1.0
        d["FU_cmask"] = cm
        insM3.append(d)
    outM3 = run(_PROGS["M3"], insM3)
    out = np.empty((B, NCORES * TLOC, D), np.float32)
    for c in range(NCORES):
        out[:, c * TLOC:(c + 1) * TLOC, :] = _from_fm(np.asarray(outM3[c]["xout"]), TLOC)
    return out
